# version 08
# baseline (speedup 1.0000x reference)
import numpy as np
import ml_dtypes
import concourse.bass as bass
import concourse.mybir as mybir
from concourse.bass_utils import run_bass_kernel_spmd

F32 = mybir.dt.float32
BF16 = mybir.dt.bfloat16
FP8 = mybir.dt.float8e4
I32 = mybir.dt.int32
OP = mybir.AluOpType
AF = mybir.ActivationFunctionType
AX = mybir.AxisListType.X

S, D, H, HD, INNER = 2048, 2048, 16, 128, 8192
NB = D // 128
NT = S // 128
EPS = 1e-5
QSC = float(1.0 / np.sqrt(128.0))
ENGINES = ("sync", "tensor", "vector", "scalar", "gpsimd")


class Prog:
    def __init__(self, nc):
        self.nc = nc
        self.cmds = {e: [] for e in ENGINES}
        self.cnt = {}
        self.sems = {}
        self._sem_cms = []

    def sem(self, name):
        if name not in self.sems:
            cm = self.nc.semaphore(name)
            self.sems[name] = cm.__enter__()
            self._sem_cms.append(cm)
            self.cnt[name] = 0
        return self.sems[name]

    def dma(self, q, out, in_, sem, transpose=False):
        s = self.sem(sem)
        self.cnt[sem] += 16
        if transpose:
            self.cmds[q].append(lambda e, o=out, i=in_: e.dma_start_transpose(out=o, in_=i).then_inc(s, 16))
        else:
            self.cmds[q].append(lambda e, o=out, i=in_: e.dma_start(out=o, in_=i).then_inc(s, 16))
        return self.cnt[sem]

    def wait(self, eng, sem, val=None):
        s = self.sem(sem)
        v = self.cnt[sem] if val is None else val
        if v > 0:
            self.cmds[eng].append(lambda e, vv=v: e.wait_ge(s, vv))

    def op(self, eng, fn, sem=None):
        if sem is None:
            self.cmds[eng].append(fn)
            return None
        s = self.sem(sem)
        self.cnt[sem] += 1
        v = self.cnt[sem]
        self.cmds[eng].append(lambda e, f=fn: f(e).then_inc(s, 1))
        return v

    def opd(self, eng, fn, sem=None):
        self.cmds[eng].append(fn)
        if sem is None:
            self.cmds[eng].append(lambda e: e.drain())
            return None
        s = self.sem(sem)
        self.cnt[sem] += 1
        v = self.cnt[sem]
        self.cmds[eng].append(lambda e: e.drain().then_inc(s, 1))
        return v

    def finish(self):
        nc = self.nc
        def mk(eng):
            cmds = self.cmds[eng]
            def run(e):
                for fn in cmds:
                    fn(e)
            return run
        with nc.Block() as block:
            for eng in ENGINES:
                getattr(block, eng)(mk(eng))


def build_nc(stage=4, dbg=(), heads=None):
    nc = bass.Bass("TRN2", target_bir_lowering=False, detect_race_conditions=False)
    P = Prog(nc)

    x_in = nc.dram_tensor("x_in", [S, D], F32, kind="ExternalInput").ap()
    wqT = nc.dram_tensor("wqT", [D, D], F32, kind="ExternalInput").ap()
    wkT = nc.dram_tensor("wkT", [D, D], F32, kind="ExternalInput").ap()
    wvT = nc.dram_tensor("wvT", [D, D], F32, kind="ExternalInput").ap()
    woT = nc.dram_tensor("woT", [D, D], F32, kind="ExternalInput").ap()
    gwT = nc.dram_tensor("gwT", [HD, HD], F32, kind="ExternalInput").ap()
    gb = nc.dram_tensor("gb", [HD, 1], F32, kind="ExternalInput").ap()
    w1T = nc.dram_tensor("w1T", [D, INNER], F32, kind="ExternalInput").ap()
    w2T = nc.dram_tensor("w2T", [INNER, D], F32, kind="ExternalInput").ap()
    b1c = nc.dram_tensor("b1c", [128, INNER], F32, kind="ExternalInput").ap()
    lngc = nc.dram_tensor("lngc", [128, INNER], F32, kind="ExternalInput").ap()
    lnbc = nc.dram_tensor("lnbc", [128, INNER], F32, kind="ExternalInput").ap()
    b2c = nc.dram_tensor("b2c", [128, D], F32, kind="ExternalInput").ap()
    y_out = nc.dram_tensor("y_out", [S, D], F32, kind="ExternalOutput").ap()

    def scratch(name, shape, dt):
        kind = "ExternalOutput" if name in dbg else "Internal"
        return nc.dram_tensor(name, shape, dt, kind=kind).ap()

    qkvs = scratch("qkvs", [3, 2, H, 128, S], BF16)
    outs_h = scratch("outs_h", [H, 2, 128, S], BF16)
    x1s = scratch("x1s", [S, D], F32)
    w1q8 = scratch("w1q8", [D, INNER], FP8)
    w2q8 = scratch("w2q8", [INNER, D], FP8)

    A = nc.alloc_sbuf_tensor
    eye_f = A("eye_f", [128, 128], F32).ap()
    ones_col_bf = A("ones_col_bf", [128, 1], BF16).ap()
    ones_row_f = A("ones_row_f", [1, 128], F32).ap()
    negones_row_f = A("negones_row_f", [1, 128], F32).ap()
    madd = A("madd", [128, 4 * 512], F32).ap()
    gbt = A("gbt", [128, 1], F32).ap()
    gw_hi = A("gw_hi", [128, 128], BF16).ap()
    gw_lo = A("gw_lo", [128, 128], BF16).ap()
    gwf = A("gwf", [128, 128], F32).ap()
    swb = A("swb", [128, 2], F32).ap()
    stats = A("stats", [128, 96], F32).ap()

    tone = A("tone", [1, 8], F32).ap()

    pb = [nc.alloc_psum_tensor(f"pb{i}", [128, 512], F32).ap() for i in range(8)]

    scopes = {}

    def alloc(group, name, shape, dt, **kw):
        cm = nc.sbuf_tensor(name, shape, dt, **kw)
        t = cm.__enter__()
        scopes.setdefault(group, []).append(cm)
        return t.ap()

    def free_group(group):
        for cm in reversed(scopes.pop(group, [])):
            cm.__exit__(None, None, None)


    wtile = [alloc("pw", f"wtile{i}", [128, 1024], F32) for i in range(2)]
    wjunk = alloc("pw", "wjunk", [128, 1024], F32)
    wtern = [alloc("pw", f"wtern{i}", [128, 1024], FP8) for i in range(2)]
    wp1 = alloc("pw", "wp1", [128, 1024], F32)
    wp2 = alloc("pw", "wp2", [128, 1024], F32)
    wabs = alloc("pw", "wabs", [128, 260], F32)
    wth = alloc("pw", "wth", [128, 4], F32)
    NPW = 256

    def wsrc(idx, q8=False):
        if idx < 128:
            r, cc = divmod(idx, 8)
            t = w1q8 if q8 else w1T
            return t[r * 128:(r + 1) * 128, cc * 1024:(cc + 1) * 1024]
        i, c2 = divmod(idx - 128, 2)
        return (w2q8 if q8 else w2T)[i * 128:(i + 1) * 128, c2 * 1024:(c2 + 1) * 1024]

    pw = {"lda": [0] * 260, "absdone": [0] * 256, "ldb": [0] * 260, "tern": [0] * 256,
          "sw_done": 0}

    def pwa_step(idx):
        bi = idx % 2
        if idx < 2:
            pw["lda"][idx] = P.dma("scalar", wtile[bi], wsrc(idx), f"wt{bi}")
        P.wait("scalar", f"wt{bi}", pw["lda"][idx])
        pw["absdone"][idx] = P.opd("scalar", lambda e, b=bi, i=idx: e.activation(
            out=wjunk, in_=wtile[b], func=AF.Abs, accum_out=wabs[:, i:i + 1]), "s0")
        if idx + 2 < NPW:
            pw["lda"][idx + 2] = P.dma("scalar", wtile[(idx + 2) % 2], wsrc(idx + 2), f"wt{bi}")

    def pwa_chain():
        P.wait("vector", "s0", pw["absdone"][NPW - 1])
        P.opd("vector", lambda e: e.reduce_sum(out=wabs[:, 256:257], in_=wabs[:, 0:128], axis=AX))
        vred = P.opd("vector", lambda e: e.reduce_sum(out=wabs[:, 257:258], in_=wabs[:, 128:256],
                                                      axis=AX), "v0")
        P.wait("tensor", "v0", vred)
        P.op("tensor", lambda e: e.transpose(pb[0][0:1, 0:128], wabs[:, 256:257], eye_f), "t0")
        t2v = P.op("tensor", lambda e: e.transpose(pb[1][0:1, 0:128], wabs[:, 257:258], eye_f), "t0")
        P.wait("vector", "t0", t2v)
        vsw = 0
        for i in range(2):
            P.opd("vector", lambda e, i=i: e.reduce_sum(out=tone[0:1, i:i + 1],
                                                        in_=pb[i][0:1, 0:128], axis=AX))
            P.opd("vector", lambda e, i=i: e.tensor_scalar(out=tone[0:1, i:i + 1],
                 in0=tone[0:1, i:i + 1], scalar1=1.0 / (D * INNER), scalar2=1e-5,
                 op0=OP.mult, op1=OP.max))
            vsw = P.opd("vector", lambda e, i=i: e.reciprocal(out=tone[0:1, i:i + 1],
                        in_=tone[0:1, i:i + 1]), "v0")
        P.wait("tensor", "v0", vsw)
        P.op("tensor", lambda e: e.matmul(pb[2][:, 0:1], ones_row_f, tone[0:1, 0:1],
                                          start=True, stop=True), "t0")
        tb2 = P.op("tensor", lambda e: e.matmul(pb[2][:, 1:2], ones_row_f, tone[0:1, 1:2],
                                                start=True, stop=True), "t0")
        P.wait("vector", "t0", tb2)
        P.opd("vector", lambda e: e.tensor_copy(out=swb, in_=pb[2][:, 0:2]), "v0")
        P.opd("vector", lambda e: e.reciprocal(out=wth[:, 0:2], in_=swb), "v0")
        P.opd("vector", lambda e: e.tensor_scalar(out=wth[:, 0:2], in0=wth[:, 0:2],
             scalar1=0.5, scalar2=None, op0=OP.mult), "v0")
        pw["sw_done"] = P.opd("vector", lambda e: e.tensor_scalar(out=wth[:, 2:4], in0=wth[:, 0:2],
             scalar1=-1.0, scalar2=None, op0=OP.mult), "v0")

    def pwb_step(idx):
        bi = idx % 2
        if idx < 2:
            pw["ldb"][idx] = P.dma("scalar", wtile[bi], wsrc(idx), f"wt{bi}")
        P.wait("vector", f"wt{bi}", pw["ldb"][idx])
        if idx < 2:
            P.wait("vector", "v0", pw["sw_done"])
        if idx >= 2:
            P.wait("vector", "w8", 16 * (idx - 1))
        col = 0 if idx < 128 else 1
        P.opd("vector", lambda e, b=bi, c=col: e.tensor_scalar(out=wp1, in0=wtile[b],
             scalar1=wth[:, c:c + 1], scalar2=None, op0=OP.is_gt), "v0")
        P.op("vector", lambda e, b=bi, c=col: e.tensor_scalar(out=wp2, in0=wtile[b],
             scalar1=wth[:, 2 + c:3 + c], scalar2=None, op0=OP.is_lt), "v0")
        tn = P.opd("vector", lambda e, b=bi: e.tensor_tensor(out=wtern[b], in0=wp1, in1=wp2,
                                                             op=OP.subtract), "v0")
        pw["tern"][idx] = tn
        P.wait("sync", "v0", tn)
        P.dma("sync", wdst_idx(idx, w1q8, w2q8), wtern[bi], "w8")
        if idx + 2 < NPW:
            P.wait("scalar", "v0", tn)
            pw["ldb"][idx + 2] = P.dma("scalar", wtile[(idx + 2) % 2], wsrc(idx + 2), f"wt{bi}")

    ln1xT_hi = alloc("p01", "ln1xT_hi", [128, NB * S], BF16)
    ln1xT_lo = alloc("p01", "ln1xT_lo", [128, NB * S], BF16)
    iot_i = alloc("pinit", "iot_i", [128, 512], I32)
    iot_f = alloc("pinit", "iot_f", [128, 512], F32)
    iot_p = alloc("pinit", "iot_p", [128, 1], I32)
    iot_pf = alloc("pinit", "iot_pf", [128, 1], F32)
    P.op("gpsimd", lambda e: e.memset(stats, 0.0), "g0")
    P.op("gpsimd", lambda e: e.iota(iot_i, pattern=[[1, 512]], base=0, channel_multiplier=0), "g0")
    P.op("gpsimd", lambda e: e.iota(iot_p, pattern=[[1, 1]], base=0, channel_multiplier=1), "g0")
    P.op("gpsimd", lambda e: e.memset(ones_col_bf, 1.0), "g0")
    P.op("gpsimd", lambda e: e.memset(ones_row_f, 1.0), "g0")
    P.op("gpsimd", lambda e: e.memset(negones_row_f, -1.0), "g0")
    P.wait("vector", "g0")
    P.op("vector", lambda e: e.tensor_copy(out=iot_f, in_=iot_i), "v0")
    P.opd("vector", lambda e: e.tensor_copy(out=iot_pf, in_=iot_p), "v0")
    P.op("vector", lambda e: e.tensor_scalar(out=eye_f, in0=iot_f[:, 0:128], scalar1=iot_pf,
                                             scalar2=None, op0=OP.is_equal), "v0")
    for j in range(4):
        P.opd("vector", lambda e, j=j: e.tensor_scalar(out=stats[:, 60:61], in0=iot_pf,
                                                       scalar1=float(128 * j), scalar2=None,
                                                       op0=OP.add), "v0")
        P.op("vector", lambda e, j=j: e.tensor_scalar(out=madd[:, j * 512:(j + 1) * 512],
                                                      in0=iot_f, scalar1=stats[:, 60:61],
                                                      scalar2=None, op0=OP.is_ge), "v0")
        P.op("vector", lambda e, j=j: e.tensor_scalar(out=madd[:, j * 512:(j + 1) * 512],
                                                      in0=madd[:, j * 512:(j + 1) * 512],
                                                      scalar1=-1.0, scalar2=1e30,
                                                      op0=OP.add, op1=OP.mult), "v0")
    P.dma("sync", gwf, gwT, "xl")
    P.dma("sync", gbt, gb, "xl")
    P.wait("scalar", "xl")
    hi0 = P.op("scalar", lambda e: e.activation(out=gw_hi, in_=gwf, func=AF.Copy), "s0")
    P.wait("vector", "s0", hi0)
    P.op("vector", lambda e: e.scalar_tensor_tensor(out=gw_lo, in0=gwf, scalar=1.0, in1=gw_hi,
                                                    op0=OP.mult, op1=OP.subtract), "v0")

    free_group("pinit")
    xrow = [alloc("p0", f"xrow{i}", [128, D], F32) for i in range(2)]
    lnx0 = alloc("p0", "lnx0", [128, D], F32)
    lnx = [lnx0, lnx0]
    lnh = [alloc("p0", f"lnh{i}", [128, D], BF16) for i in range(2)]
    lnl = [alloc("p0", f"lnl{i}", [128, D], BF16) for i in range(2)]

    tile_done_s = [0] * NT
    tile_tr = [0] * NT
    tile_lo = [0] * NT
    P.wait("sync", "v0", P.cnt["v0"])

    for t in range(NT):
        bi = t % 2
        if t >= 2:
            P.wait("sync", "s0", tile_done_s[t - 2])
        xl_done = P.dma("sync", xrow[bi], x_in[t * 128:(t + 1) * 128, :], f"xl{bi}")
        P.wait("vector", f"xl{bi}", xl_done)
        P.wait("scalar", f"xl{bi}", xl_done)
        if t >= 2:
            P.wait("scalar", f"tr{bi}", tile_tr[t - 2])
            P.wait("vector", f"tr{bi}", tile_tr[t - 2])
        if t >= 1:
            P.wait("scalar", "v0", tile_lo[t - 1])
        c = 8 + bi * 8
        P.opd("vector", lambda e, b=bi, c=c: e.reduce_sum(out=stats[:, c:c + 1], in_=xrow[b], axis=AX))
        sq_done = P.opd("scalar", lambda e, b=bi, c=c: e.activation(out=lnx[b], in_=xrow[b], func=AF.Square,
                                                                    accum_out=stats[:, c + 1:c + 2]), "s0")
        P.wait("vector", "s0", sq_done)
        P.opd("vector", lambda e, c=c: e.tensor_scalar(out=stats[:, c + 2:c + 3], in0=stats[:, c:c + 1],
                                                       scalar1=1.0 / D, scalar2=None, op0=OP.mult))
        P.opd("vector", lambda e, c=c: e.tensor_scalar(out=stats[:, c + 3:c + 4], in0=stats[:, c + 1:c + 2],
                                                       scalar1=1.0 / D, scalar2=None, op0=OP.mult))
        P.opd("vector", lambda e, c=c: e.tensor_tensor(out=stats[:, c + 4:c + 5], in0=stats[:, c + 2:c + 3],
                                                       in1=stats[:, c + 2:c + 3], op=OP.mult))
        P.opd("vector", lambda e, c=c: e.tensor_tensor(out=stats[:, c + 4:c + 5], in0=stats[:, c + 3:c + 4],
                                                       in1=stats[:, c + 4:c + 5], op=OP.subtract))
        P.opd("vector", lambda e, c=c: e.tensor_scalar(out=stats[:, c + 4:c + 5], in0=stats[:, c + 4:c + 5],
                                                       scalar1=EPS, scalar2=None, op0=OP.add))
        rc = P.opd("vector", lambda e, c=c: e.reciprocal(out=stats[:, c + 4:c + 5], in_=stats[:, c + 4:c + 5]), "v0")
        P.wait("scalar", "v0", rc)
        sq2 = P.opd("scalar", lambda e, c=c: e.activation(out=stats[:, c + 5:c + 6], in_=stats[:, c + 4:c + 5],
                                                          func=AF.Sqrt), "s0")
        P.wait("vector", "s0", sq2)
        P.opd("vector", lambda e, c=c: e.tensor_scalar(out=stats[:, c + 6:c + 7], in0=stats[:, c + 2:c + 3],
                                                       scalar1=stats[:, c + 5:c + 6], scalar2=None,
                                                       op0=OP.mult))
        nmr = P.opd("vector", lambda e, c=c: e.tensor_scalar(out=stats[:, c + 6:c + 7], in0=stats[:, c + 6:c + 7],
                                                             scalar1=-1.0, scalar2=None, op0=OP.mult), "v0")
        P.wait("scalar", "v0", nmr)
        P.op("scalar", lambda e, b=bi, c=c: e.activation(out=lnx[b], in_=xrow[b], func=AF.Identity,
                                                         scale=stats[:, c + 5:c + 6],
                                                         bias=stats[:, c + 6:c + 7]), "s0")
        hi_done = P.op("scalar", lambda e, b=bi: e.activation(out=lnh[b], in_=lnx[b], func=AF.Copy), "s0")
        tile_done_s[t] = hi_done
        P.wait("vector", "s0", hi_done)
        lo_done = P.op("vector", lambda e, b=bi: e.scalar_tensor_tensor(out=lnl[b], in0=lnx[b], scalar=1.0,
                                                                        in1=lnh[b], op0=OP.mult,
                                                                        op1=OP.subtract), "v0")
        tile_lo[t] = lo_done
        P.wait("sync", "s0", hi_done)
        P.wait("sync", "v0", lo_done)
        for src, dst in ((lnh[bi], ln1xT_hi), (lnl[bi], ln1xT_lo)):
            dst3 = dst.rearrange("p (b w) -> p b w", w=S)[:, :, t * 128:(t + 1) * 128]
            src3 = src.rearrange("p (b c) -> p b c", c=128)
            P.dma("sync", dst3, src3, f"tr{bi}", transpose=True)
        tile_tr[t] = P.cnt[f"tr{bi}"]
    tr_all = (P.cnt["tr0"], P.cnt["tr1"])

    free_group("p0")
    wcol_f = [alloc("p1", f"wcol_f{i}", [128, NB * 128], F32) for i in range(2)]
    wcol_h = [alloc("p1", f"wcol_h{i}", [128, NB * 128], BF16) for i in range(2)]
    wcol_l = [alloc("p1", f"wcol_l{i}", [128, NB * 128], BF16) for i in range(2)]
    qkv_h = alloc("p1", "qkv_h", [128, D], BF16)
    qkv_l = alloc("p1", "qkv_l", [128, D], BF16)

    P.wait("tensor", "tr0", tr_all[0])
    P.wait("tensor", "tr1", tr_all[1])
    P.wait("gpsimd", "tr0", tr_all[0])
    P.wait("gpsimd", "tr1", tr_all[1])
    wTs = {0: wqT, 1: wkT, 2: wvT}
    it_t1 = [0] * 48
    it_evv = [0] * 48
    it_evs = [0] * 48
    it_lo = [0] * 48
    for p in range(3):
        qsc = QSC if p == 0 else 1.0
        for eb in range(NB):
            it = p * NB + eb
            bi = it % 2
            if it >= 2:
                P.wait("gpsimd", "t1", it_t1[it - 2])
                P.wait("gpsimd", "v1", it_lo[it - 2])
            src = wTs[p].rearrange("(a q) e -> q a e", q=128)[:, :, eb * 128:(eb + 1) * 128]
            dstf = wcol_f[bi].rearrange("q (a c) -> q a c", c=128)
            dsth = wcol_h[bi].rearrange("q (a c) -> q a c", c=128)
            P.dma("gpsimd", dstf, src, f"wq{bi}")
            wq_done = P.dma("gpsimd", dsth, src, f"wq{bi}")
            P.wait("vector", f"wq{bi}", wq_done)
            if it >= 2:
                P.wait("vector", "t1", it_t1[it - 2])
            it_lo[it] = P.op("vector", lambda e, b=bi: e.scalar_tensor_tensor(
                out=wcol_l[b], in0=wcol_f[b], scalar=1.0, in1=wcol_h[b],
                op0=OP.mult, op1=OP.subtract), "v1")
            grp = it % 2
            banks = pb[grp * 4:grp * 4 + 4]
            P.wait("tensor", f"wq{bi}", wq_done)
            P.wait("tensor", "v1", it_lo[it])
            if it >= 2:
                P.wait("tensor", "v1s", it_evv[it - 2])
                P.wait("tensor", "s1s", it_evs[it - 2])
            n = 0
            for k in range(NB):
                wh = wcol_h[bi][:, k * 128:(k + 1) * 128]
                wl = wcol_l[bi][:, k * 128:(k + 1) * 128]
                for wop, xops in ((wh, (ln1xT_hi, ln1xT_lo)), (wl, (ln1xT_hi,))):
                    for xop in xops:
                        for j in range(4):
                            n += 1
                            first = n <= 4
                            last = n == 192
                            sem = "t1" if last else None
                            P.op("tensor", lambda e, bk=banks[j], w=wop, x=xop, k=k, j=j,
                                 st=first, sp=last or n > 188:
                                 e.matmul(bk, w, x[:, k * S + j * 512:k * S + (j + 1) * 512],
                                          start=st, stop=sp), sem)
            it_t1[it] = P.cnt["t1"]
            P.wait("scalar", "t1", it_t1[it])
            if it >= 1:
                P.wait("scalar", "spq", 32 * it)
            for j in range(4):
                P.op("scalar", lambda e, bk=banks[j], j=j, sc=qsc: e.activation(
                    out=qkv_h[:, j * 512:(j + 1) * 512], in_=bk, func=AF.Copy, scale=sc), "s1s")
            it_evs[it] = P.cnt["s1s"]
            P.wait("vector", "t1", it_t1[it])
            P.wait("vector", "s1s", it_evs[it])
            if it >= 1:
                P.wait("vector", "spq", 32 * it)
            for j in range(4):
                P.op("vector", lambda e, bk=banks[j], j=j, sc=qsc: e.scalar_tensor_tensor(
                    out=qkv_l[:, j * 512:(j + 1) * 512], in0=bk, scalar=sc,
                    in1=qkv_h[:, j * 512:(j + 1) * 512], op0=OP.mult, op1=OP.subtract), "v1s")
            it_evv[it] = P.cnt["v1s"]
            P.wait("sync", "s1s", it_evs[it])
            P.wait("sync", "v1s", it_evv[it])
            P.dma("sync", qkvs[p, 0, eb], qkv_h, "spq")
            P.dma("sync", qkvs[p, 1, eb], qkv_l, "spq")
            for ii in range((it * 256) // 48, ((it + 1) * 256) // 48):
                pwa_step(ii)
    spq_all = P.cnt["spq"]
    P.wait("tensor", "s1s", P.cnt["s1s"])
    P.wait("tensor", "v1s", P.cnt["v1s"])
    pwa_chain()

    if stage <= 1:
        P.wait("sync", "spq", spq_all)
        P.wait("sync", "w8", P.cnt["w8"])
        P.dma("sync", y_out[0:128, 0:64], stats[:, 0:64], "st")
        P.wait("sync", "st", P.cnt["st"])
        P.finish()
        return nc

    free_group("p1")
    free_group("p01")
    qrb = [[alloc("p2", f"q{pt}{i}", [128, S], BF16) for pt in (0, 1)] for i in range(2)]
    krb = [[alloc("p2", f"k{pt}{i}", [128, S], BF16) for pt in (0, 1)] for i in range(2)]
    vrb = [[alloc("p2", f"v{pt}{i}", [128, S], BF16) for pt in (0, 1)] for i in range(2)]
    vtok = [[alloc("p2", f"vt{pt}{i}", [128, S], BF16, is_transpose_dest=True) for pt in (0, 1)]
            for i in range(2)]
    S_sb = [alloc("p2", "S_sb0", [128, 12 * 512], F32), alloc("p2", "S_sb1", [128, 16 * 512], F32)]
    P32 = [alloc("p2", f"P32_{i}", [128, 512], F32) for i in range(2)]
    Phi = [alloc("p2", f"Phi{i}", [128, 512], BF16) for i in range(4)]
    Plo = [alloc("p2", f"Plo{i}", [128, 512], BF16) for i in range(4)]
    sgbuf = [alloc("p2", f"sg{i}", [128, S], F32) for i in range(2)]
    outst = [[alloc("p2", f"os{pt}{i}", [128, S], BF16) for pt in (0, 1)] for i in range(2)]
    t2sb = alloc("p2", "t2sb", [128, 512], F32)
    t3sb = alloc("p2", "t3sb", [128, 512], F32)
    macc = alloc("p2", "macc", [128, 16], F32)
    mpcol = alloc("p2", "mpcol", [128, 1], F32)
    cnrow = alloc("p2", "cnrow", [1, 1], F32)
    srow = alloc("p2", "srow", [1, 512], F32)
    rrow = alloc("p2", "rrow", [1, 512], F32)
    negc = [alloc("p2", f"negc{i}", [128, 1], F32) for i in range(2)]

    SC = pb[0:4]
    AVB = pb[4]
    GB = pb[5]
    SURB = [pb[6], pb[7]]

    P.wait("sync", "spq", spq_all)
    P.wait("tensor", "v0", pw["sw_done"])

    st2 = {"bank_ev": {}, "gb_read": 0, "av_read": 0, "sur_read": [0, 0],
           "head_t2_end": [0] * (H + 2), "spo_after": [0] * (H + 2),
           "head_v2_end": [0] * (H + 2), "sg_free": [0] * (H + 2)}

    def emit_head(h):
        hp = h % 2
        if h >= 2:
            P.wait("sync", "t2", st2["head_t2_end"][h - 1])
        for p, dstl in ((0, qrb), (1, krb), (2, vrb)):
            for part in (0, 1):
                P.dma("sync", dstl[hp][part], qkvs[p, part, h], f"rq{hp}")
        rq_done = P.cnt[f"rq{hp}"]
        P.wait("sync", f"rq{hp}", rq_done)
        for part in (0, 1):
            src3 = vrb[hp][part].rearrange("p (b c) -> p b c", c=128)
            dst3 = vtok[hp][part].rearrange("p (b c) -> p b c", c=128)
            P.dma("sync", dst3, src3, f"trv{hp}", transpose=True)
        trv_done = P.cnt[f"trv{hp}"]

        P.wait("tensor", f"rq{hp}", rq_done)
        P.wait("tensor", f"trv{hp}", trv_done)
        P.wait("tensor", "v2", st2["gb_read"])
        if h >= 2:
            P.wait("scalar", "v2", st2["sg_free"][h - 2])
        sg_done = [0] * 4
        for c in range(4):
            n = 0
            for wop, xop in ((gw_hi, qrb[hp][0]), (gw_hi, qrb[hp][1]), (gw_lo, qrb[hp][0])):
                n += 1
                P.op("tensor", lambda e, w=wop, x=xop, c=c, st=(n == 1), sp=(n == 3):
                     e.matmul(GB, w, x[:, c * 512:(c + 1) * 512], start=st, stop=sp),
                     "t2" if n == 3 else None)
            P.wait("scalar", "t2", P.cnt["t2"])
            sg_done[c] = P.op("scalar", lambda e, c=c, b=hp: e.activation(
                out=sgbuf[b][:, c * 512:(c + 1) * 512], in_=GB, func=AF.Sigmoid,
                scale=float(np.sqrt(128.0)), bias=gbt), "s2")
            if c < 3:
                P.wait("tensor", "s2", sg_done[c])

        split_v2 = {}
        blk_t2 = {}
        for c in range(4):
            nblk = 4 * c + 4
            cp = c % 2
            ssb = S_sb[cp]
            ev_v2 = [0] * nblk
            for b in range(nblk):
                bki = 2 * cp + b % 2
                bank = SC[bki]
                prev_ev = st2["bank_ev"].get(bki, 0)
                if prev_ev:
                    P.wait("tensor", "v2", prev_ev)
                n = 0
                for wop, xop in ((krb[hp][0], qrb[hp][0]), (krb[hp][0], qrb[hp][1]),
                                 (krb[hp][1], qrb[hp][0])):
                    n += 1
                    P.op("tensor", lambda e, bk=bank, w=wop, x=xop, b=b, c=c,
                         st=(n == 1), sp=(n == 3):
                         e.matmul(bk, w[:, b * 128:(b + 1) * 128],
                                  x[:, c * 512:(c + 1) * 512], start=st, stop=sp),
                         "t2" if n == 3 else None)
                t_sc = P.cnt["t2"]
                P.wait("vector", "t2", t_sc)
                if b >= 4 * c:
                    j = b - 4 * c
                    P.op("vector", lambda e, bk=bank, b=b, j=j, s=ssb: e.tensor_tensor(
                        out=s[:, b * 512:(b + 1) * 512], in0=bk, in1=madd[:, j * 512:(j + 1) * 512],
                        op=OP.add), "v2")
                    ev_v2[b] = P.op("vector", lambda e, b=b, s=ssb: e.reduce_max(
                        out=macc[:, b:b + 1], in_=s[:, b * 512:(b + 1) * 512], axis=AX), "v2")
                else:
                    ev_v2[b] = P.op("vector", lambda e, bk=bank, b=b, s=ssb: e.tensor_scalar(
                        out=s[:, b * 512:(b + 1) * 512], in0=bk, scalar1=1.0, scalar2=None,
                        op0=OP.mult, op1=OP.max, accum_out=macc[:, b:b + 1]), "v2")
                st2["bank_ev"][bki] = ev_v2[b]
            P.op("vector", lambda e: e.drain())
            mp_v2 = P.opd("vector", lambda e, n=nblk: e.reduce_max(out=mpcol, in_=macc[:, 0:n],
                                                                   axis=AX), "v2")
            P.wait("tensor", "v2", mp_v2)
            P.wait("tensor", "v2", st2["sur_read"][cp])
            P.op("tensor", lambda e, cp=cp: e.transpose(SURB[cp][0:1, 0:128], mpcol, eye_f), "t2")
            P.wait("vector", "t2", P.cnt["t2"])
            cn_v2 = P.opd("vector", lambda e, cp=cp: e.reduce_max(out=cnrow, in_=SURB[cp][0:1, 0:128],
                                                                  axis=AX, negate=True), "v2")
            P.wait("tensor", "v2", cn_v2)
            nb_t2 = P.op("tensor", lambda e, cp=cp: e.matmul(SURB[cp][:, 128:129], ones_row_f,
                                                             cnrow[0:1, 0:1], start=True, stop=True), "t2")
            P.wait("scalar", "t2", nb_t2)
            P.opd("scalar", lambda e, cp=cp: e.copy(out=negc[cp], in_=SURB[cp][:, 128:129]), "s2")

            av_n = sum_n = 0
            for b in range(nblk):
                rot = b % 4
                if b >= 2:
                    P.wait("scalar", "v2", split_v2[(c, b - 2)])
                ex = P.op("scalar", lambda e, b=b, cp=cp, s=ssb: e.activation(
                    out=P32[b % 2], in_=s[:, b * 512:(b + 1) * 512], func=AF.Exp,
                    bias=negc[cp], scale=1.0), "s2")
                P.wait("vector", "s2", ex)
                if b >= 4:
                    P.wait("vector", "t2", blk_t2[(c, b - 4)])
                P.op("vector", lambda e, b=b, rot=rot: e.tensor_copy(out=Phi[rot], in_=P32[b % 2]), "v2")
                split_v2[(c, b)] = P.op("vector", lambda e, b=b, rot=rot: e.scalar_tensor_tensor(
                    out=Plo[rot], in0=P32[b % 2], scalar=1.0, in1=Phi[rot],
                    op0=OP.mult, op1=OP.subtract), "v2")
                P.wait("tensor", "v2", split_v2[(c, b)])
                if b == 0:
                    P.wait("tensor", "v2", st2["av_read"])
                for k5, (wop, xop) in enumerate((
                        (vtok[hp][0], Phi[rot]), (vtok[hp][0], Plo[rot]), (vtok[hp][1], Phi[rot]),
                        (ones_col_bf, Phi[rot]), (ones_col_bf, Plo[rot]))):
                    if k5 < 3:
                        av_n += 1
                        st, sp, o = av_n == 1, av_n == 3 * nblk, AVB
                        w_ap = wop[:, b * 128:(b + 1) * 128]
                    else:
                        sum_n += 1
                        st, sp, o = sum_n == 1, sum_n == 2 * nblk, SURB[cp][0:1, 0:512]
                        w_ap = wop
                    P.op("tensor", lambda e, o=o, w=w_ap, x=xop, st=st, sp=sp:
                         e.matmul(o, w, x, start=st, stop=sp),
                         "t2" if k5 == 4 else None)
                blk_t2[(c, b)] = P.cnt["t2"]
            mm_end = P.cnt["t2"]

            P.wait("vector", "t2", mm_end)
            P.opd("vector", lambda e, cp=cp: e.tensor_copy(out=srow, in_=SURB[cp][0:1, 0:512]))
            rec_v2 = P.opd("vector", lambda e: e.reciprocal(out=rrow, in_=srow), "v2")
            st2["sur_read"][cp] = rec_v2
            P.wait("tensor", "v2", rec_v2)
            if c == 0:
                P.wait("tensor", "s2", sg_done[3])
            bc_t2 = P.op("tensor", lambda e: e.matmul(GB, ones_row_f, rrow[0:1, 0:512],
                                                      start=True, stop=True), "t2")
            P.wait("vector", "t2", bc_t2)
            if h >= 2 and c == 0:
                P.wait("vector", f"spo{hp}", st2["spo_after"][h - 2])
            P.op("vector", lambda e, c=c, b=hp: e.tensor_tensor(
                out=t2sb, in0=AVB, in1=sgbuf[b][:, c * 512:(c + 1) * 512], op=OP.mult), "v2")
            gb_read = P.op("vector", lambda e: e.tensor_tensor(out=t3sb, in0=t2sb, in1=GB,
                                                               op=OP.mult), "v2")
            st2["gb_read"] = gb_read
            st2["av_read"] = gb_read
            P.wait("scalar", "v2", gb_read)
            if h >= 2 and c == 0:
                P.wait("scalar", f"spo{hp}", st2["spo_after"][h - 2])
            sh = P.op("scalar", lambda e, c=c, b=hp: e.activation(
                out=outst[b][0][:, c * 512:(c + 1) * 512], in_=t3sb, func=AF.Copy), "s2")
            P.wait("vector", "s2", sh)
            P.op("vector", lambda e, c=c, b=hp: e.scalar_tensor_tensor(
                out=outst[b][1][:, c * 512:(c + 1) * 512], in0=t3sb, scalar=1.0,
                in1=outst[b][0][:, c * 512:(c + 1) * 512], op0=OP.mult, op1=OP.subtract), "v2")

        st2["head_t2_end"][h + 1] = P.cnt["t2"]
        st2["sg_free"][h] = P.cnt["v2"]
        st2["head_v2_end"][h] = P.cnt["v2"]
        P.wait("sync", "s2", P.cnt["s2"])
        P.wait("sync", "v2", P.cnt["v2"])
        for part in (0, 1):
            P.dma("sync", outs_h[h, part], outst[hp][part], f"spo{hp}")
        st2["spo_after"][h] = P.cnt[f"spo{hp}"]

    for h in range(heads if heads else H):
        emit_head(h)
    spo_all = (P.cnt.get("spo0", 0), P.cnt.get("spo1", 0))

    if stage <= 2:
        P.wait("sync", "spo0", spo_all[0])
        P.wait("sync", "spo1", spo_all[1])
        P.wait("sync", "w8", w8_all)
        P.dma("sync", y_out[0:128, 0:64], stats[:, 0:64], "st")
        P.wait("sync", "st", P.cnt["st"])
        P.finish()
        return nc

    free_group("p2")
    wocf = [alloc("p3", f"wocf{i}", [128, NB * 128], F32) for i in range(2)]
    woch = [alloc("p3", f"woch{i}", [128, NB * 128], BF16) for i in range(2)]
    wocl = [alloc("p3", f"wocl{i}", [128, NB * 128], BF16) for i in range(2)]
    ortile = alloc("p3", "ortile", [128, 32 * 512], BF16)
    attnT = alloc("p3", "attnT", [128, 16 * 512], F32)
    x1st = alloc("p3", "x1st", [128, 4 * D], F32)
    xrd = alloc("p3", "xrd", [128, 4 * D], F32)

    P.wait("sync", "spo0", spo_all[0])
    P.wait("sync", "spo1", spo_all[1])
    P.wait("gpsimd", "spo0", spo_all[0])
    P.wait("gpsimd", "spo1", spo_all[1])
    ACC = pb[0:4]
    TRB = pb[4:6]
    ev_s3 = {}
    tr_read_v3 = {}
    wo_lo_v3 = {}
    wo_t3 = {}
    def wo_load(it):
        bi = it % 2
        eb = it % NB
        if it >= 2:
            P.wait("gpsimd", "t3", wo_t3.get(it - 2, 0))
        src = woT.rearrange("(a q) e -> q a e", q=128)[:, :, eb * 128:(eb + 1) * 128]
        P.dma("gpsimd", wocf[bi].rearrange("q (a k) -> q a k", k=128), src, f"wo{bi}")
        wo_ld[it] = P.dma("gpsimd", woch[bi].rearrange("q (a k) -> q a k", k=128), src, f"wo{bi}")

    def wo_split(it):
        bi = it % 2
        P.wait("vector", f"wo{bi}", wo_ld[it])
        if it >= 2:
            P.wait("vector", "t3", wo_t3.get(it - 2, 0))
        wo_lo_v3[it] = P.op("vector", lambda e, b=bi: e.scalar_tensor_tensor(
            out=wocl[b], in0=wocf[b], scalar=1.0, in1=woch[b],
            op0=OP.mult, op1=OP.subtract), "v3")

    wo_ld = {}
    wo_load(0)
    wo_load(1)
    wo_split(0)
    for c in range(4):
        if c >= 1:
            P.wait("sync", "t3", P.cnt["t3"])
        for h2 in range(H):
            for part in (0, 1):
                P.dma("sync", ortile[:, (2 * h2 + part) * 512:(2 * h2 + part + 1) * 512],
                      outs_h[h2, part][:, c * 512:(c + 1) * 512], "ro")
        for ts in range(4):
            P.dma("sync", xrd[:, ts * D:(ts + 1) * D],
                  x_in[c * 512 + ts * 128:c * 512 + (ts + 1) * 128, :], "rx")
        ro_done, rx_done = P.cnt["ro"], P.cnt["rx"]
        if c >= 1:
            P.wait("vector", "spx", P.cnt["spx"])
        for eb in range(NB):
            it = c * NB + eb
            bi = it % 2
            bank = ACC[eb % 4]
            P.wait("tensor", f"wo{bi}", wo_ld[it])
            P.wait("tensor", "v3", wo_lo_v3[it])
            P.wait("tensor", "ro", ro_done)
            if it >= 4:
                P.wait("tensor", "s3", ev_s3[it - 4])
            n = 0
            for k in range(NB):
                for wop, part in ((woch[bi], 0), (woch[bi], 1), (wocl[bi], 0)):
                    n += 1
                    P.op("tensor", lambda e, bk=bank, w=wop, k=k, pt=part,
                         st=(n == 1), sp=(n == 48):
                         e.matmul(bk, w[:, k * 128:(k + 1) * 128],
                                  ortile[:, (2 * k + pt) * 512:(2 * k + pt + 1) * 512],
                                  start=st, stop=sp), "t3" if n == 48 else None)
            wo_t3[it] = P.cnt["t3"]
            if it + 2 < 64:
                wo_load(it + 2)
            if it + 1 < 64:
                wo_split(it + 1)
            P.wait("scalar", "t3", wo_t3[it])
            ev_s3[it] = P.op("scalar", lambda e, bk=bank, eb=eb: e.activation(
                out=attnT[:, eb * 512:(eb + 1) * 512], in_=bk, func=AF.Copy), "s3")
            P.wait("tensor", "s3", ev_s3[it])
            P.wait("vector", "rx", rx_done)
            for ts in range(4):
                trb = TRB[(eb * 4 + ts) % 2]
                prev = tr_read_v3.get((eb * 4 + ts) % 2, 0)
                if prev:
                    P.wait("tensor", "v3", prev)
                t_tr = P.op("tensor", lambda e, t=trb, eb=eb, ts=ts: e.transpose(
                    t[:, 0:128], attnT[:, eb * 512 + ts * 128:eb * 512 + (ts + 1) * 128], eye_f), "t3")
                P.wait("vector", "t3", t_tr)
                tr_read_v3[(eb * 4 + ts) % 2] = P.op("vector", lambda e, t=trb, eb=eb, ts=ts:
                    e.tensor_tensor(out=x1st[:, ts * D + eb * 128:ts * D + (eb + 1) * 128],
                                    in0=t[:, 0:128], in1=xrd[:, ts * D + eb * 128:ts * D + (eb + 1) * 128],
                                    op=OP.add), "v3")
            for ii in range((it * 256) // 64, ((it + 1) * 256) // 64):
                pwb_step(ii)
        P.wait("sync", "v3", P.cnt["v3"])
        for ts in range(4):
            P.dma("sync", x1s[c * 512 + ts * 128:c * 512 + (ts + 1) * 128, :],
                  x1st[:, ts * D:(ts + 1) * D], "spx")
    spx_all = P.cnt["spx"]

    w8_all = P.cnt["w8"]
    free_group("p3")
    free_group("pw")
    h_sb = [alloc("p4", f"h{i}", [128, INNER], F32) for i in range(2)]
    x1rd = [[alloc("p4", f"x1rd{p}{i}", [128, D], F32) for i in range(2)] for p in range(2)]
    xq1tok = [alloc("p4", f"xq1tok{i}", [128, D], BF16) for i in range(2)]
    xq1T = alloc("p4", "xq1T", [128, 16 * 256], BF16, is_transpose_dest=True)
    xq2c = [alloc("p4", f"xq2c{i}", [128, 512], BF16) for i in range(4)]
    xq2T = alloc("p4", "xq2T", [128, 64 * 256], BF16, is_transpose_dest=True)
    w1tile = [alloc("p4", f"w1tile{i}", [128, 16 * 256], FP8) for i in range(2)]
    w2tile = [alloc("p4", f"w2tile{i}", [128, 4 * 1024], FP8) for i in range(2)]
    b1ct = [alloc("p4", f"b1ct{i}", [128, 256], F32) for i in range(2)]
    gct = [alloc("p4", f"gct{i}", [128, 512], F32) for i in range(2)]
    bct = [alloc("p4", f"bct{i}", [128, 512], F32) for i in range(2)]
    b2t = alloc("p4", "b2t", [128, D], F32)
    i32a = [alloc("p4", f"i32a{i}", [128, 512], I32) for i in range(2)]
    i32b = [alloc("p4", f"i32b{i}", [128, 512], I32) for i in range(2)]
    sttmp = [alloc("p4", f"sttmp{i}", [128, 512], F32) for i in range(2)]
    macc4 = alloc("p4", "macc4", [128, 64], F32)

    P.wait("gpsimd", "spx", spx_all)
    P.wait("gpsimd", "w8", w8_all)
    P.wait("sync", "spx", spx_all)
    bb_b2 = P.dma("gpsimd", b2t, b2c, "bbz")

    def SCL(cc, tb, i):
        p = (cc % 2) * 2 + tb
        return stats[:, 24 * p + i:24 * p + i + 1]

    CS = {"v4_end": [0] * 10, "st_end": [0] * 10, "mm1_t4": [0] * 10, "mm2_t4": [0] * 10,
          "q1_v4": [[0, 0] for _ in range(10)], "tq1": [0] * 10, "q2_ready": [0] * 10,
          "ev2_v4": [0] * 10, "scl_free_s": [0] * 10, "scl_free_v": [0] * 10}
    tiny = P.opd

    def drain_inc(eng, sem):
        sm = P.sem(sem)
        P.cnt[sem] += 1
        v = P.cnt[sem]
        P.cmds[eng].append(lambda e, s=sm, vv=v: e.drain().then_inc(s, 1))
        return v

    def emit_A(cc):
        cp = cc % 2
        if cc >= 2:
            P.wait("gpsimd", f"st{cp}", CS["st_end"][cc - 2])
        for tb in range(2):
            P.dma("gpsimd", x1rd[cp][tb], x1s[(2 * cc + tb) * 128:(2 * cc + tb + 1) * 128, :],
                  f"r1{cp}")
        r1_done = P.cnt[f"r1{cp}"]
        P.wait("vector", f"r1{cp}", r1_done)
        P.wait("scalar", f"r1{cp}", r1_done)
        if cc >= 2:
            P.wait("vector", "v4", CS["ev2_v4"][cc - 2])
            P.wait("scalar", "v4", CS["ev2_v4"][cc - 2])
        if cc >= 1:
            P.wait("scalar", "tq10", CS["tq1"][cc - 1][0])
            P.wait("scalar", "tq11", CS["tq1"][cc - 1][1])
        sq_s4 = [0, 0]
        for tb in range(2):
            tiny("vector", lambda e, tb=tb: e.reduce_sum(out=SCL(cc, tb, 0), in_=x1rd[cp][tb], axis=AX))
            sq_s4[tb] = tiny("scalar", lambda e, tb=tb: e.activation(
                out=xq1tok[tb], in_=x1rd[cp][tb], func=AF.Square, accum_out=SCL(cc, tb, 1)), "s4")
            tiny("vector", lambda e, tb=tb: e.reduce_max(out=SCL(cc, tb, 2), in_=x1rd[cp][tb], axis=AX))
            tiny("vector", lambda e, tb=tb: e.tensor_reduce(out=SCL(cc, tb, 3), in_=x1rd[cp][tb],
                                                            axis=AX, op=OP.min))
        sqr_v4 = [0, 0]
        for tb in range(2):
            P.wait("vector", "s4", sq_s4[tb])
            tiny("vector", lambda e, tb=tb: e.tensor_scalar(out=SCL(cc, tb, 4), in0=SCL(cc, tb, 0),
                 scalar1=1.0 / D, scalar2=None, op0=OP.mult))
            tiny("vector", lambda e, tb=tb: e.tensor_scalar(out=SCL(cc, tb, 5), in0=SCL(cc, tb, 1),
                 scalar1=1.0 / D, scalar2=None, op0=OP.mult))
            tiny("vector", lambda e, tb=tb: e.tensor_tensor(out=SCL(cc, tb, 6), in0=SCL(cc, tb, 4),
                 in1=SCL(cc, tb, 4), op=OP.mult))
            tiny("vector", lambda e, tb=tb: e.tensor_tensor(out=SCL(cc, tb, 5), in0=SCL(cc, tb, 5),
                 in1=SCL(cc, tb, 6), op=OP.subtract))
            tiny("vector", lambda e, tb=tb: e.tensor_scalar(out=SCL(cc, tb, 6), in0=SCL(cc, tb, 5),
                 scalar1=EPS, scalar2=None, op0=OP.add))
            tiny("vector", lambda e, tb=tb: e.reciprocal(out=SCL(cc, tb, 6), in_=SCL(cc, tb, 6)))
            tiny("vector", lambda e, tb=tb: e.tensor_tensor(out=SCL(cc, tb, 8), in0=SCL(cc, tb, 5),
                 in1=SCL(cc, tb, 6), op=OP.mult))
            tiny("vector", lambda e, tb=tb: e.tensor_scalar(out=SCL(cc, tb, 8), in0=SCL(cc, tb, 8),
                 scalar1=EPS, scalar2=None, op0=OP.add))
            sqr_v4[tb] = tiny("vector", lambda e, tb=tb: e.reciprocal(out=SCL(cc, tb, 8),
                              in_=SCL(cc, tb, 8)), "v4")
        r2_s4 = [0, 0]
        for tb in range(2):
            P.wait("scalar", "v4", sqr_v4[tb])
            tiny("scalar", lambda e, tb=tb: e.activation(out=SCL(cc, tb, 7), in_=SCL(cc, tb, 6),
                                                         func=AF.Sqrt))
            r2_s4[tb] = tiny("scalar", lambda e, tb=tb: e.activation(out=SCL(cc, tb, 9),
                             in_=SCL(cc, tb, 8), func=AF.Sqrt), "s4")
        qs_v4 = [0, 0]
        for tb in range(2):
            P.wait("vector", "s4", r2_s4[tb])
            tiny("vector", lambda e, tb=tb: e.tensor_tensor(out=SCL(cc, tb, 10), in0=SCL(cc, tb, 2),
                 in1=SCL(cc, tb, 4), op=OP.subtract))
            tiny("vector", lambda e, tb=tb: e.tensor_tensor(out=SCL(cc, tb, 11), in0=SCL(cc, tb, 4),
                 in1=SCL(cc, tb, 3), op=OP.subtract))
            tiny("vector", lambda e, tb=tb: e.tensor_tensor(out=SCL(cc, tb, 10), in0=SCL(cc, tb, 10),
                 in1=SCL(cc, tb, 11), op=OP.max))
            tiny("vector", lambda e, tb=tb: e.tensor_tensor(out=SCL(cc, tb, 11), in0=SCL(cc, tb, 7),
                 in1=SCL(cc, tb, 9), op=OP.mult))
            tiny("vector", lambda e, tb=tb: e.tensor_tensor(out=SCL(cc, tb, 10), in0=SCL(cc, tb, 10),
                 in1=SCL(cc, tb, 11), op=OP.mult))
            tiny("vector", lambda e, tb=tb: e.tensor_scalar(out=SCL(cc, tb, 10), in0=SCL(cc, tb, 10),
                 scalar1=1e-5, scalar2=None, op0=OP.max))
            tiny("vector", lambda e, tb=tb: e.reciprocal(out=SCL(cc, tb, 10), in_=SCL(cc, tb, 10)))
            tiny("vector", lambda e, tb=tb: e.tensor_scalar(out=SCL(cc, tb, 12), in0=SCL(cc, tb, 10),
                 scalar1=127.0, scalar2=None, op0=OP.mult))
            tiny("vector", lambda e, tb=tb: e.tensor_tensor(out=SCL(cc, tb, 13), in0=SCL(cc, tb, 11),
                 in1=SCL(cc, tb, 12), op=OP.mult))
            tiny("vector", lambda e, tb=tb: e.tensor_tensor(out=SCL(cc, tb, 14), in0=SCL(cc, tb, 4),
                 in1=SCL(cc, tb, 13), op=OP.mult))
            tiny("vector", lambda e, tb=tb: e.tensor_scalar(out=SCL(cc, tb, 14), in0=SCL(cc, tb, 14),
                 scalar1=-1.0, scalar2=None, op0=OP.mult))
            tiny("vector", lambda e, tb=tb: e.tensor_tensor(out=SCL(cc, tb, 15), in0=SCL(cc, tb, 12),
                 in1=swb[:, 0:1], op=OP.mult))
            qs_v4[tb] = tiny("vector", lambda e, tb=tb: e.reciprocal(out=SCL(cc, tb, 15),
                             in_=SCL(cc, tb, 15)), "v4")
        uq = {}
        for tb in range(2):
            P.wait("scalar", "v4", qs_v4[tb])
            for pc in range(4):
                pp = pc % 2
                prev = uq.get(pp, 0)
                if prev:
                    P.wait("scalar", "v4", prev)
                u = P.op("scalar", lambda e, tb=tb, pc=pc, pp=pp: e.activation(
                    out=i32a[pp], in_=x1rd[cp][tb][:, pc * 512:(pc + 1) * 512], func=AF.Identity,
                    scale=SCL(cc, tb, 13), bias=SCL(cc, tb, 14)), "s4")
                P.wait("vector", "s4", u)
                uq[pp] = P.op("vector", lambda e, tb=tb, pc=pc, pp=pp: e.tensor_copy(
                    out=xq1tok[tb][:, pc * 512:(pc + 1) * 512], in_=i32a[pp]), "v4")
        CS["q1_v4"][cc] = [uq.get(0, 0), uq.get(1, 0), P.cnt["v4"]]

    def emit_tq1(cc):
        P.wait("sync", "v4", CS["q1_v4"][cc][2])
        if cc >= 1:
            P.wait("sync", "t4", CS["mm1_t4"][cc - 1])
        for tb in range(2):
            dst = xq1T.rearrange("p (b w) -> p b w", w=256)[:, :, tb * 128:(tb + 1) * 128]
            src = xq1tok[tb].rearrange("p (b c) -> p b c", c=128)
            P.dma("sync", dst, src, f"tq1{tb}", transpose=True)
        CS["tq1"][cc] = (P.cnt["tq10"], P.cnt["tq11"])

    def emit_B1(cc):
        cp = cc % 2
        w1_t4 = {}
        ev1_v4 = {}
        for ic in range(32):
            icp = ic % 2
            prev = w1_t4.get(icp, CS["mm2_t4"][cc - 1] if cc >= 1 else 0)
            if prev:
                P.wait("gpsimd", "t4", prev)
            wsrc3 = w1q8.rearrange("(a q) i -> q a i", q=128)[:, :, ic * 256:(ic + 1) * 256]
            wdst3 = w1tile[icp].rearrange("q (a w) -> q a w", w=256)
            wf_done = P.dma("gpsimd", wdst3, wsrc3, f"wf{icp}")
            pe_ev = ev1_v4.get((1, icp), 0)
            if pe_ev:
                P.wait("gpsimd", "v4", pe_ev)
            bb_done = P.dma("gpsimd", b1ct[icp], b1c[:, ic * 256:(ic + 1) * 256], f"bb{icp}")
            P.wait("tensor", f"wf{icp}", wf_done)
            if ic == 0:
                P.wait("tensor", "tq10", CS["tq1"][cc][0])
                P.wait("tensor", "tq11", CS["tq1"][cc][1])
                if cc >= 1:
                    P.wait("tensor", "v4", CS["v4_end"][cc - 1])
            for tb in range(2):
                bank = pb[tb * 2 + icp][:, 0:256]
                pe = ev1_v4.get((tb, icp), 0)
                if pe:
                    P.wait("tensor", "v4", pe)
                for k in range(NB):
                    P.op("tensor", lambda e, bk=bank, k=k, tb=tb, ic=ic,
                         st=(k == 0), sp=(k == NB - 1):
                         e.matmul(bk, xq1T[:, k * 256 + tb * 128:k * 256 + (tb + 1) * 128],
                                  w1tile[ic % 2][:, k * 256:(k + 1) * 256], start=st, stop=sp),
                         "t4" if k == NB - 1 else None)
            w1_t4[icp] = P.cnt["t4"]
            t4n = P.cnt["t4"]
            P.wait("vector", "t4", t4n)
            P.wait("vector", f"bb{icp}", bb_done)
            for tb in range(2):
                bank = pb[tb * 2 + icp][:, 0:256]
                ev1_v4[(tb, icp)] = P.op("vector", lambda e, bk=bank, tb=tb, ic=ic, icp=icp:
                    e.scalar_tensor_tensor(out=h_sb[tb][:, ic * 256:(ic + 1) * 256], in0=bk,
                                           scalar=SCL(cc, tb, 15), in1=b1ct[icp],
                                           op0=OP.mult, op1=OP.add), "v4")
        CS["mm1_t4"][cc] = P.cnt["t4"]
        CS["v4_end"][cc] = P.cnt["v4"]
        evall = P.cnt["v4"]
        P.wait("scalar", "v4", evall)
        for tb in range(2):
            for ic in range(16):
                P.op("scalar", lambda e, tb=tb, ic=ic: e.activation(
                    out=h_sb[tb][:, ic * 512:(ic + 1) * 512],
                    in_=h_sb[tb][:, ic * 512:(ic + 1) * 512], func=AF.Silu), None)
        for tb in range(2):
            for ic in range(16):
                P.op("scalar", lambda e, tb=tb, ic=ic: e.activation(
                    out=h_sb[tb][:, ic * 512:(ic + 1) * 512],
                    in_=h_sb[tb][:, ic * 512:(ic + 1) * 512], func=AF.Gelu), None)
        gelu_done = drain_inc("scalar", "s4")
        P.wait("vector", "s4", gelu_done)
        for tb in range(2):
            tiny("vector", lambda e, tb=tb: e.reduce_sum(out=SCL(cc, tb, 16), in_=h_sb[tb], axis=AX))
            for ic in range(16):
                P.op("scalar", lambda e, tb=tb, ic=ic: e.activation(
                    out=sttmp[ic % 2], in_=h_sb[tb][:, ic * 512:(ic + 1) * 512], func=AF.Square,
                    accum_out=macc4[:, 16 * tb + ic:16 * tb + ic + 1]), None)
        lnsq_s4 = drain_inc("scalar", "s4")
        rh_v4 = [0, 0]
        for tb in range(2):
            P.wait("vector", "s4", lnsq_s4)
            tiny("vector", lambda e, tb=tb: e.reduce_sum(out=SCL(cc, tb, 17),
                 in_=macc4[:, 16 * tb:16 * tb + 16], axis=AX))
            tiny("vector", lambda e, tb=tb: e.tensor_scalar(out=SCL(cc, tb, 16), in0=SCL(cc, tb, 16),
                 scalar1=1.0 / INNER, scalar2=None, op0=OP.mult))
            tiny("vector", lambda e, tb=tb: e.tensor_scalar(out=SCL(cc, tb, 17), in0=SCL(cc, tb, 17),
                 scalar1=1.0 / INNER, scalar2=None, op0=OP.mult))
            tiny("vector", lambda e, tb=tb: e.tensor_tensor(out=SCL(cc, tb, 20), in0=SCL(cc, tb, 16),
                 in1=SCL(cc, tb, 16), op=OP.mult))
            tiny("vector", lambda e, tb=tb: e.tensor_tensor(out=SCL(cc, tb, 17), in0=SCL(cc, tb, 17),
                 in1=SCL(cc, tb, 20), op=OP.subtract))
            tiny("vector", lambda e, tb=tb: e.tensor_scalar(out=SCL(cc, tb, 17), in0=SCL(cc, tb, 17),
                 scalar1=EPS, scalar2=None, op0=OP.add))
            rh_v4[tb] = tiny("vector", lambda e, tb=tb: e.reciprocal(out=SCL(cc, tb, 17),
                             in_=SCL(cc, tb, 17)), "v4")
        nmh_v4 = [0, 0]
        for tb in range(2):
            P.wait("scalar", "v4", rh_v4[tb])
            s1 = tiny("scalar", lambda e, tb=tb: e.activation(out=SCL(cc, tb, 18),
                      in_=SCL(cc, tb, 17), func=AF.Sqrt), "s4")
            P.wait("vector", "s4", s1)
            tiny("vector", lambda e, tb=tb: e.tensor_tensor(out=SCL(cc, tb, 19), in0=SCL(cc, tb, 16),
                 in1=SCL(cc, tb, 18), op=OP.mult))
            nmh_v4[tb] = tiny("vector", lambda e, tb=tb: e.tensor_scalar(out=SCL(cc, tb, 19),
                 in0=SCL(cc, tb, 19), scalar1=-1.0, scalar2=None, op0=OP.mult), "v4")
        bgr = {}
        for ic in range(16):
            icp = ic % 2
            pg = bgr.get(icp, 0)
            if pg:
                P.wait("gpsimd", "v4", pg)
            P.dma("gpsimd", gct[icp], lngc[:, ic * 512:(ic + 1) * 512], f"bg{icp}")
            gb_done = P.dma("gpsimd", bct[icp], lnbc[:, ic * 512:(ic + 1) * 512], f"bg{icp}")
            for tb in range(2):
                if ic == 0:
                    P.wait("scalar", "v4", nmh_v4[tb])
                zs = P.op("scalar", lambda e, tb=tb, ic=ic: e.activation(
                    out=sttmp[ic % 2], in_=h_sb[tb][:, ic * 512:(ic + 1) * 512], func=AF.Identity,
                    scale=SCL(cc, tb, 18), bias=SCL(cc, tb, 19)), "s4")
                P.wait("vector", "s4", zs)
                P.wait("vector", f"bg{icp}", gb_done)
                P.op("vector", lambda e, ic=ic, icp=icp: e.tensor_tensor(
                    out=sttmp[ic % 2], in0=sttmp[ic % 2], in1=gct[icp], op=OP.mult), "v4")
                yv = P.op("vector", lambda e, tb=tb, ic=ic, icp=icp: e.tensor_tensor(
                    out=h_sb[tb][:, ic * 512:(ic + 1) * 512], in0=sttmp[ic % 2], in1=bct[icp],
                    op=OP.add), "v4")
                P.wait("scalar", "v4", yv)
                P.op("scalar", lambda e, tb=tb, ic=ic: e.activation(
                    out=sttmp[ic % 2], in_=h_sb[tb][:, ic * 512:(ic + 1) * 512], func=AF.Square,
                    accum_out=macc4[:, 16 * tb + ic:16 * tb + ic + 1]), None)
                P.op("vector", lambda e, tb=tb, ic=ic: e.reduce_max(
                    out=macc4[:, 32 + 16 * tb + ic:32 + 16 * tb + ic + 1],
                    in_=h_sb[tb][:, ic * 512:(ic + 1) * 512], axis=AX,
                    apply_absolute_value=True), None)
            bgr[icp] = yv
        y2sq_s4 = drain_inc("scalar", "s4")
        drain_inc("vector", "v4")
        q2_v4 = [0, 0]
        for tb in range(2):
            P.wait("vector", "s4", y2sq_s4)
            tiny("vector", lambda e, tb=tb: e.reduce_sum(out=SCL(cc, tb, 20),
                 in_=macc4[:, 16 * tb:16 * tb + 16], axis=AX))
            tiny("vector", lambda e, tb=tb: e.tensor_scalar(out=SCL(cc, tb, 20), in0=SCL(cc, tb, 20),
                 scalar1=1.0 / INNER, scalar2=EPS, op0=OP.mult, op1=OP.add))
            q2_v4[tb] = tiny("vector", lambda e, tb=tb: e.reciprocal(out=SCL(cc, tb, 20),
                             in_=SCL(cc, tb, 20)), "v4")
        r2b_s4 = [0, 0]
        for tb in range(2):
            P.wait("scalar", "v4", q2_v4[tb])
            r2b_s4[tb] = tiny("scalar", lambda e, tb=tb: e.activation(out=SCL(cc, tb, 21),
                              in_=SCL(cc, tb, 20), func=AF.Sqrt), "s4")
        c2_v4 = [0, 0]
        for tb in range(2):
            P.wait("vector", "s4", r2b_s4[tb])
            tiny("vector", lambda e, tb=tb: e.reduce_max(out=SCL(cc, tb, 22),
                 in_=macc4[:, 32 + 16 * tb:32 + 16 * tb + 16], axis=AX))
            tiny("vector", lambda e, tb=tb: e.tensor_tensor(out=SCL(cc, tb, 22), in0=SCL(cc, tb, 22),
                 in1=SCL(cc, tb, 21), op=OP.mult))
            tiny("vector", lambda e, tb=tb: e.tensor_scalar(out=SCL(cc, tb, 22), in0=SCL(cc, tb, 22),
                 scalar1=1e-5, scalar2=None, op0=OP.max))
            tiny("vector", lambda e, tb=tb: e.reciprocal(out=SCL(cc, tb, 22), in_=SCL(cc, tb, 22)))
            tiny("vector", lambda e, tb=tb: e.tensor_scalar(out=SCL(cc, tb, 22), in0=SCL(cc, tb, 22),
                 scalar1=127.0, scalar2=None, op0=OP.mult))
            tiny("vector", lambda e, tb=tb: e.tensor_tensor(out=SCL(cc, tb, 23), in0=SCL(cc, tb, 22),
                 in1=swb[:, 1:2], op=OP.mult))
            tiny("vector", lambda e, tb=tb: e.reciprocal(out=SCL(cc, tb, 23), in_=SCL(cc, tb, 23)))
            c2_v4[tb] = tiny("vector", lambda e, tb=tb: e.tensor_tensor(out=SCL(cc, tb, 22),
                 in0=SCL(cc, tb, 22), in1=SCL(cc, tb, 21), op=OP.mult), "v4")
        tq2_use = ({r: CS["q2_ready"][cc - 1][f"tq2_{r}"] for r in range(4)}
                   if cc >= 1 else {})
        uf2 = {}
        for ic in range(16):
            for tb in range(2):
                rr = (2 * ic + tb) % 2
                rot4 = (2 * ic + tb) % 4
                if ic == 0 and tb == 0:
                    for t2b in range(2):
                        P.wait("scalar", "v4", c2_v4[t2b])
                pu = uf2.get(rr, 0)
                if pu:
                    P.wait("scalar", "v4", pu)
                u2 = P.op("scalar", lambda e, tb=tb, ic=ic, rr=rr: e.activation(
                    out=i32b[rr], in_=h_sb[tb][:, ic * 512:(ic + 1) * 512], func=AF.Identity,
                    scale=SCL(cc, tb, 22), bias=0.0), "s4")
                P.wait("vector", "s4", u2)
                pt = tq2_use.get(rot4, 0)
                if pt:
                    P.wait("vector", f"tq2_{rot4}", pt)
                vq = P.op("vector", lambda e, rr=rr, rot4=rot4: e.tensor_copy(
                    out=xq2c[rot4], in_=i32b[rr]), "v4")
                uf2[rr] = vq
                P.wait("sync", "v4", vq)
                if cc >= 1 and ic == 0 and tb == 0:
                    P.wait("sync", "t4", CS["mm2_t4"][cc - 1])
                dst = xq2T.rearrange("p (b w) -> p b w", w=256)[:, 4 * ic:4 * ic + 4,
                                                               tb * 128:(tb + 1) * 128]
                srcq = xq2c[rot4].rearrange("p (b c) -> p b c", c=128)
                tq2_use[rot4] = P.dma("sync", dst, srcq, f"tq2_{rot4}", transpose=True)
        CS["q2_ready"][cc] = {f"tq2_{r}": P.cnt.get(f"tq2_{r}", 0) for r in range(4)}

    def emit_B2(cc):
        cp = cc % 2
        ev_half0 = 0
        w2_t4 = ({0: CS["mm2_t4"][cc - 1], 1: CS["mm2_t4"][cc - 1]} if cc >= 1 else {})
        for half in range(2):
            for ic in range(16):
                icp = ic % 2
                prev = w2_t4.get(icp, 0)
                if prev:
                    P.wait("gpsimd", "t4", prev)
                w2src = w2q8.rearrange("(a q) d -> q a d", q=128)[:, 4 * ic:4 * ic + 4,
                                                                 half * 1024:(half + 1) * 1024]
                w2dst = w2tile[icp].rearrange("q (a d) -> q a d", d=1024)
                wf2 = P.dma("gpsimd", w2dst, w2src, f"w2{icp}")
                P.wait("tensor", f"w2{icp}", wf2)
                if half == 0 and ic == 0:
                    for r4 in range(4):
                        P.wait("tensor", f"tq2_{r4}", CS["q2_ready"][cc][f"tq2_{r4}"])
                    if cc >= 1:
                        P.wait("tensor", "v4", CS["ev2_v4"][cc - 1])
                if half == 1 and ic == 0:
                    P.wait("tensor", "v4", ev_half0)
                for k in range(4):
                    for tb in range(2):
                        for dc in range(2):
                            st4 = (ic == 0 and k == 0)
                            sp4 = (ic == 15 and k == 3)
                            last = (k == 3 and tb == 1 and dc == 1)
                            P.op("tensor", lambda e, bk=pb[4 + tb * 2 + dc], ic=ic, k=k, tb=tb,
                                 dc=dc, st=st4, sp=sp4, half=half:
                                 e.matmul(bk, xq2T[:, (4 * ic + k) * 256 + tb * 128:
                                                   (4 * ic + k) * 256 + (tb + 1) * 128],
                                          w2tile[ic % 2][:, k * 1024 + dc * 512:
                                                         k * 1024 + (dc + 1) * 512],
                                          start=st, stop=sp),
                                 "t4" if last else None)
                w2_t4[icp] = P.cnt["t4"]
            mm_done = P.cnt["t4"]
            P.wait("vector", "t4", mm_done)
            if half == 0:
                pass
            if cc == 0 and half == 0:
                P.wait("vector", "bbz", bb_b2)
            for tb in range(2):
                for dc in range(2):
                    col = half * 1024 + dc * 512
                    P.op("vector", lambda e, bk=pb[4 + tb * 2 + dc], tb=tb, col=col:
                         e.scalar_tensor_tensor(out=sttmp[tb], in0=bk, scalar=SCL(cc, tb, 23),
                                                in1=b2t[:, col:col + 512],
                                                op0=OP.mult, op1=OP.add), "v4")
                    P.op("vector", lambda e, tb=tb, col=col: e.tensor_tensor(
                        out=x1rd[cp][tb][:, col:col + 512], in0=sttmp[tb],
                        in1=x1rd[cp][tb][:, col:col + 512], op=OP.add), "v4")
            ev_half0 = P.cnt["v4"]
        CS["mm2_t4"][cc] = P.cnt["t4"]
        CS["ev2_v4"][cc] = P.cnt["v4"]
        P.wait("sync", "v4", P.cnt["v4"])
        for tb in range(2):
            P.dma("sync", y_out[(2 * cc + tb) * 128:(2 * cc + tb + 1) * 128, :],
                  x1rd[cp][tb], f"st{cp}")
        CS["st_end"][cc] = P.cnt[f"st{cp}"]

    emit_A(0)
    emit_tq1(0)
    emit_A(1)
    emit_B1(0)
    for i in range(1, 8):
        emit_B2(i - 1)
        emit_tq1(i)
        if i + 1 < 8:
            emit_A(i + 1)
        emit_B1(i)
    emit_B2(7)

    P.wait("sync", "st0", P.cnt["st0"])
    P.wait("sync", "st1", P.cnt["st1"])
    P.finish()
    return nc


def wdst_idx(idx, w1q8, w2q8):
    if idx < 128:
        r, cc = divmod(idx, 8)
        return w1q8[r * 128:(r + 1) * 128, cc * 1024:(cc + 1) * 1024]
    i, c2 = divmod(idx - 128, 2)
    return w2q8[i * 128:(i + 1) * 128, c2 * 1024:(c2 + 1) * 1024]


def prep_inputs(ins, b):
    co = np.ascontiguousarray
    d = {
        "x_in": co(np.asarray(ins["L0_attn_x"])[b]).astype(np.float32),
        "wqT": co(np.asarray(ins["L0_attn_q_w"]).T),
        "wkT": co(np.asarray(ins["L0_attn_k_w"]).T),
        "wvT": co(np.asarray(ins["L0_attn_v_w"]).T),
        "woT": co(np.asarray(ins["L0_attn_o_w"]).T),
        "gwT": co(np.asarray(ins["L0_attn_gate_w"]).T),
        "gb": co(np.asarray(ins["L0_attn_gate_b"]).reshape(HD, 1)),
        "w1T": co(np.asarray(ins["L0_ffn_w1"]).T),
        "w2T": co(np.asarray(ins["L0_ffn_w2"]).T),
        "b1c": co(np.broadcast_to(np.asarray(ins["L0_ffn_b1"])[None, :], (128, INNER))),
        "lngc": co(np.broadcast_to(np.asarray(ins["L0_ffn_ln_g"])[None, :], (128, INNER))),
        "lnbc": co(np.broadcast_to(np.asarray(ins["L0_ffn_ln_b"])[None, :], (128, INNER))),
        "b2c": co(np.broadcast_to(np.asarray(ins["L0_ffn_b2"])[None, :], (128, D))),
    }
    return {k: np.asarray(v, dtype=np.float32) for k, v in d.items()}


def kernel(**inputs):
    nc = build_nc(stage=4)
    x = np.asarray(inputs["L0_attn_x"])
    ncores = x.shape[0]
    common = None
    in_maps = []
    for b in range(ncores):
        d = prep_inputs(inputs, b)
        if common is None:
            common = {k: v for k, v in d.items() if k != "x_in"}
        else:
            for k in common:
                d[k] = common[k]
        in_maps.append(d)
    res = run_bass_kernel_spmd(nc, in_maps, core_ids=list(range(ncores)))
    out = np.stack([np.asarray(r["y_out"]) for r in res.results], axis=0)
    return out.astype(np.float32)


# version 09
# speedup vs baseline: 1.0670x; 1.0670x over previous
import numpy as np
import ml_dtypes
import concourse.bass as bass
import concourse.mybir as mybir
from concourse.bass_utils import run_bass_kernel_spmd

F32 = mybir.dt.float32
BF16 = mybir.dt.bfloat16
FP8 = mybir.dt.float8e4
I32 = mybir.dt.int32
OP = mybir.AluOpType
AF = mybir.ActivationFunctionType
AX = mybir.AxisListType.X

S, D, H, HD, INNER = 2048, 2048, 16, 128, 8192
NB = D // 128
NT = S // 128
EPS = 1e-5
QSC = float(1.0 / np.sqrt(128.0))
ENGINES = ("sync", "tensor", "vector", "scalar", "gpsimd")


class Prog:
    def __init__(self, nc):
        self.nc = nc
        self.cmds = {e: [] for e in ENGINES}
        self.cnt = {}
        self.sems = {}
        self._sem_cms = []

    def sem(self, name):
        if name not in self.sems:
            cm = self.nc.semaphore(name)
            self.sems[name] = cm.__enter__()
            self._sem_cms.append(cm)
            self.cnt[name] = 0
        return self.sems[name]

    def dma(self, q, out, in_, sem, transpose=False):
        s = self.sem(sem)
        self.cnt[sem] += 16
        if transpose:
            self.cmds[q].append(lambda e, o=out, i=in_: e.dma_start_transpose(out=o, in_=i).then_inc(s, 16))
        else:
            self.cmds[q].append(lambda e, o=out, i=in_: e.dma_start(out=o, in_=i).then_inc(s, 16))
        return self.cnt[sem]

    def wait(self, eng, sem, val=None):
        s = self.sem(sem)
        v = self.cnt[sem] if val is None else val
        if v > 0:
            self.cmds[eng].append(lambda e, vv=v: e.wait_ge(s, vv))

    def op(self, eng, fn, sem=None):
        if sem is None:
            self.cmds[eng].append(fn)
            return None
        s = self.sem(sem)
        self.cnt[sem] += 1
        v = self.cnt[sem]
        self.cmds[eng].append(lambda e, f=fn: f(e).then_inc(s, 1))
        return v

    def opd(self, eng, fn, sem=None):
        self.cmds[eng].append(fn)
        if sem is None:
            self.cmds[eng].append(lambda e: e.drain())
            return None
        s = self.sem(sem)
        self.cnt[sem] += 1
        v = self.cnt[sem]
        self.cmds[eng].append(lambda e: e.drain().then_inc(s, 1))
        return v

    def finish(self):
        nc = self.nc
        def mk(eng):
            cmds = self.cmds[eng]
            def run(e):
                for fn in cmds:
                    fn(e)
            return run
        with nc.Block() as block:
            for eng in ENGINES:
                getattr(block, eng)(mk(eng))


def build_nc(stage=4, dbg=(), heads=None):
    nc = bass.Bass("TRN2", target_bir_lowering=False, detect_race_conditions=False)
    P = Prog(nc)

    x_in = nc.dram_tensor("x_in", [S, D], F32, kind="ExternalInput").ap()
    wqT = nc.dram_tensor("wqT", [D, D], F32, kind="ExternalInput").ap()
    wkT = nc.dram_tensor("wkT", [D, D], F32, kind="ExternalInput").ap()
    wvT = nc.dram_tensor("wvT", [D, D], F32, kind="ExternalInput").ap()
    woT = nc.dram_tensor("woT", [D, D], F32, kind="ExternalInput").ap()
    gwT = nc.dram_tensor("gwT", [HD, HD], F32, kind="ExternalInput").ap()
    gb = nc.dram_tensor("gb", [HD, 1], F32, kind="ExternalInput").ap()
    w1T = nc.dram_tensor("w1T", [D, INNER], F32, kind="ExternalInput").ap()
    w2T = nc.dram_tensor("w2T", [INNER, D], F32, kind="ExternalInput").ap()
    b1c = nc.dram_tensor("b1c", [128, INNER], F32, kind="ExternalInput").ap()
    lngc = nc.dram_tensor("lngc", [128, INNER], F32, kind="ExternalInput").ap()
    lnbc = nc.dram_tensor("lnbc", [128, INNER], F32, kind="ExternalInput").ap()
    b2c = nc.dram_tensor("b2c", [128, D], F32, kind="ExternalInput").ap()
    y_out = nc.dram_tensor("y_out", [S, D], F32, kind="ExternalOutput").ap()

    def scratch(name, shape, dt):
        kind = "ExternalOutput" if name in dbg else "Internal"
        return nc.dram_tensor(name, shape, dt, kind=kind).ap()

    qkvs = scratch("qkvs", [3, 2, H, 128, S], BF16)
    outs_h = scratch("outs_h", [H, 2, 128, S], BF16)
    x1s = scratch("x1s", [S, D], F32)
    w1q8 = scratch("w1q8", [D, INNER], FP8)
    w2q8 = scratch("w2q8", [INNER, D], FP8)

    A = nc.alloc_sbuf_tensor
    eye_f = A("eye_f", [128, 128], F32).ap()
    ones_col_bf = A("ones_col_bf", [128, 1], BF16).ap()
    ones_row_f = A("ones_row_f", [1, 128], F32).ap()
    negones_row_f = A("negones_row_f", [1, 128], F32).ap()
    madd = A("madd", [128, 4 * 512], F32).ap()
    gbt = A("gbt", [128, 1], F32).ap()
    gw_hi = A("gw_hi", [128, 128], BF16).ap()
    gw_lo = A("gw_lo", [128, 128], BF16).ap()
    gwf = A("gwf", [128, 128], F32).ap()
    swb = A("swb", [128, 2], F32).ap()
    stats = A("stats", [128, 96], F32).ap()

    tone = A("tone", [1, 8], F32).ap()

    pb = [nc.alloc_psum_tensor(f"pb{i}", [128, 512], F32).ap() for i in range(8)]

    scopes = {}

    def alloc(group, name, shape, dt, **kw):
        cm = nc.sbuf_tensor(name, shape, dt, **kw)
        t = cm.__enter__()
        scopes.setdefault(group, []).append(cm)
        return t.ap()

    def free_group(group):
        for cm in reversed(scopes.pop(group, [])):
            cm.__exit__(None, None, None)


    wtile = [alloc("pw", f"wtile{i}", [128, 1024], F32) for i in range(2)]
    wjunk = alloc("pw", "wjunk", [128, 1024], F32)
    wtern = [alloc("pw", f"wtern{i}", [128, 1024], FP8) for i in range(2)]
    wp1 = alloc("pw", "wp1", [128, 1024], F32)
    wp2 = alloc("pw", "wp2", [128, 1024], F32)
    wabs = alloc("pw", "wabs", [128, 260], F32)
    wth = alloc("pw", "wth", [128, 4], F32)
    NPW = 256

    def wsrc(idx, q8=False):
        if idx < 128:
            r, cc = divmod(idx, 8)
            t = w1q8 if q8 else w1T
            return t[r * 128:(r + 1) * 128, cc * 1024:(cc + 1) * 1024]
        i, c2 = divmod(idx - 128, 2)
        return (w2q8 if q8 else w2T)[i * 128:(i + 1) * 128, c2 * 1024:(c2 + 1) * 1024]

    pw = {"lda": [0] * 260, "absdone": [0] * 256, "ldb": [0] * 260, "tern": [0] * 256,
          "sw_done": 0}

    def pwa_step(idx):
        bi = idx % 2
        if idx < 2:
            pw["lda"][idx] = P.dma("scalar", wtile[bi], wsrc(idx), f"wt{bi}")
        P.wait("scalar", f"wt{bi}", pw["lda"][idx])
        pw["absdone"][idx] = P.opd("scalar", lambda e, b=bi, i=idx: e.activation(
            out=wjunk, in_=wtile[b], func=AF.Abs, accum_out=wabs[:, i:i + 1]), "s0")
        if idx + 2 < NPW:
            pw["lda"][idx + 2] = P.dma("scalar", wtile[(idx + 2) % 2], wsrc(idx + 2), f"wt{bi}")

    def pwa_chain():
        P.wait("vector", "s0", pw["absdone"][NPW - 1])
        P.opd("vector", lambda e: e.reduce_sum(out=wabs[:, 256:257], in_=wabs[:, 0:128], axis=AX))
        vred = P.opd("vector", lambda e: e.reduce_sum(out=wabs[:, 257:258], in_=wabs[:, 128:256],
                                                      axis=AX), "v0")
        P.wait("tensor", "v0", vred)
        P.op("tensor", lambda e: e.transpose(pb[0][0:1, 0:128], wabs[:, 256:257], eye_f), "t0")
        t2v = P.op("tensor", lambda e: e.transpose(pb[1][0:1, 0:128], wabs[:, 257:258], eye_f), "t0")
        P.wait("vector", "t0", t2v)
        vsw = 0
        for i in range(2):
            P.opd("vector", lambda e, i=i: e.reduce_sum(out=tone[0:1, i:i + 1],
                                                        in_=pb[i][0:1, 0:128], axis=AX))
            P.opd("vector", lambda e, i=i: e.tensor_scalar(out=tone[0:1, i:i + 1],
                 in0=tone[0:1, i:i + 1], scalar1=1.0 / (D * INNER), scalar2=1e-5,
                 op0=OP.mult, op1=OP.max))
            vsw = P.opd("vector", lambda e, i=i: e.reciprocal(out=tone[0:1, i:i + 1],
                        in_=tone[0:1, i:i + 1]), "v0")
        P.wait("tensor", "v0", vsw)
        P.op("tensor", lambda e: e.matmul(pb[2][:, 0:1], ones_row_f, tone[0:1, 0:1],
                                          start=True, stop=True), "t0")
        tb2 = P.op("tensor", lambda e: e.matmul(pb[2][:, 1:2], ones_row_f, tone[0:1, 1:2],
                                                start=True, stop=True), "t0")
        P.wait("vector", "t0", tb2)
        P.opd("vector", lambda e: e.tensor_copy(out=swb, in_=pb[2][:, 0:2]), "v0")
        P.opd("vector", lambda e: e.reciprocal(out=wth[:, 0:2], in_=swb), "v0")
        P.opd("vector", lambda e: e.tensor_scalar(out=wth[:, 0:2], in0=wth[:, 0:2],
             scalar1=0.5, scalar2=None, op0=OP.mult), "v0")
        pw["sw_done"] = P.opd("vector", lambda e: e.tensor_scalar(out=wth[:, 2:4], in0=wth[:, 0:2],
             scalar1=-1.0, scalar2=None, op0=OP.mult), "v0")

    def pwb_step(idx):
        bi = idx % 2
        if idx < 2:
            pw["ldb"][idx] = P.dma("scalar", wtile[bi], wsrc(idx), f"wt{bi}")
        P.wait("vector", f"wt{bi}", pw["ldb"][idx])
        if idx < 2:
            P.wait("vector", "v0", pw["sw_done"])
        if idx >= 2:
            P.wait("vector", "w8", 16 * (idx - 1))
        col = 0 if idx < 128 else 1
        P.opd("vector", lambda e, b=bi, c=col: e.tensor_scalar(out=wp1, in0=wtile[b],
             scalar1=wth[:, c:c + 1], scalar2=None, op0=OP.is_gt), "v0")
        P.op("vector", lambda e, b=bi, c=col: e.tensor_scalar(out=wp2, in0=wtile[b],
             scalar1=wth[:, 2 + c:3 + c], scalar2=None, op0=OP.is_lt), "v0")
        tn = P.opd("vector", lambda e, b=bi: e.tensor_tensor(out=wtern[b], in0=wp1, in1=wp2,
                                                             op=OP.subtract), "v0")
        pw["tern"][idx] = tn
        P.wait("sync", "v0", tn)
        P.dma("sync", wdst_idx(idx, w1q8, w2q8), wtern[bi], "w8")
        if idx + 2 < NPW:
            P.wait("scalar", "v0", tn)
            pw["ldb"][idx + 2] = P.dma("scalar", wtile[(idx + 2) % 2], wsrc(idx + 2), f"wt{bi}")

    ln1xT_hi = alloc("p01", "ln1xT_hi", [128, NB * S], BF16)
    ln1xT_lo = alloc("p01", "ln1xT_lo", [128, NB * S], BF16)
    iot_i = alloc("pinit", "iot_i", [128, 512], I32)
    iot_f = alloc("pinit", "iot_f", [128, 512], F32)
    iot_p = alloc("pinit", "iot_p", [128, 1], I32)
    iot_pf = alloc("pinit", "iot_pf", [128, 1], F32)
    P.op("gpsimd", lambda e: e.memset(stats, 0.0), "g0")
    P.op("gpsimd", lambda e: e.iota(iot_i, pattern=[[1, 512]], base=0, channel_multiplier=0), "g0")
    P.op("gpsimd", lambda e: e.iota(iot_p, pattern=[[1, 1]], base=0, channel_multiplier=1), "g0")
    P.op("gpsimd", lambda e: e.memset(ones_col_bf, 1.0), "g0")
    P.op("gpsimd", lambda e: e.memset(ones_row_f, 1.0), "g0")
    P.op("gpsimd", lambda e: e.memset(negones_row_f, -1.0), "g0")
    P.wait("vector", "g0")
    P.op("vector", lambda e: e.tensor_copy(out=iot_f, in_=iot_i), "v0")
    P.opd("vector", lambda e: e.tensor_copy(out=iot_pf, in_=iot_p), "v0")
    P.op("vector", lambda e: e.tensor_scalar(out=eye_f, in0=iot_f[:, 0:128], scalar1=iot_pf,
                                             scalar2=None, op0=OP.is_equal), "v0")
    for j in range(4):
        P.opd("vector", lambda e, j=j: e.tensor_scalar(out=stats[:, 60:61], in0=iot_pf,
                                                       scalar1=float(128 * j), scalar2=None,
                                                       op0=OP.add), "v0")
        P.op("vector", lambda e, j=j: e.tensor_scalar(out=madd[:, j * 512:(j + 1) * 512],
                                                      in0=iot_f, scalar1=stats[:, 60:61],
                                                      scalar2=None, op0=OP.is_ge), "v0")
        P.op("vector", lambda e, j=j: e.tensor_scalar(out=madd[:, j * 512:(j + 1) * 512],
                                                      in0=madd[:, j * 512:(j + 1) * 512],
                                                      scalar1=-1.0, scalar2=1e30,
                                                      op0=OP.add, op1=OP.mult), "v0")
    P.dma("sync", gwf, gwT, "xl")
    P.dma("sync", gbt, gb, "xl")
    P.wait("scalar", "xl")
    hi0 = P.op("scalar", lambda e: e.activation(out=gw_hi, in_=gwf, func=AF.Copy), "s0")
    P.wait("vector", "s0", hi0)
    P.op("vector", lambda e: e.scalar_tensor_tensor(out=gw_lo, in0=gwf, scalar=1.0, in1=gw_hi,
                                                    op0=OP.mult, op1=OP.subtract), "v0")

    free_group("pinit")
    xrow = [alloc("p0", f"xrow{i}", [128, D], F32) for i in range(2)]
    lnx0 = alloc("p0", "lnx0", [128, D], F32)
    lnx = [lnx0, lnx0]
    lnh = [alloc("p0", f"lnh{i}", [128, D], BF16) for i in range(2)]
    lnl = [alloc("p0", f"lnl{i}", [128, D], BF16) for i in range(2)]

    tile_done_s = [0] * NT
    tile_tr = [0] * NT
    tile_lo = [0] * NT
    P.wait("sync", "v0", P.cnt["v0"])

    for t in range(NT):
        bi = t % 2
        if t >= 2:
            P.wait("sync", "s0", tile_done_s[t - 2])
        xl_done = P.dma("sync", xrow[bi], x_in[t * 128:(t + 1) * 128, :], f"xl{bi}")
        P.wait("vector", f"xl{bi}", xl_done)
        P.wait("scalar", f"xl{bi}", xl_done)
        if t >= 2:
            P.wait("scalar", f"tr{bi}", tile_tr[t - 2])
            P.wait("vector", f"tr{bi}", tile_tr[t - 2])
        if t >= 1:
            P.wait("scalar", "v0", tile_lo[t - 1])
        c = 8 + bi * 8
        P.opd("vector", lambda e, b=bi, c=c: e.reduce_sum(out=stats[:, c:c + 1], in_=xrow[b], axis=AX))
        sq_done = P.opd("scalar", lambda e, b=bi, c=c: e.activation(out=lnx[b], in_=xrow[b], func=AF.Square,
                                                                    accum_out=stats[:, c + 1:c + 2]), "s0")
        P.wait("vector", "s0", sq_done)
        P.opd("vector", lambda e, c=c: e.tensor_scalar(out=stats[:, c + 2:c + 3], in0=stats[:, c:c + 1],
                                                       scalar1=1.0 / D, scalar2=None, op0=OP.mult))
        P.opd("vector", lambda e, c=c: e.tensor_scalar(out=stats[:, c + 3:c + 4], in0=stats[:, c + 1:c + 2],
                                                       scalar1=1.0 / D, scalar2=None, op0=OP.mult))
        P.opd("vector", lambda e, c=c: e.tensor_tensor(out=stats[:, c + 4:c + 5], in0=stats[:, c + 2:c + 3],
                                                       in1=stats[:, c + 2:c + 3], op=OP.mult))
        P.opd("vector", lambda e, c=c: e.tensor_tensor(out=stats[:, c + 4:c + 5], in0=stats[:, c + 3:c + 4],
                                                       in1=stats[:, c + 4:c + 5], op=OP.subtract))
        P.opd("vector", lambda e, c=c: e.tensor_scalar(out=stats[:, c + 4:c + 5], in0=stats[:, c + 4:c + 5],
                                                       scalar1=EPS, scalar2=None, op0=OP.add))
        rc = P.opd("vector", lambda e, c=c: e.reciprocal(out=stats[:, c + 4:c + 5], in_=stats[:, c + 4:c + 5]), "v0")
        P.wait("scalar", "v0", rc)
        sq2 = P.opd("scalar", lambda e, c=c: e.activation(out=stats[:, c + 5:c + 6], in_=stats[:, c + 4:c + 5],
                                                          func=AF.Sqrt), "s0")
        P.wait("vector", "s0", sq2)
        P.opd("vector", lambda e, c=c: e.tensor_scalar(out=stats[:, c + 6:c + 7], in0=stats[:, c + 2:c + 3],
                                                       scalar1=stats[:, c + 5:c + 6], scalar2=None,
                                                       op0=OP.mult))
        nmr = P.opd("vector", lambda e, c=c: e.tensor_scalar(out=stats[:, c + 6:c + 7], in0=stats[:, c + 6:c + 7],
                                                             scalar1=-1.0, scalar2=None, op0=OP.mult), "v0")
        P.wait("scalar", "v0", nmr)
        P.op("scalar", lambda e, b=bi, c=c: e.activation(out=lnx[b], in_=xrow[b], func=AF.Identity,
                                                         scale=stats[:, c + 5:c + 6],
                                                         bias=stats[:, c + 6:c + 7]), "s0")
        hi_done = P.op("scalar", lambda e, b=bi: e.activation(out=lnh[b], in_=lnx[b], func=AF.Copy), "s0")
        tile_done_s[t] = hi_done
        P.wait("vector", "s0", hi_done)
        lo_done = P.op("vector", lambda e, b=bi: e.scalar_tensor_tensor(out=lnl[b], in0=lnx[b], scalar=1.0,
                                                                        in1=lnh[b], op0=OP.mult,
                                                                        op1=OP.subtract), "v0")
        tile_lo[t] = lo_done
        P.wait("sync", "s0", hi_done)
        P.wait("sync", "v0", lo_done)
        for src, dst in ((lnh[bi], ln1xT_hi), (lnl[bi], ln1xT_lo)):
            dst3 = dst.rearrange("p (b w) -> p b w", w=S)[:, :, t * 128:(t + 1) * 128]
            src3 = src.rearrange("p (b c) -> p b c", c=128)
            P.dma("sync", dst3, src3, f"tr{bi}", transpose=True)
        tile_tr[t] = P.cnt[f"tr{bi}"]
    tr_all = (P.cnt["tr0"], P.cnt["tr1"])

    free_group("p0")
    wcol_f = [alloc("p1", f"wcol_f{i}", [128, NB * 128], F32) for i in range(2)]
    wcol_h = [alloc("p1", f"wcol_h{i}", [128, NB * 128], BF16) for i in range(2)]
    wcol_l = [alloc("p1", f"wcol_l{i}", [128, NB * 128], BF16) for i in range(2)]
    qkv_h = alloc("p1", "qkv_h", [128, D], BF16)
    qkv_l = alloc("p1", "qkv_l", [128, D], BF16)

    P.wait("tensor", "tr0", tr_all[0])
    P.wait("tensor", "tr1", tr_all[1])
    P.wait("gpsimd", "tr0", tr_all[0])
    P.wait("gpsimd", "tr1", tr_all[1])
    wTs = {0: wqT, 1: wkT, 2: wvT}
    it_t1 = [0] * 48
    it_evv = [0] * 48
    it_evs = [0] * 48
    it_lo = [0] * 48
    for p in range(3):
        qsc = QSC if p == 0 else 1.0
        for eb in range(NB):
            it = p * NB + eb
            bi = it % 2
            if it >= 2:
                P.wait("gpsimd", "t1", it_t1[it - 2])
                P.wait("gpsimd", "v1", it_lo[it - 2])
            src = wTs[p].rearrange("(a q) e -> q a e", q=128)[:, :, eb * 128:(eb + 1) * 128]
            dstf = wcol_f[bi].rearrange("q (a c) -> q a c", c=128)
            dsth = wcol_h[bi].rearrange("q (a c) -> q a c", c=128)
            P.dma("gpsimd", dstf, src, f"wq{bi}")
            wq_done = P.dma("gpsimd", dsth, src, f"wq{bi}")
            P.wait("vector", f"wq{bi}", wq_done)
            if it >= 2:
                P.wait("vector", "t1", it_t1[it - 2])
            it_lo[it] = P.op("vector", lambda e, b=bi: e.scalar_tensor_tensor(
                out=wcol_l[b], in0=wcol_f[b], scalar=1.0, in1=wcol_h[b],
                op0=OP.mult, op1=OP.subtract), "v1")
            grp = it % 2
            banks = pb[grp * 4:grp * 4 + 4]
            P.wait("tensor", f"wq{bi}", wq_done)
            P.wait("tensor", "v1", it_lo[it])
            if it >= 2:
                P.wait("tensor", "v1s", it_evv[it - 2])
                P.wait("tensor", "s1s", it_evs[it - 2])
            n = 0
            for k in range(NB):
                wh = wcol_h[bi][:, k * 128:(k + 1) * 128]
                wl = wcol_l[bi][:, k * 128:(k + 1) * 128]
                for wop, xops in ((wh, (ln1xT_hi, ln1xT_lo)), (wl, (ln1xT_hi,))):
                    for xop in xops:
                        for j in range(4):
                            n += 1
                            first = n <= 4
                            last = n == 192
                            sem = "t1" if last else None
                            P.op("tensor", lambda e, bk=banks[j], w=wop, x=xop, k=k, j=j,
                                 st=first, sp=last or n > 188:
                                 e.matmul(bk, w, x[:, k * S + j * 512:k * S + (j + 1) * 512],
                                          start=st, stop=sp), sem)
            it_t1[it] = P.cnt["t1"]
            P.wait("scalar", "t1", it_t1[it])
            if it >= 1:
                P.wait("scalar", "spq", 32 * it)
            for j in range(4):
                P.op("scalar", lambda e, bk=banks[j], j=j, sc=qsc: e.activation(
                    out=qkv_h[:, j * 512:(j + 1) * 512], in_=bk, func=AF.Copy, scale=sc), "s1s")
            it_evs[it] = P.cnt["s1s"]
            P.wait("vector", "t1", it_t1[it])
            P.wait("vector", "s1s", it_evs[it])
            if it >= 1:
                P.wait("vector", "spq", 32 * it)
            for j in range(4):
                P.op("vector", lambda e, bk=banks[j], j=j, sc=qsc: e.scalar_tensor_tensor(
                    out=qkv_l[:, j * 512:(j + 1) * 512], in0=bk, scalar=sc,
                    in1=qkv_h[:, j * 512:(j + 1) * 512], op0=OP.mult, op1=OP.subtract), "v1s")
            it_evv[it] = P.cnt["v1s"]
            P.wait("sync", "s1s", it_evs[it])
            P.wait("sync", "v1s", it_evv[it])
            P.dma("sync", qkvs[p, 0, eb], qkv_h, "spq")
            P.dma("sync", qkvs[p, 1, eb], qkv_l, "spq")
            for ii in range((it * 256) // 48, ((it + 1) * 256) // 48):
                pwa_step(ii)
    spq_all = P.cnt["spq"]
    P.wait("tensor", "s1s", P.cnt["s1s"])
    P.wait("tensor", "v1s", P.cnt["v1s"])
    pwa_chain()

    if stage <= 1:
        P.wait("sync", "spq", spq_all)
        P.wait("sync", "w8", P.cnt["w8"])
        P.dma("sync", y_out[0:128, 0:64], stats[:, 0:64], "st")
        P.wait("sync", "st", P.cnt["st"])
        P.finish()
        return nc

    free_group("p1")
    free_group("p01")
    qrb = [[alloc("p2", f"q{pt}{i}", [128, S], BF16) for pt in (0, 1)] for i in range(2)]
    krb = [[alloc("p2", f"k{pt}{i}", [128, S], BF16) for pt in (0, 1)] for i in range(2)]
    vrb = [[alloc("p2", f"v{pt}{i}", [128, S], BF16) for pt in (0, 1)] for i in range(2)]
    vtok = [[alloc("p2", f"vt{pt}{i}", [128, S], BF16, is_transpose_dest=True) for pt in (0, 1)]
            for i in range(2)]
    S_sb = [alloc("p2", "S_sb0", [128, 12 * 512], F32), alloc("p2", "S_sb1", [128, 16 * 512], F32)]
    P32 = [alloc("p2", f"P32_{i}", [128, 512], F32) for i in range(2)]
    Phi = [alloc("p2", f"Phi{i}", [128, 512], BF16) for i in range(4)]
    Plo = [alloc("p2", f"Plo{i}", [128, 512], BF16) for i in range(4)]
    sgbuf = [alloc("p2", f"sg{i}", [128, S], F32) for i in range(2)]
    outst = [[alloc("p2", f"os{pt}{i}", [128, S], BF16) for pt in (0, 1)] for i in range(2)]
    t2sb = alloc("p2", "t2sb", [128, 512], F32)
    t3sb = alloc("p2", "t3sb", [128, 512], F32)
    macc = alloc("p2", "macc", [128, 16], F32)
    mpcol = alloc("p2", "mpcol", [128, 1], F32)
    cnrow = alloc("p2", "cnrow", [1, 1], F32)
    srow = alloc("p2", "srow", [1, 512], F32)
    rrow = alloc("p2", "rrow", [1, 512], F32)
    negc = [alloc("p2", f"negc{i}", [128, 1], F32) for i in range(2)]

    SC = pb[0:4]
    AVB = pb[4]
    GB = pb[5]
    SURB = [pb[6], pb[7]]

    P.wait("sync", "spq", spq_all)
    P.wait("tensor", "v0", pw["sw_done"])

    st2 = {"bank_ev": {}, "gb_read": 0, "av_read": 0, "sur_read": [0, 0],
           "head_t2_end": [0] * (H + 2), "spo_after": [0] * (H + 2),
           "head_v2_end": [0] * (H + 2), "sg_free": [0] * (H + 2)}

    def emit_head(h):
        hp = h % 2
        if h >= 2:
            P.wait("sync", "t2", st2["head_t2_end"][h - 1])
        for p, dstl in ((0, qrb), (1, krb), (2, vrb)):
            for part in (0, 1):
                P.dma("sync", dstl[hp][part], qkvs[p, part, h], f"rq{hp}")
        rq_done = P.cnt[f"rq{hp}"]
        P.wait("sync", f"rq{hp}", rq_done)
        for part in (0, 1):
            src3 = vrb[hp][part].rearrange("p (b c) -> p b c", c=128)
            dst3 = vtok[hp][part].rearrange("p (b c) -> p b c", c=128)
            P.dma("sync", dst3, src3, f"trv{hp}", transpose=True)
        trv_done = P.cnt[f"trv{hp}"]

        P.wait("tensor", f"rq{hp}", rq_done)
        P.wait("tensor", f"trv{hp}", trv_done)
        P.wait("tensor", "v2", st2["gb_read"])
        if h >= 2:
            P.wait("scalar", "v2", st2["sg_free"][h - 2])
        sg_done = [0] * 4
        for c in range(4):
            n = 0
            for wop, xop in ((gw_hi, qrb[hp][0]), (gw_hi, qrb[hp][1]), (gw_lo, qrb[hp][0])):
                n += 1
                P.op("tensor", lambda e, w=wop, x=xop, c=c, st=(n == 1), sp=(n == 3):
                     e.matmul(GB, w, x[:, c * 512:(c + 1) * 512], start=st, stop=sp),
                     "t2" if n == 3 else None)
            P.wait("scalar", "t2", P.cnt["t2"])
            sg_done[c] = P.op("scalar", lambda e, c=c, b=hp: e.activation(
                out=sgbuf[b][:, c * 512:(c + 1) * 512], in_=GB, func=AF.Sigmoid,
                scale=float(np.sqrt(128.0)), bias=gbt), "s2")
            if c < 3:
                P.wait("tensor", "s2", sg_done[c])

        split_v2 = {}
        blk_t2 = {}
        for c in range(4):
            nblk = 4 * c + 4
            cp = c % 2
            ssb = S_sb[cp]
            ev_v2 = [0] * nblk
            for b in range(nblk):
                bki = 2 * cp + b % 2
                bank = SC[bki]
                prev_ev = st2["bank_ev"].get(bki, 0)
                if prev_ev:
                    P.wait("tensor", "v2", prev_ev)
                n = 0
                for wop, xop in ((krb[hp][0], qrb[hp][0]), (krb[hp][0], qrb[hp][1]),
                                 (krb[hp][1], qrb[hp][0])):
                    n += 1
                    P.op("tensor", lambda e, bk=bank, w=wop, x=xop, b=b, c=c,
                         st=(n == 1), sp=(n == 3):
                         e.matmul(bk, w[:, b * 128:(b + 1) * 128],
                                  x[:, c * 512:(c + 1) * 512], start=st, stop=sp),
                         "t2" if n == 3 else None)
                t_sc = P.cnt["t2"]
                P.wait("vector", "t2", t_sc)
                if b >= 4 * c:
                    j = b - 4 * c
                    P.op("vector", lambda e, bk=bank, b=b, j=j, s=ssb: e.tensor_tensor(
                        out=s[:, b * 512:(b + 1) * 512], in0=bk, in1=madd[:, j * 512:(j + 1) * 512],
                        op=OP.add), "v2")
                    ev_v2[b] = P.op("vector", lambda e, b=b, s=ssb: e.reduce_max(
                        out=macc[:, b:b + 1], in_=s[:, b * 512:(b + 1) * 512], axis=AX), "v2")
                else:
                    ev_v2[b] = P.op("vector", lambda e, bk=bank, b=b, s=ssb: e.tensor_scalar(
                        out=s[:, b * 512:(b + 1) * 512], in0=bk, scalar1=1.0, scalar2=None,
                        op0=OP.mult, op1=OP.max, accum_out=macc[:, b:b + 1]), "v2")
                st2["bank_ev"][bki] = ev_v2[b]
            P.op("vector", lambda e: e.drain())
            mp_v2 = P.opd("vector", lambda e, n=nblk: e.reduce_max(out=mpcol, in_=macc[:, 0:n],
                                                                   axis=AX), "v2")
            P.wait("tensor", "v2", mp_v2)
            P.wait("tensor", "v2", st2["sur_read"][cp])
            P.op("tensor", lambda e, cp=cp: e.transpose(SURB[cp][0:1, 0:128], mpcol, eye_f), "t2")
            P.wait("vector", "t2", P.cnt["t2"])
            cn_v2 = P.opd("vector", lambda e, cp=cp: e.reduce_max(out=cnrow, in_=SURB[cp][0:1, 0:128],
                                                                  axis=AX, negate=True), "v2")
            P.wait("tensor", "v2", cn_v2)
            nb_t2 = P.op("tensor", lambda e, cp=cp: e.matmul(SURB[cp][:, 128:129], ones_row_f,
                                                             cnrow[0:1, 0:1], start=True, stop=True), "t2")
            P.wait("scalar", "t2", nb_t2)
            P.opd("scalar", lambda e, cp=cp: e.copy(out=negc[cp], in_=SURB[cp][:, 128:129]), "s2")

            av_n = sum_n = 0
            for b in range(nblk):
                rot = b % 4
                if b >= 2:
                    P.wait("scalar", "v2", split_v2[(c, b - 2)])
                ex = P.op("scalar", lambda e, b=b, cp=cp, s=ssb: e.activation(
                    out=P32[b % 2], in_=s[:, b * 512:(b + 1) * 512], func=AF.Exp,
                    bias=negc[cp], scale=1.0), "s2")
                P.wait("vector", "s2", ex)
                if b >= 4:
                    P.wait("vector", "t2", blk_t2[(c, b - 4)])
                P.op("vector", lambda e, b=b, rot=rot: e.tensor_copy(out=Phi[rot], in_=P32[b % 2]), "v2")
                split_v2[(c, b)] = P.op("vector", lambda e, b=b, rot=rot: e.scalar_tensor_tensor(
                    out=Plo[rot], in0=P32[b % 2], scalar=1.0, in1=Phi[rot],
                    op0=OP.mult, op1=OP.subtract), "v2")
                P.wait("tensor", "v2", split_v2[(c, b)])
                if b == 0:
                    P.wait("tensor", "v2", st2["av_read"])
                for k5, (wop, xop) in enumerate((
                        (vtok[hp][0], Phi[rot]), (vtok[hp][0], Plo[rot]), (vtok[hp][1], Phi[rot]),
                        (ones_col_bf, Phi[rot]), (ones_col_bf, Plo[rot]))):
                    if k5 < 3:
                        av_n += 1
                        st, sp, o = av_n == 1, av_n == 3 * nblk, AVB
                        w_ap = wop[:, b * 128:(b + 1) * 128]
                    else:
                        sum_n += 1
                        st, sp, o = sum_n == 1, sum_n == 2 * nblk, SURB[cp][0:1, 0:512]
                        w_ap = wop
                    P.op("tensor", lambda e, o=o, w=w_ap, x=xop, st=st, sp=sp:
                         e.matmul(o, w, x, start=st, stop=sp),
                         "t2" if k5 == 4 else None)
                blk_t2[(c, b)] = P.cnt["t2"]
            mm_end = P.cnt["t2"]

            P.wait("vector", "t2", mm_end)
            P.opd("vector", lambda e, cp=cp: e.tensor_copy(out=srow, in_=SURB[cp][0:1, 0:512]))
            rec_v2 = P.opd("vector", lambda e: e.reciprocal(out=rrow, in_=srow), "v2")
            st2["sur_read"][cp] = rec_v2
            P.wait("tensor", "v2", rec_v2)
            if c == 0:
                P.wait("tensor", "s2", sg_done[3])
            bc_t2 = P.op("tensor", lambda e: e.matmul(GB, ones_row_f, rrow[0:1, 0:512],
                                                      start=True, stop=True), "t2")
            P.wait("vector", "t2", bc_t2)
            if h >= 2 and c == 0:
                P.wait("vector", f"spo{hp}", st2["spo_after"][h - 2])
            P.op("vector", lambda e, c=c, b=hp: e.tensor_tensor(
                out=t2sb, in0=AVB, in1=sgbuf[b][:, c * 512:(c + 1) * 512], op=OP.mult), "v2")
            gb_read = P.op("vector", lambda e: e.tensor_tensor(out=t3sb, in0=t2sb, in1=GB,
                                                               op=OP.mult), "v2")
            st2["gb_read"] = gb_read
            st2["av_read"] = gb_read
            P.wait("scalar", "v2", gb_read)
            if h >= 2 and c == 0:
                P.wait("scalar", f"spo{hp}", st2["spo_after"][h - 2])
            sh = P.op("scalar", lambda e, c=c, b=hp: e.activation(
                out=outst[b][0][:, c * 512:(c + 1) * 512], in_=t3sb, func=AF.Copy), "s2")
            P.wait("vector", "s2", sh)
            P.op("vector", lambda e, c=c, b=hp: e.scalar_tensor_tensor(
                out=outst[b][1][:, c * 512:(c + 1) * 512], in0=t3sb, scalar=1.0,
                in1=outst[b][0][:, c * 512:(c + 1) * 512], op0=OP.mult, op1=OP.subtract), "v2")

        st2["head_t2_end"][h + 1] = P.cnt["t2"]
        st2["sg_free"][h] = P.cnt["v2"]
        st2["head_v2_end"][h] = P.cnt["v2"]
        P.wait("sync", "s2", P.cnt["s2"])
        P.wait("sync", "v2", P.cnt["v2"])
        for part in (0, 1):
            P.dma("sync", outs_h[h, part], outst[hp][part], f"spo{hp}")
        st2["spo_after"][h] = P.cnt[f"spo{hp}"]

    for h in range(heads if heads else H):
        emit_head(h)
    spo_all = (P.cnt.get("spo0", 0), P.cnt.get("spo1", 0))

    if stage <= 2:
        P.wait("sync", "spo0", spo_all[0])
        P.wait("sync", "spo1", spo_all[1])
        P.wait("sync", "w8", w8_all)
        P.dma("sync", y_out[0:128, 0:64], stats[:, 0:64], "st")
        P.wait("sync", "st", P.cnt["st"])
        P.finish()
        return nc

    free_group("p2")
    wocf = [alloc("p3", f"wocf{i}", [128, NB * 128], F32) for i in range(2)]
    woch = [alloc("p3", f"woch{i}", [128, NB * 128], BF16) for i in range(2)]
    wocl = [alloc("p3", f"wocl{i}", [128, NB * 128], BF16) for i in range(2)]
    ortile = alloc("p3", "ortile", [128, 32 * 512], BF16)
    attnT = alloc("p3", "attnT", [128, 16 * 512], F32)
    x1st = alloc("p3", "x1st", [128, 4 * D], F32)
    xrd = alloc("p3", "xrd", [128, 4 * D], F32)

    P.wait("sync", "spo0", spo_all[0])
    P.wait("sync", "spo1", spo_all[1])
    P.wait("gpsimd", "spo0", spo_all[0])
    P.wait("gpsimd", "spo1", spo_all[1])
    ACC = pb[0:4]
    TRB = pb[4:6]
    ev_s3 = {}
    tr_read_v3 = {}
    wo_lo_v3 = {}
    wo_t3 = {}
    def wo_load(it):
        bi = it % 2
        eb = it % NB
        if it >= 2:
            P.wait("gpsimd", "t3", wo_t3.get(it - 2, 0))
        src = woT.rearrange("(a q) e -> q a e", q=128)[:, :, eb * 128:(eb + 1) * 128]
        P.dma("gpsimd", wocf[bi].rearrange("q (a k) -> q a k", k=128), src, f"wo{bi}")
        wo_ld[it] = P.dma("gpsimd", woch[bi].rearrange("q (a k) -> q a k", k=128), src, f"wo{bi}")

    def wo_split(it):
        bi = it % 2
        P.wait("vector", f"wo{bi}", wo_ld[it])
        if it >= 2:
            P.wait("vector", "t3", wo_t3.get(it - 2, 0))
        wo_lo_v3[it] = P.op("vector", lambda e, b=bi: e.scalar_tensor_tensor(
            out=wocl[b], in0=wocf[b], scalar=1.0, in1=woch[b],
            op0=OP.mult, op1=OP.subtract), "v3")

    wo_ld = {}
    wo_load(0)
    wo_load(1)
    wo_split(0)
    for c in range(4):
        if c >= 1:
            P.wait("sync", "t3", P.cnt["t3"])
        for h2 in range(H):
            for part in (0, 1):
                P.dma("sync", ortile[:, (2 * h2 + part) * 512:(2 * h2 + part + 1) * 512],
                      outs_h[h2, part][:, c * 512:(c + 1) * 512], "ro")
        for ts in range(4):
            P.dma("sync", xrd[:, ts * D:(ts + 1) * D],
                  x_in[c * 512 + ts * 128:c * 512 + (ts + 1) * 128, :], "rx")
        ro_done, rx_done = P.cnt["ro"], P.cnt["rx"]
        if c >= 1:
            P.wait("vector", "spx", P.cnt["spx"])
        for eb in range(NB):
            it = c * NB + eb
            bi = it % 2
            bank = ACC[eb % 4]
            P.wait("tensor", f"wo{bi}", wo_ld[it])
            P.wait("tensor", "v3", wo_lo_v3[it])
            P.wait("tensor", "ro", ro_done)
            if it >= 4:
                P.wait("tensor", "s3", ev_s3[it - 4])
            n = 0
            for k in range(NB):
                for wop, part in ((woch[bi], 0), (woch[bi], 1), (wocl[bi], 0)):
                    n += 1
                    P.op("tensor", lambda e, bk=bank, w=wop, k=k, pt=part,
                         st=(n == 1), sp=(n == 48):
                         e.matmul(bk, w[:, k * 128:(k + 1) * 128],
                                  ortile[:, (2 * k + pt) * 512:(2 * k + pt + 1) * 512],
                                  start=st, stop=sp), "t3" if n == 48 else None)
            wo_t3[it] = P.cnt["t3"]
            if it + 2 < 64:
                wo_load(it + 2)
            if it + 1 < 64:
                wo_split(it + 1)
            P.wait("scalar", "t3", wo_t3[it])
            ev_s3[it] = P.op("scalar", lambda e, bk=bank, eb=eb: e.activation(
                out=attnT[:, eb * 512:(eb + 1) * 512], in_=bk, func=AF.Copy), "s3")
            P.wait("tensor", "s3", ev_s3[it])
            P.wait("vector", "rx", rx_done)
            for ts in range(4):
                trb = TRB[(eb * 4 + ts) % 2]
                prev = tr_read_v3.get((eb * 4 + ts) % 2, 0)
                if prev:
                    P.wait("tensor", "v3", prev)
                t_tr = P.op("tensor", lambda e, t=trb, eb=eb, ts=ts: e.transpose(
                    t[:, 0:128], attnT[:, eb * 512 + ts * 128:eb * 512 + (ts + 1) * 128], eye_f), "t3")
                P.wait("vector", "t3", t_tr)
                tr_read_v3[(eb * 4 + ts) % 2] = P.op("vector", lambda e, t=trb, eb=eb, ts=ts:
                    e.tensor_tensor(out=x1st[:, ts * D + eb * 128:ts * D + (eb + 1) * 128],
                                    in0=t[:, 0:128], in1=xrd[:, ts * D + eb * 128:ts * D + (eb + 1) * 128],
                                    op=OP.add), "v3")
            for ii in range((it * 256) // 64, ((it + 1) * 256) // 64):
                pwb_step(ii)
        P.wait("sync", "v3", P.cnt["v3"])
        for ts in range(4):
            P.dma("sync", x1s[c * 512 + ts * 128:c * 512 + (ts + 1) * 128, :],
                  x1st[:, ts * D:(ts + 1) * D], "spx")
    spx_all = P.cnt["spx"]

    w8_all = P.cnt["w8"]
    free_group("p3")
    free_group("pw")
    h_sb = [alloc("p4", f"h{i}", [128, INNER], F32) for i in range(2)]
    x1rd = [[alloc("p4", f"x1rd{p}{i}", [128, D], F32) for i in range(2)] for p in range(2)]
    xq1tok = [alloc("p4", f"xq1tok{i}", [128, D], BF16) for i in range(2)]
    xq1T = alloc("p4", "xq1T", [128, 16 * 256], BF16, is_transpose_dest=True)
    xq2c = [alloc("p4", f"xq2c{i}", [128, 512], BF16) for i in range(4)]
    xq2T = alloc("p4", "xq2T", [128, 64 * 256], BF16, is_transpose_dest=True)
    w1tile = [alloc("p4", f"w1tile{i}", [128, 16 * 256], FP8) for i in range(2)]
    w2tile = [alloc("p4", f"w2tile{i}", [128, 4 * 1024], FP8) for i in range(2)]
    b1ct = [alloc("p4", f"b1ct{i}", [128, 256], F32) for i in range(2)]
    gct = [alloc("p4", f"gct{i}", [128, 512], F32) for i in range(2)]
    bct = [alloc("p4", f"bct{i}", [128, 512], F32) for i in range(2)]
    b2t = alloc("p4", "b2t", [128, D], F32)
    i32a = [alloc("p4", f"i32a{i}", [128, 512], I32) for i in range(2)]
    i32b = [alloc("p4", f"i32b{i}", [128, 512], I32) for i in range(2)]
    sttmp = [alloc("p4", f"sttmp{i}", [128, 512], F32) for i in range(2)]
    jnk = alloc("p4", "jnk", [128, 512], F32)
    macc4 = alloc("p4", "macc4", [128, 128], F32)

    P.wait("gpsimd", "spx", spx_all)
    P.wait("gpsimd", "w8", w8_all)
    P.wait("sync", "spx", spx_all)
    bb_b2 = P.dma("gpsimd", b2t, b2c, "bbz")

    def SCL(cc, tb, i):
        p = (cc % 2) * 2 + tb
        return stats[:, 24 * p + i:24 * p + i + 1]

    CS = {"v4_end": [0] * 10, "st_end": [0] * 10, "mm1_t4": [0] * 10, "mm2_t4": [0] * 10,
          "q1_v4": [[0, 0] for _ in range(10)], "tq1": [0] * 10, "q2_ready": [0] * 10,
          "ev2_v4": [0] * 10, "q2p": [None] * 10}
    tiny = P.opd

    def drain_inc(eng, sem):
        sm = P.sem(sem)
        P.cnt[sem] += 1
        v = P.cnt[sem]
        P.cmds[eng].append(lambda e, s=sm, vv=v: e.drain().then_inc(s, 1))
        return v

    def emit_A(cc):
        cp = cc % 2
        if cc >= 2:
            P.wait("sync", f"st{cp}", CS["st_end"][cc - 2])
        for tb in range(2):
            P.dma("sync", x1rd[cp][tb], x1s[(2 * cc + tb) * 128:(2 * cc + tb + 1) * 128, :],
                  f"r1{cp}")
        r1_done = P.cnt[f"r1{cp}"]
        P.wait("vector", f"r1{cp}", r1_done)
        P.wait("scalar", f"r1{cp}", r1_done)
        if cc >= 2:
            P.wait("vector", "v4", CS["ev2_v4"][cc - 2])
            P.wait("scalar", "v4", CS["ev2_v4"][cc - 2])
        if cc >= 1:
            P.wait("scalar", "tq10", CS["tq1"][cc - 1][0])
            P.wait("scalar", "tq11", CS["tq1"][cc - 1][1])
        sq_s4 = [0, 0]
        for tb in range(2):
            tiny("scalar", lambda e, tb=tb: e.activation(
                out=xq1tok[tb], in_=x1rd[cp][tb], func=AF.Identity, accum_out=SCL(cc, tb, 0)))
            sq_s4[tb] = tiny("scalar", lambda e, tb=tb: e.activation(
                out=xq1tok[tb], in_=x1rd[cp][tb], func=AF.Square, accum_out=SCL(cc, tb, 1)), "s4")
            tiny("vector", lambda e, tb=tb: e.reduce_max(out=SCL(cc, tb, 2), in_=x1rd[cp][tb], axis=AX))
            tiny("vector", lambda e, tb=tb: e.tensor_reduce(out=SCL(cc, tb, 3), in_=x1rd[cp][tb],
                                                            axis=AX, op=OP.min))
        sqr_v4 = [0, 0]
        for tb in range(2):
            P.wait("vector", "s4", sq_s4[tb])
            tiny("vector", lambda e, tb=tb: e.tensor_scalar(out=SCL(cc, tb, 4), in0=SCL(cc, tb, 0),
                 scalar1=1.0 / D, scalar2=None, op0=OP.mult))
            tiny("vector", lambda e, tb=tb: e.tensor_scalar(out=SCL(cc, tb, 5), in0=SCL(cc, tb, 1),
                 scalar1=1.0 / D, scalar2=None, op0=OP.mult))
            tiny("vector", lambda e, tb=tb: e.tensor_tensor(out=SCL(cc, tb, 6), in0=SCL(cc, tb, 4),
                 in1=SCL(cc, tb, 4), op=OP.mult))
            tiny("vector", lambda e, tb=tb: e.tensor_tensor(out=SCL(cc, tb, 5), in0=SCL(cc, tb, 5),
                 in1=SCL(cc, tb, 6), op=OP.subtract))
            tiny("vector", lambda e, tb=tb: e.tensor_scalar(out=SCL(cc, tb, 6), in0=SCL(cc, tb, 5),
                 scalar1=EPS, scalar2=None, op0=OP.add))
            tiny("vector", lambda e, tb=tb: e.reciprocal(out=SCL(cc, tb, 6), in_=SCL(cc, tb, 6)))
            tiny("vector", lambda e, tb=tb: e.tensor_tensor(out=SCL(cc, tb, 8), in0=SCL(cc, tb, 5),
                 in1=SCL(cc, tb, 6), op=OP.mult))
            tiny("vector", lambda e, tb=tb: e.tensor_scalar(out=SCL(cc, tb, 8), in0=SCL(cc, tb, 8),
                 scalar1=EPS, scalar2=None, op0=OP.add))
            sqr_v4[tb] = tiny("vector", lambda e, tb=tb: e.reciprocal(out=SCL(cc, tb, 8),
                              in_=SCL(cc, tb, 8)), "v4")
        r2_s4 = [0, 0]
        for tb in range(2):
            P.wait("scalar", "v4", sqr_v4[tb])
            tiny("scalar", lambda e, tb=tb: e.activation(out=SCL(cc, tb, 7), in_=SCL(cc, tb, 6),
                                                         func=AF.Sqrt))
            r2_s4[tb] = tiny("scalar", lambda e, tb=tb: e.activation(out=SCL(cc, tb, 9),
                             in_=SCL(cc, tb, 8), func=AF.Sqrt), "s4")
        qs_v4 = [0, 0]
        for tb in range(2):
            P.wait("vector", "s4", r2_s4[tb])
            tiny("vector", lambda e, tb=tb: e.tensor_tensor(out=SCL(cc, tb, 10), in0=SCL(cc, tb, 2),
                 in1=SCL(cc, tb, 4), op=OP.subtract))
            tiny("vector", lambda e, tb=tb: e.tensor_tensor(out=SCL(cc, tb, 11), in0=SCL(cc, tb, 4),
                 in1=SCL(cc, tb, 3), op=OP.subtract))
            tiny("vector", lambda e, tb=tb: e.tensor_tensor(out=SCL(cc, tb, 10), in0=SCL(cc, tb, 10),
                 in1=SCL(cc, tb, 11), op=OP.max))
            tiny("vector", lambda e, tb=tb: e.tensor_tensor(out=SCL(cc, tb, 11), in0=SCL(cc, tb, 7),
                 in1=SCL(cc, tb, 9), op=OP.mult))
            tiny("vector", lambda e, tb=tb: e.tensor_tensor(out=SCL(cc, tb, 10), in0=SCL(cc, tb, 10),
                 in1=SCL(cc, tb, 11), op=OP.mult))
            tiny("vector", lambda e, tb=tb: e.tensor_scalar(out=SCL(cc, tb, 10), in0=SCL(cc, tb, 10),
                 scalar1=1e-5, scalar2=None, op0=OP.max))
            tiny("vector", lambda e, tb=tb: e.reciprocal(out=SCL(cc, tb, 10), in_=SCL(cc, tb, 10)))
            tiny("vector", lambda e, tb=tb: e.tensor_scalar(out=SCL(cc, tb, 12), in0=SCL(cc, tb, 10),
                 scalar1=127.0, scalar2=None, op0=OP.mult))
            tiny("vector", lambda e, tb=tb: e.tensor_tensor(out=SCL(cc, tb, 13), in0=SCL(cc, tb, 11),
                 in1=SCL(cc, tb, 12), op=OP.mult))
            tiny("vector", lambda e, tb=tb: e.tensor_tensor(out=SCL(cc, tb, 14), in0=SCL(cc, tb, 4),
                 in1=SCL(cc, tb, 13), op=OP.mult))
            tiny("vector", lambda e, tb=tb: e.tensor_scalar(out=SCL(cc, tb, 14), in0=SCL(cc, tb, 14),
                 scalar1=-1.0, scalar2=None, op0=OP.mult))
            tiny("vector", lambda e, tb=tb: e.tensor_tensor(out=SCL(cc, tb, 15), in0=SCL(cc, tb, 12),
                 in1=swb[:, 0:1], op=OP.mult))
            qs_v4[tb] = tiny("vector", lambda e, tb=tb: e.reciprocal(out=SCL(cc, tb, 15),
                             in_=SCL(cc, tb, 15)), "v4")
        uq = {}
        for tb in range(2):
            P.wait("scalar", "v4", qs_v4[tb])
            for pc in range(4):
                pp = pc % 2
                prev = uq.get(pp, 0)
                if prev:
                    P.wait("scalar", "v4", prev)
                u = P.op("scalar", lambda e, tb=tb, pc=pc, pp=pp: e.activation(
                    out=i32a[pp], in_=x1rd[cp][tb][:, pc * 512:(pc + 1) * 512], func=AF.Identity,
                    scale=SCL(cc, tb, 13), bias=SCL(cc, tb, 14)), "s4")
                P.wait("vector", "s4", u)
                uq[pp] = P.op("vector", lambda e, tb=tb, pc=pc, pp=pp: e.tensor_copy(
                    out=xq1tok[tb][:, pc * 512:(pc + 1) * 512], in_=i32a[pp]), "v4")
        CS["q1_v4"][cc] = [uq.get(0, 0), uq.get(1, 0), P.cnt["v4"]]

    def emit_tq1(cc):
        P.wait("sync", "v4", CS["q1_v4"][cc][2])
        if cc >= 1:
            P.wait("sync", "t4", CS["mm1_t4"][cc - 1])
        for tb in range(2):
            dst = xq1T.rearrange("p (b w) -> p b w", w=256)[:, :, tb * 128:(tb + 1) * 128]
            src = xq1tok[tb].rearrange("p (b c) -> p b c", c=128)
            P.dma("sync", dst, src, f"tq1{tb}", transpose=True)
        CS["tq1"][cc] = (P.cnt["tq10"], P.cnt["tq11"])

    def emit_B1(cc):
        cp = cc % 2
        w1_t4 = {}
        ev1_v4 = {}
        ev1p = {}
        for ic in range(32):
            icp = ic % 2
            prev = w1_t4.get(icp, CS["mm2_t4"][cc - 1] if cc >= 1 else 0)
            if prev:
                P.wait("gpsimd", "t4", prev)
            wsrc3 = w1q8.rearrange("(a q) i -> q a i", q=128)[:, :, ic * 256:(ic + 1) * 256]
            wdst3 = w1tile[icp].rearrange("q (a w) -> q a w", w=256)
            wf_done = P.dma("gpsimd", wdst3, wsrc3, f"wf{icp}")
            pe_ev = ev1_v4.get((1, icp), 0)
            if pe_ev:
                P.wait("gpsimd", "v4", pe_ev)
            bb_done = P.dma("gpsimd", b1ct[icp], b1c[:, ic * 256:(ic + 1) * 256], f"bb{icp}")
            P.wait("tensor", f"wf{icp}", wf_done)
            if ic == 0:
                P.wait("tensor", "tq10", CS["tq1"][cc][0])
                P.wait("tensor", "tq11", CS["tq1"][cc][1])
                if cc >= 1:
                    P.wait("tensor", "v4", CS["v4_end"][cc - 1])
            for tb in range(2):
                bank = pb[tb * 2 + icp][:, 0:256]
                pe = ev1_v4.get((tb, icp), 0)
                if pe:
                    P.wait("tensor", "v4", pe)
                for k in range(NB):
                    P.op("tensor", lambda e, bk=bank, k=k, tb=tb, ic=ic,
                         st=(k == 0), sp=(k == NB - 1):
                         e.matmul(bk, xq1T[:, k * 256 + tb * 128:k * 256 + (tb + 1) * 128],
                                  w1tile[ic % 2][:, k * 256:(k + 1) * 256], start=st, stop=sp),
                         "t4" if k == NB - 1 else None)
            w1_t4[icp] = P.cnt["t4"]
            t4n = P.cnt["t4"]
            P.wait("vector", "t4", t4n)
            P.wait("vector", f"bb{icp}", bb_done)
            for tb in range(2):
                bank = pb[tb * 2 + icp][:, 0:256]
                ev1_v4[(tb, icp)] = P.op("vector", lambda e, bk=bank, tb=tb, ic=ic, icp=icp:
                    e.scalar_tensor_tensor(out=h_sb[tb][:, ic * 256:(ic + 1) * 256], in0=bk,
                                           scalar=SCL(cc, tb, 15), in1=b1ct[icp],
                                           op0=OP.mult, op1=OP.add), "v4")
                ev1p[(tb, ic)] = ev1_v4[(tb, icp)]
        CS["mm1_t4"][cc] = P.cnt["t4"]
        CS["v4_end"][cc] = P.cnt["v4"]
        sw = 0
        for jb in range(0, 16, 4):
            for j in range(jb, jb + 4):
                for tb in range(2):
                    g = ev1p[(tb, 2 * j + 1)]
                    if g > sw:
                        P.wait("scalar", "v4", g)
                        sw = g
                    P.op("scalar", lambda e, tb=tb, j=j: e.activation(
                        out=h_sb[tb][:, j * 512:(j + 1) * 512],
                        in_=h_sb[tb][:, j * 512:(j + 1) * 512], func=AF.Silu), None)
            for j in range(jb, jb + 4):
                for tb in range(2):
                    P.op("scalar", lambda e, tb=tb, j=j: e.activation(
                        out=h_sb[tb][:, j * 512:(j + 1) * 512],
                        in_=h_sb[tb][:, j * 512:(j + 1) * 512], func=AF.Gelu,
                        accum_out=macc4[:, 16 * tb + j:16 * tb + j + 1]), None)
            for j in range(jb, jb + 4):
                for tb in range(2):
                    P.op("scalar", lambda e, tb=tb, j=j: e.activation(
                        out=jnk, in_=h_sb[tb][:, j * 512:(j + 1) * 512], func=AF.Square,
                        accum_out=macc4[:, 32 + 16 * tb + j:32 + 16 * tb + j + 1]), None)
        lnsq_s4 = drain_inc("scalar", "s4")
        rh_v4 = [0, 0]
        for tb in range(2):
            P.wait("vector", "s4", lnsq_s4)
            tiny("vector", lambda e, tb=tb: e.reduce_sum(out=SCL(cc, tb, 16),
                 in_=macc4[:, 16 * tb:16 * tb + 16], axis=AX))
            tiny("vector", lambda e, tb=tb: e.reduce_sum(out=SCL(cc, tb, 17),
                 in_=macc4[:, 32 + 16 * tb:32 + 16 * tb + 16], axis=AX))
            tiny("vector", lambda e, tb=tb: e.tensor_scalar(out=SCL(cc, tb, 16), in0=SCL(cc, tb, 16),
                 scalar1=1.0 / INNER, scalar2=None, op0=OP.mult))
            tiny("vector", lambda e, tb=tb: e.tensor_scalar(out=SCL(cc, tb, 17), in0=SCL(cc, tb, 17),
                 scalar1=1.0 / INNER, scalar2=None, op0=OP.mult))
            tiny("vector", lambda e, tb=tb: e.tensor_tensor(out=SCL(cc, tb, 20), in0=SCL(cc, tb, 16),
                 in1=SCL(cc, tb, 16), op=OP.mult))
            tiny("vector", lambda e, tb=tb: e.tensor_tensor(out=SCL(cc, tb, 17), in0=SCL(cc, tb, 17),
                 in1=SCL(cc, tb, 20), op=OP.subtract))
            tiny("vector", lambda e, tb=tb: e.tensor_scalar(out=SCL(cc, tb, 17), in0=SCL(cc, tb, 17),
                 scalar1=EPS, scalar2=None, op0=OP.add))
            rh_v4[tb] = tiny("vector", lambda e, tb=tb: e.reciprocal(out=SCL(cc, tb, 17),
                             in_=SCL(cc, tb, 17)), "v4")
        nmh_v4 = [0, 0]
        for tb in range(2):
            P.wait("scalar", "v4", rh_v4[tb])
            s1 = tiny("scalar", lambda e, tb=tb: e.activation(out=SCL(cc, tb, 18),
                      in_=SCL(cc, tb, 17), func=AF.Sqrt), "s4")
            P.wait("vector", "s4", s1)
            tiny("vector", lambda e, tb=tb: e.tensor_tensor(out=SCL(cc, tb, 19), in0=SCL(cc, tb, 16),
                 in1=SCL(cc, tb, 18), op=OP.mult))
            nmh_v4[tb] = tiny("vector", lambda e, tb=tb: e.tensor_scalar(out=SCL(cc, tb, 19),
                 in0=SCL(cc, tb, 19), scalar1=-1.0, scalar2=None, op0=OP.mult), "v4")
        bgr = {}
        for ic in range(16):
            icp = ic % 2
            pg = bgr.get(icp, 0)
            if pg:
                P.wait("gpsimd", "v4", pg)
            P.dma("gpsimd", gct[icp], lngc[:, ic * 512:(ic + 1) * 512], f"bg{icp}")
            gb_done = P.dma("gpsimd", bct[icp], lnbc[:, ic * 512:(ic + 1) * 512], f"bg{icp}")
            for tb in range(2):
                if ic == 0:
                    P.wait("scalar", "v4", nmh_v4[tb])
                zs = P.op("scalar", lambda e, tb=tb, ic=ic: e.activation(
                    out=sttmp[ic % 2], in_=h_sb[tb][:, ic * 512:(ic + 1) * 512], func=AF.Identity,
                    scale=SCL(cc, tb, 18), bias=SCL(cc, tb, 19)), "s4")
                P.wait("vector", "s4", zs)
                P.wait("vector", f"bg{icp}", gb_done)
                P.op("vector", lambda e, ic=ic, icp=icp: e.tensor_tensor(
                    out=sttmp[ic % 2], in0=sttmp[ic % 2], in1=gct[icp], op=OP.mult), "v4")
                yv = P.op("vector", lambda e, tb=tb, ic=ic, icp=icp: e.tensor_tensor(
                    out=h_sb[tb][:, ic * 512:(ic + 1) * 512], in0=sttmp[ic % 2], in1=bct[icp],
                    op=OP.add), "v4")
                P.wait("scalar", "v4", yv)
                P.op("vector", lambda e, tb=tb, ic=ic: e.tensor_tensor(
                    out=jnk, in0=h_sb[tb][:, ic * 512:(ic + 1) * 512],
                    in1=h_sb[tb][:, ic * 512:(ic + 1) * 512], op=OP.mult), None)
                P.op("vector", lambda e, tb=tb, ic=ic: e.reduce_sum(
                    out=macc4[:, 96 + 16 * tb + ic:96 + 16 * tb + ic + 1], in_=jnk, axis=AX), None)
                P.op("vector", lambda e, tb=tb, ic=ic: e.reduce_max(
                    out=macc4[:, 64 + 16 * tb + ic:64 + 16 * tb + ic + 1],
                    in_=h_sb[tb][:, ic * 512:(ic + 1) * 512], axis=AX,
                    apply_absolute_value=True), None)
            bgr[icp] = yv
        drain_inc("vector", "v4")
        q2_v4 = [0, 0]
        for tb in range(2):
            tiny("vector", lambda e, tb=tb: e.reduce_sum(out=SCL(cc, tb, 20),
                 in_=macc4[:, 96 + 16 * tb:96 + 16 * tb + 16], axis=AX))
            tiny("vector", lambda e, tb=tb: e.tensor_scalar(out=SCL(cc, tb, 20), in0=SCL(cc, tb, 20),
                 scalar1=1.0 / INNER, scalar2=EPS, op0=OP.mult, op1=OP.add))
            q2_v4[tb] = tiny("vector", lambda e, tb=tb: e.reciprocal(out=SCL(cc, tb, 20),
                             in_=SCL(cc, tb, 20)), "v4")
        r2b_s4 = [0, 0]
        for tb in range(2):
            P.wait("scalar", "v4", q2_v4[tb])
            r2b_s4[tb] = tiny("scalar", lambda e, tb=tb: e.activation(out=SCL(cc, tb, 21),
                              in_=SCL(cc, tb, 20), func=AF.Sqrt), "s4")
        c2_v4 = [0, 0]
        for tb in range(2):
            P.wait("vector", "s4", r2b_s4[tb])
            tiny("vector", lambda e, tb=tb: e.reduce_max(out=SCL(cc, tb, 22),
                 in_=macc4[:, 64 + 16 * tb:64 + 16 * tb + 16], axis=AX))
            tiny("vector", lambda e, tb=tb: e.tensor_tensor(out=SCL(cc, tb, 22), in0=SCL(cc, tb, 22),
                 in1=SCL(cc, tb, 21), op=OP.mult))
            tiny("vector", lambda e, tb=tb: e.tensor_scalar(out=SCL(cc, tb, 22), in0=SCL(cc, tb, 22),
                 scalar1=1e-5, scalar2=None, op0=OP.max))
            tiny("vector", lambda e, tb=tb: e.reciprocal(out=SCL(cc, tb, 22), in_=SCL(cc, tb, 22)))
            tiny("vector", lambda e, tb=tb: e.tensor_scalar(out=SCL(cc, tb, 22), in0=SCL(cc, tb, 22),
                 scalar1=127.0, scalar2=None, op0=OP.mult))
            tiny("vector", lambda e, tb=tb: e.tensor_tensor(out=SCL(cc, tb, 23), in0=SCL(cc, tb, 22),
                 in1=swb[:, 1:2], op=OP.mult))
            tiny("vector", lambda e, tb=tb: e.reciprocal(out=SCL(cc, tb, 23), in_=SCL(cc, tb, 23)))
            c2_v4[tb] = tiny("vector", lambda e, tb=tb: e.tensor_tensor(out=SCL(cc, tb, 22),
                 in0=SCL(cc, tb, 22), in1=SCL(cc, tb, 21), op=OP.mult), "v4")
        tq2_use = ({r: CS["q2_ready"][cc - 1][f"tq2_{r}"] for r in range(4)}
                   if cc >= 1 else {})
        uf2 = {}
        q2p = {}
        for ic in range(16):
            for tb in range(2):
                rr = (2 * ic + tb) % 2
                rot4 = (2 * ic + tb) % 4
                if ic == 0 and tb == 0:
                    for t2b in range(2):
                        P.wait("scalar", "v4", c2_v4[t2b])
                pu = uf2.get(rr, 0)
                if pu:
                    P.wait("scalar", "v4", pu)
                u2 = P.op("scalar", lambda e, tb=tb, ic=ic, rr=rr: e.activation(
                    out=i32b[rr], in_=h_sb[tb][:, ic * 512:(ic + 1) * 512], func=AF.Identity,
                    scale=SCL(cc, tb, 22), bias=0.0), "s4")
                P.wait("vector", "s4", u2)
                pt = tq2_use.get(rot4, 0)
                if pt:
                    P.wait("vector", f"tq2_{rot4}", pt)
                vq = P.op("vector", lambda e, rr=rr, rot4=rot4: e.tensor_copy(
                    out=xq2c[rot4], in_=i32b[rr]), "v4")
                uf2[rr] = vq
                P.wait("sync", "v4", vq)
                if cc >= 1 and ic == 0 and tb == 0:
                    P.wait("sync", "t4", CS["mm2_t4"][cc - 1])
                dst = xq2T.rearrange("p (b w) -> p b w", w=256)[:, 4 * ic:4 * ic + 4,
                                                               tb * 128:(tb + 1) * 128]
                srcq = xq2c[rot4].rearrange("p (b c) -> p b c", c=128)
                tq2_use[rot4] = P.dma("sync", dst, srcq, f"tq2_{rot4}", transpose=True)
                q2p.setdefault(ic, []).append((f"tq2_{rot4}", tq2_use[rot4]))
        CS["q2_ready"][cc] = {f"tq2_{r}": P.cnt.get(f"tq2_{r}", 0) for r in range(4)}
        CS["q2p"][cc] = q2p

    def emit_B2(cc):
        cp = cc % 2
        ev_half0 = 0
        w2_t4 = ({0: CS["mm2_t4"][cc - 1], 1: CS["mm2_t4"][cc - 1]} if cc >= 1 else {})
        for half in range(2):
            for ic in range(16):
                icp = ic % 2
                prev = w2_t4.get(icp, 0)
                if prev:
                    P.wait("gpsimd", "t4", prev)
                w2src = w2q8.rearrange("(a q) d -> q a d", q=128)[:, 4 * ic:4 * ic + 4,
                                                                 half * 1024:(half + 1) * 1024]
                w2dst = w2tile[icp].rearrange("q (a d) -> q a d", d=1024)
                wf2 = P.dma("gpsimd", w2dst, w2src, f"w2{icp}")
                P.wait("tensor", f"w2{icp}", wf2)
                if half == 0:
                    for nm, cnt in CS["q2p"][cc][ic]:
                        P.wait("tensor", nm, cnt)
                if half == 0 and ic == 0 and cc >= 1:
                    P.wait("tensor", "v4", CS["ev2_v4"][cc - 1])
                if half == 1 and ic == 0:
                    P.wait("tensor", "v4", ev_half0)
                for k in range(4):
                    for tb in range(2):
                        for dc in range(2):
                            st4 = (ic == 0 and k == 0)
                            sp4 = (ic == 15 and k == 3)
                            last = (k == 3 and tb == 1 and dc == 1)
                            P.op("tensor", lambda e, bk=pb[4 + tb * 2 + dc], ic=ic, k=k, tb=tb,
                                 dc=dc, st=st4, sp=sp4, half=half:
                                 e.matmul(bk, xq2T[:, (4 * ic + k) * 256 + tb * 128:
                                                   (4 * ic + k) * 256 + (tb + 1) * 128],
                                          w2tile[ic % 2][:, k * 1024 + dc * 512:
                                                         k * 1024 + (dc + 1) * 512],
                                          start=st, stop=sp),
                                 "t4" if last else None)
                w2_t4[icp] = P.cnt["t4"]
            mm_done = P.cnt["t4"]
            P.wait("vector", "t4", mm_done)
            if half == 0:
                pass
            if cc == 0 and half == 0:
                P.wait("vector", "bbz", bb_b2)
            for tb in range(2):
                for dc in range(2):
                    col = half * 1024 + dc * 512
                    P.op("vector", lambda e, bk=pb[4 + tb * 2 + dc], tb=tb, col=col:
                         e.scalar_tensor_tensor(out=sttmp[tb], in0=bk, scalar=SCL(cc, tb, 23),
                                                in1=b2t[:, col:col + 512],
                                                op0=OP.mult, op1=OP.add), "v4")
                    P.op("vector", lambda e, tb=tb, col=col: e.tensor_tensor(
                        out=x1rd[cp][tb][:, col:col + 512], in0=sttmp[tb],
                        in1=x1rd[cp][tb][:, col:col + 512], op=OP.add), "v4")
            ev_half0 = P.cnt["v4"]
        CS["mm2_t4"][cc] = P.cnt["t4"]
        CS["ev2_v4"][cc] = P.cnt["v4"]
        P.wait("sync", "v4", P.cnt["v4"])
        for tb in range(2):
            P.dma("sync", y_out[(2 * cc + tb) * 128:(2 * cc + tb + 1) * 128, :],
                  x1rd[cp][tb], f"st{cp}")
        CS["st_end"][cc] = P.cnt[f"st{cp}"]

    emit_A(0)
    emit_tq1(0)
    emit_B1(0)
    emit_A(1)
    for i in range(1, 8):
        emit_B2(i - 1)
        emit_tq1(i)
        emit_B1(i)
        if i + 1 < 8:
            emit_A(i + 1)
    emit_B2(7)

    P.wait("sync", "st0", P.cnt["st0"])
    P.wait("sync", "st1", P.cnt["st1"])
    P.finish()
    return nc


def wdst_idx(idx, w1q8, w2q8):
    if idx < 128:
        r, cc = divmod(idx, 8)
        return w1q8[r * 128:(r + 1) * 128, cc * 1024:(cc + 1) * 1024]
    i, c2 = divmod(idx - 128, 2)
    return w2q8[i * 128:(i + 1) * 128, c2 * 1024:(c2 + 1) * 1024]


def prep_inputs(ins, b):
    co = np.ascontiguousarray
    d = {
        "x_in": co(np.asarray(ins["L0_attn_x"])[b]).astype(np.float32),
        "wqT": co(np.asarray(ins["L0_attn_q_w"]).T),
        "wkT": co(np.asarray(ins["L0_attn_k_w"]).T),
        "wvT": co(np.asarray(ins["L0_attn_v_w"]).T),
        "woT": co(np.asarray(ins["L0_attn_o_w"]).T),
        "gwT": co(np.asarray(ins["L0_attn_gate_w"]).T),
        "gb": co(np.asarray(ins["L0_attn_gate_b"]).reshape(HD, 1)),
        "w1T": co(np.asarray(ins["L0_ffn_w1"]).T),
        "w2T": co(np.asarray(ins["L0_ffn_w2"]).T),
        "b1c": co(np.broadcast_to(np.asarray(ins["L0_ffn_b1"])[None, :], (128, INNER))),
        "lngc": co(np.broadcast_to(np.asarray(ins["L0_ffn_ln_g"])[None, :], (128, INNER))),
        "lnbc": co(np.broadcast_to(np.asarray(ins["L0_ffn_ln_b"])[None, :], (128, INNER))),
        "b2c": co(np.broadcast_to(np.asarray(ins["L0_ffn_b2"])[None, :], (128, D))),
    }
    return {k: np.asarray(v, dtype=np.float32) for k, v in d.items()}


def kernel(**inputs):
    nc = build_nc(stage=4)
    x = np.asarray(inputs["L0_attn_x"])
    ncores = x.shape[0]
    common = None
    in_maps = []
    for b in range(ncores):
        d = prep_inputs(inputs, b)
        if common is None:
            common = {k: v for k, v in d.items() if k != "x_in"}
        else:
            for k in common:
                d[k] = common[k]
        in_maps.append(d)
    res = run_bass_kernel_spmd(nc, in_maps, core_ids=list(range(ncores)))
    out = np.stack([np.asarray(r["y_out"]) for r in res.results], axis=0)
    return out.astype(np.float32)


# version 10
# speedup vs baseline: 1.0820x; 1.0141x over previous
import numpy as np
import ml_dtypes
import concourse.bass as bass
import concourse.mybir as mybir
from concourse.bass_utils import run_bass_kernel_spmd

F32 = mybir.dt.float32
BF16 = mybir.dt.bfloat16
FP8 = mybir.dt.float8e4
I32 = mybir.dt.int32
OP = mybir.AluOpType
AF = mybir.ActivationFunctionType
AX = mybir.AxisListType.X

S, D, H, HD, INNER = 2048, 2048, 16, 128, 8192
NB = D // 128
NT = S // 128
EPS = 1e-5
QSC = float(1.0 / np.sqrt(128.0))
ENGINES = ("sync", "tensor", "vector", "scalar", "gpsimd")


class Prog:
    def __init__(self, nc):
        self.nc = nc
        self.cmds = {e: [] for e in ENGINES}
        self.cnt = {}
        self.sems = {}
        self._sem_cms = []

    def sem(self, name):
        if name not in self.sems:
            cm = self.nc.semaphore(name)
            self.sems[name] = cm.__enter__()
            self._sem_cms.append(cm)
            self.cnt[name] = 0
        return self.sems[name]

    def dma(self, q, out, in_, sem, transpose=False):
        s = self.sem(sem)
        self.cnt[sem] += 16
        if transpose:
            self.cmds[q].append(lambda e, o=out, i=in_: e.dma_start_transpose(out=o, in_=i).then_inc(s, 16))
        else:
            self.cmds[q].append(lambda e, o=out, i=in_: e.dma_start(out=o, in_=i).then_inc(s, 16))
        return self.cnt[sem]

    def wait(self, eng, sem, val=None):
        s = self.sem(sem)
        v = self.cnt[sem] if val is None else val
        if v > 0:
            self.cmds[eng].append(lambda e, vv=v: e.wait_ge(s, vv))

    def op(self, eng, fn, sem=None):
        if sem is None:
            self.cmds[eng].append(fn)
            return None
        s = self.sem(sem)
        self.cnt[sem] += 1
        v = self.cnt[sem]
        self.cmds[eng].append(lambda e, f=fn: f(e).then_inc(s, 1))
        return v

    def opd(self, eng, fn, sem=None):
        self.cmds[eng].append(fn)
        if sem is None:
            self.cmds[eng].append(lambda e: e.drain())
            return None
        s = self.sem(sem)
        self.cnt[sem] += 1
        v = self.cnt[sem]
        self.cmds[eng].append(lambda e: e.drain().then_inc(s, 1))
        return v

    def finish(self):
        nc = self.nc
        def mk(eng):
            cmds = self.cmds[eng]
            def run(e):
                for fn in cmds:
                    fn(e)
            return run
        with nc.Block() as block:
            for eng in ENGINES:
                getattr(block, eng)(mk(eng))


def build_nc(stage=4, dbg=(), heads=None):
    nc = bass.Bass("TRN2", target_bir_lowering=False, detect_race_conditions=False)
    P = Prog(nc)

    x_in = nc.dram_tensor("x_in", [S, D], F32, kind="ExternalInput").ap()
    wqT = nc.dram_tensor("wqT", [D, D], F32, kind="ExternalInput").ap()
    wkT = nc.dram_tensor("wkT", [D, D], F32, kind="ExternalInput").ap()
    wvT = nc.dram_tensor("wvT", [D, D], F32, kind="ExternalInput").ap()
    woT = nc.dram_tensor("woT", [D, D], F32, kind="ExternalInput").ap()
    gwT = nc.dram_tensor("gwT", [HD, HD], F32, kind="ExternalInput").ap()
    gb = nc.dram_tensor("gb", [HD, 1], F32, kind="ExternalInput").ap()
    w1T = nc.dram_tensor("w1T", [D, INNER], F32, kind="ExternalInput").ap()
    w2T = nc.dram_tensor("w2T", [INNER, D], F32, kind="ExternalInput").ap()
    b1c = nc.dram_tensor("b1c", [128, INNER], F32, kind="ExternalInput").ap()
    lngc = nc.dram_tensor("lngc", [128, INNER], F32, kind="ExternalInput").ap()
    lnbc = nc.dram_tensor("lnbc", [128, INNER], F32, kind="ExternalInput").ap()
    b2c = nc.dram_tensor("b2c", [128, D], F32, kind="ExternalInput").ap()
    y_out = nc.dram_tensor("y_out", [S, D], F32, kind="ExternalOutput").ap()

    def scratch(name, shape, dt):
        kind = "ExternalOutput" if name in dbg else "Internal"
        return nc.dram_tensor(name, shape, dt, kind=kind).ap()

    qkvs = scratch("qkvs", [3, 2, H, 128, S], BF16)
    outs_h = scratch("outs_h", [H, 2, 128, S], BF16)
    x1s = scratch("x1s", [S, D], F32)
    w1q8 = scratch("w1q8", [D, INNER], FP8)
    w2q8 = scratch("w2q8", [INNER, D], FP8)

    A = nc.alloc_sbuf_tensor
    eye_f = A("eye_f", [128, 128], F32).ap()
    ones_col_bf = A("ones_col_bf", [128, 1], BF16).ap()
    ones_row_f = A("ones_row_f", [1, 128], F32).ap()
    negones_row_f = A("negones_row_f", [1, 128], F32).ap()
    madd = A("madd", [128, 4 * 512], F32).ap()
    gbt = A("gbt", [128, 1], F32).ap()
    gw_hi = A("gw_hi", [128, 128], BF16).ap()
    gw_lo = A("gw_lo", [128, 128], BF16).ap()
    gwf = A("gwf", [128, 128], F32).ap()
    swb = A("swb", [128, 2], F32).ap()
    stats = A("stats", [128, 96], F32).ap()

    tone = A("tone", [1, 8], F32).ap()

    pb = [nc.alloc_psum_tensor(f"pb{i}", [128, 512], F32).ap() for i in range(8)]

    scopes = {}

    def alloc(group, name, shape, dt, **kw):
        cm = nc.sbuf_tensor(name, shape, dt, **kw)
        t = cm.__enter__()
        scopes.setdefault(group, []).append(cm)
        return t.ap()

    def free_group(group):
        for cm in reversed(scopes.pop(group, [])):
            cm.__exit__(None, None, None)


    wtile = [alloc("pw", f"wtile{i}", [128, 1024], F32) for i in range(2)]
    wjunk = alloc("pw", "wjunk", [128, 1024], F32)
    wtern = [alloc("pw", f"wtern{i}", [128, 1024], FP8) for i in range(2)]
    wp1 = alloc("pw", "wp1", [128, 1024], F32)
    wp2 = alloc("pw", "wp2", [128, 1024], F32)
    wabs = alloc("pw", "wabs", [128, 260], F32)
    wth = alloc("pw", "wth", [128, 4], F32)
    NPW = 256

    def wsrc(idx, q8=False):
        if idx < 128:
            r, cc = divmod(idx, 8)
            t = w1q8 if q8 else w1T
            return t[r * 128:(r + 1) * 128, cc * 1024:(cc + 1) * 1024]
        i, c2 = divmod(idx - 128, 2)
        return (w2q8 if q8 else w2T)[i * 128:(i + 1) * 128, c2 * 1024:(c2 + 1) * 1024]

    pw = {"lda": [0] * 260, "absdone": [0] * 256, "ldb": [0] * 260, "tern": [0] * 256,
          "sw_done": 0}

    def pwa_step(idx):
        bi = idx % 2
        if idx < 2:
            pw["lda"][idx] = P.dma("scalar", wtile[bi], wsrc(idx), f"wt{bi}")
        P.wait("scalar", f"wt{bi}", pw["lda"][idx])
        pw["absdone"][idx] = P.opd("scalar", lambda e, b=bi, i=idx: e.activation(
            out=wjunk, in_=wtile[b], func=AF.Abs, accum_out=wabs[:, i:i + 1]), "s0")
        if idx + 2 < NPW:
            pw["lda"][idx + 2] = P.dma("scalar", wtile[(idx + 2) % 2], wsrc(idx + 2), f"wt{bi}")

    def pwa_chain():
        P.wait("vector", "s0", pw["absdone"][NPW - 1])
        P.opd("vector", lambda e: e.reduce_sum(out=wabs[:, 256:257], in_=wabs[:, 0:128], axis=AX))
        vred = P.opd("vector", lambda e: e.reduce_sum(out=wabs[:, 257:258], in_=wabs[:, 128:256],
                                                      axis=AX), "v0")
        P.wait("tensor", "v0", vred)
        P.op("tensor", lambda e: e.transpose(pb[0][0:1, 0:128], wabs[:, 256:257], eye_f), "t0")
        t2v = P.op("tensor", lambda e: e.transpose(pb[1][0:1, 0:128], wabs[:, 257:258], eye_f), "t0")
        P.wait("vector", "t0", t2v)
        vsw = 0
        for i in range(2):
            P.opd("vector", lambda e, i=i: e.reduce_sum(out=tone[0:1, i:i + 1],
                                                        in_=pb[i][0:1, 0:128], axis=AX))
            P.opd("vector", lambda e, i=i: e.tensor_scalar(out=tone[0:1, i:i + 1],
                 in0=tone[0:1, i:i + 1], scalar1=1.0 / (D * INNER), scalar2=1e-5,
                 op0=OP.mult, op1=OP.max))
            vsw = P.opd("vector", lambda e, i=i: e.reciprocal(out=tone[0:1, i:i + 1],
                        in_=tone[0:1, i:i + 1]), "v0")
        P.wait("tensor", "v0", vsw)
        P.op("tensor", lambda e: e.matmul(pb[2][:, 0:1], ones_row_f, tone[0:1, 0:1],
                                          start=True, stop=True), "t0")
        tb2 = P.op("tensor", lambda e: e.matmul(pb[2][:, 1:2], ones_row_f, tone[0:1, 1:2],
                                                start=True, stop=True), "t0")
        P.wait("vector", "t0", tb2)
        P.opd("vector", lambda e: e.tensor_copy(out=swb, in_=pb[2][:, 0:2]), "v0")
        P.opd("vector", lambda e: e.reciprocal(out=wth[:, 0:2], in_=swb), "v0")
        P.opd("vector", lambda e: e.tensor_scalar(out=wth[:, 0:2], in0=wth[:, 0:2],
             scalar1=0.5, scalar2=None, op0=OP.mult), "v0")
        pw["sw_done"] = P.opd("vector", lambda e: e.tensor_scalar(out=wth[:, 2:4], in0=wth[:, 0:2],
             scalar1=-1.0, scalar2=None, op0=OP.mult), "v0")

    def pwb_step(idx):
        bi = idx % 2
        if idx < 2:
            pw["ldb"][idx] = P.dma("scalar", wtile[bi], wsrc(idx), f"wt{bi}")
        P.wait("vector", f"wt{bi}", pw["ldb"][idx])
        if idx < 2:
            P.wait("vector", "v0", pw["sw_done"])
        if idx >= 2:
            P.wait("vector", "w8", 16 * (idx - 1))
        col = 0 if idx < 128 else 1
        P.opd("vector", lambda e, b=bi, c=col: e.tensor_scalar(out=wp1, in0=wtile[b],
             scalar1=wth[:, c:c + 1], scalar2=None, op0=OP.is_gt), "v0")
        P.op("vector", lambda e, b=bi, c=col: e.tensor_scalar(out=wp2, in0=wtile[b],
             scalar1=wth[:, 2 + c:3 + c], scalar2=None, op0=OP.is_lt), "v0")
        tn = P.opd("vector", lambda e, b=bi: e.tensor_tensor(out=wtern[b], in0=wp1, in1=wp2,
                                                             op=OP.subtract), "v0")
        pw["tern"][idx] = tn
        P.wait("sync", "v0", tn)
        P.dma("sync", wdst_idx(idx, w1q8, w2q8), wtern[bi], "w8")
        if idx + 2 < NPW:
            P.wait("scalar", "v0", tn)
            pw["ldb"][idx + 2] = P.dma("scalar", wtile[(idx + 2) % 2], wsrc(idx + 2), f"wt{bi}")

    ln1xT_hi = alloc("p01", "ln1xT_hi", [128, NB * S], BF16)
    ln1xT_lo = alloc("p01", "ln1xT_lo", [128, NB * S], BF16)
    iot_i = alloc("pinit", "iot_i", [128, 512], I32)
    iot_f = alloc("pinit", "iot_f", [128, 512], F32)
    iot_p = alloc("pinit", "iot_p", [128, 1], I32)
    iot_pf = alloc("pinit", "iot_pf", [128, 1], F32)
    P.op("gpsimd", lambda e: e.memset(stats, 0.0), "g0")
    P.op("gpsimd", lambda e: e.iota(iot_i, pattern=[[1, 512]], base=0, channel_multiplier=0), "g0")
    P.op("gpsimd", lambda e: e.iota(iot_p, pattern=[[1, 1]], base=0, channel_multiplier=1), "g0")
    P.op("gpsimd", lambda e: e.memset(ones_col_bf, 1.0), "g0")
    P.op("gpsimd", lambda e: e.memset(ones_row_f, 1.0), "g0")
    P.op("gpsimd", lambda e: e.memset(negones_row_f, -1.0), "g0")
    P.wait("vector", "g0")
    P.op("vector", lambda e: e.tensor_copy(out=iot_f, in_=iot_i), "v0")
    P.opd("vector", lambda e: e.tensor_copy(out=iot_pf, in_=iot_p), "v0")
    P.op("vector", lambda e: e.tensor_scalar(out=eye_f, in0=iot_f[:, 0:128], scalar1=iot_pf,
                                             scalar2=None, op0=OP.is_equal), "v0")
    for j in range(4):
        P.opd("vector", lambda e, j=j: e.tensor_scalar(out=stats[:, 60:61], in0=iot_pf,
                                                       scalar1=float(128 * j), scalar2=None,
                                                       op0=OP.add), "v0")
        P.op("vector", lambda e, j=j: e.tensor_scalar(out=madd[:, j * 512:(j + 1) * 512],
                                                      in0=iot_f, scalar1=stats[:, 60:61],
                                                      scalar2=None, op0=OP.is_ge), "v0")
        P.op("vector", lambda e, j=j: e.tensor_scalar(out=madd[:, j * 512:(j + 1) * 512],
                                                      in0=madd[:, j * 512:(j + 1) * 512],
                                                      scalar1=-1.0, scalar2=1e30,
                                                      op0=OP.add, op1=OP.mult), "v0")
    P.dma("sync", gwf, gwT, "xl")
    P.dma("sync", gbt, gb, "xl")
    P.wait("scalar", "xl")
    hi0 = P.op("scalar", lambda e: e.activation(out=gw_hi, in_=gwf, func=AF.Copy), "s0")
    P.wait("vector", "s0", hi0)
    P.op("vector", lambda e: e.scalar_tensor_tensor(out=gw_lo, in0=gwf, scalar=1.0, in1=gw_hi,
                                                    op0=OP.mult, op1=OP.subtract), "v0")

    free_group("pinit")
    xrow = [alloc("p0", f"xrow{i}", [128, D], F32) for i in range(2)]
    lnx0 = alloc("p0", "lnx0", [128, D], F32)
    lnx = [lnx0, lnx0]
    lnh = [alloc("p0", f"lnh{i}", [128, D], BF16) for i in range(2)]
    lnl = [alloc("p0", f"lnl{i}", [128, D], BF16) for i in range(2)]

    tile_done_s = [0] * NT
    tile_tr = [0] * NT
    tile_lo = [0] * NT
    P.wait("sync", "v0", P.cnt["v0"])

    for t in range(NT):
        bi = t % 2
        if t >= 2:
            P.wait("sync", "s0", tile_done_s[t - 2])
        xl_done = P.dma("sync", xrow[bi], x_in[t * 128:(t + 1) * 128, :], f"xl{bi}")
        P.wait("vector", f"xl{bi}", xl_done)
        P.wait("scalar", f"xl{bi}", xl_done)
        if t >= 2:
            P.wait("scalar", f"tr{bi}", tile_tr[t - 2])
            P.wait("vector", f"tr{bi}", tile_tr[t - 2])
        if t >= 1:
            P.wait("scalar", "v0", tile_lo[t - 1])
        c = 8 + bi * 8
        P.opd("vector", lambda e, b=bi, c=c: e.reduce_sum(out=stats[:, c:c + 1], in_=xrow[b], axis=AX))
        sq_done = P.opd("scalar", lambda e, b=bi, c=c: e.activation(out=lnx[b], in_=xrow[b], func=AF.Square,
                                                                    accum_out=stats[:, c + 1:c + 2]), "s0")
        P.wait("vector", "s0", sq_done)
        P.opd("vector", lambda e, c=c: e.tensor_scalar(out=stats[:, c + 2:c + 3], in0=stats[:, c:c + 1],
                                                       scalar1=1.0 / D, scalar2=None, op0=OP.mult))
        P.opd("vector", lambda e, c=c: e.tensor_scalar(out=stats[:, c + 3:c + 4], in0=stats[:, c + 1:c + 2],
                                                       scalar1=1.0 / D, scalar2=None, op0=OP.mult))
        P.opd("vector", lambda e, c=c: e.tensor_tensor(out=stats[:, c + 4:c + 5], in0=stats[:, c + 2:c + 3],
                                                       in1=stats[:, c + 2:c + 3], op=OP.mult))
        P.opd("vector", lambda e, c=c: e.tensor_tensor(out=stats[:, c + 4:c + 5], in0=stats[:, c + 3:c + 4],
                                                       in1=stats[:, c + 4:c + 5], op=OP.subtract))
        P.opd("vector", lambda e, c=c: e.tensor_scalar(out=stats[:, c + 4:c + 5], in0=stats[:, c + 4:c + 5],
                                                       scalar1=EPS, scalar2=None, op0=OP.add))
        rc = P.opd("vector", lambda e, c=c: e.reciprocal(out=stats[:, c + 4:c + 5], in_=stats[:, c + 4:c + 5]), "v0")
        P.wait("scalar", "v0", rc)
        sq2 = P.opd("scalar", lambda e, c=c: e.activation(out=stats[:, c + 5:c + 6], in_=stats[:, c + 4:c + 5],
                                                          func=AF.Sqrt), "s0")
        P.wait("vector", "s0", sq2)
        P.opd("vector", lambda e, c=c: e.tensor_scalar(out=stats[:, c + 6:c + 7], in0=stats[:, c + 2:c + 3],
                                                       scalar1=stats[:, c + 5:c + 6], scalar2=None,
                                                       op0=OP.mult))
        nmr = P.opd("vector", lambda e, c=c: e.tensor_scalar(out=stats[:, c + 6:c + 7], in0=stats[:, c + 6:c + 7],
                                                             scalar1=-1.0, scalar2=None, op0=OP.mult), "v0")
        P.wait("scalar", "v0", nmr)
        P.op("scalar", lambda e, b=bi, c=c: e.activation(out=lnx[b], in_=xrow[b], func=AF.Identity,
                                                         scale=stats[:, c + 5:c + 6],
                                                         bias=stats[:, c + 6:c + 7]), "s0")
        hi_done = P.op("scalar", lambda e, b=bi: e.activation(out=lnh[b], in_=lnx[b], func=AF.Copy), "s0")
        tile_done_s[t] = hi_done
        P.wait("vector", "s0", hi_done)
        lo_done = P.op("vector", lambda e, b=bi: e.scalar_tensor_tensor(out=lnl[b], in0=lnx[b], scalar=1.0,
                                                                        in1=lnh[b], op0=OP.mult,
                                                                        op1=OP.subtract), "v0")
        tile_lo[t] = lo_done
        P.wait("sync", "s0", hi_done)
        P.wait("sync", "v0", lo_done)
        for src, dst in ((lnh[bi], ln1xT_hi), (lnl[bi], ln1xT_lo)):
            dst3 = dst.rearrange("p (b w) -> p b w", w=S)[:, :, t * 128:(t + 1) * 128]
            src3 = src.rearrange("p (b c) -> p b c", c=128)
            P.dma("sync", dst3, src3, f"tr{bi}", transpose=True)
        tile_tr[t] = P.cnt[f"tr{bi}"]
    tr_all = (P.cnt["tr0"], P.cnt["tr1"])

    free_group("p0")
    wcol_f = [alloc("p1", f"wcol_f{i}", [128, NB * 128], F32) for i in range(2)]
    wcol_h = [alloc("p1", f"wcol_h{i}", [128, NB * 128], BF16) for i in range(2)]
    wcol_l = [alloc("p1", f"wcol_l{i}", [128, NB * 128], BF16) for i in range(2)]
    qkv_h = alloc("p1", "qkv_h", [128, D], BF16)
    qkv_l = alloc("p1", "qkv_l", [128, D], BF16)

    P.wait("tensor", "tr0", tr_all[0])
    P.wait("tensor", "tr1", tr_all[1])
    P.wait("gpsimd", "tr0", tr_all[0])
    P.wait("gpsimd", "tr1", tr_all[1])
    wTs = {0: wqT, 1: wkT, 2: wvT}
    it_t1 = [0] * 48
    it_evv = [0] * 48
    it_evs = [0] * 48
    it_lo = [0] * 48
    for p in range(3):
        qsc = QSC if p == 0 else 1.0
        for eb in range(NB):
            it = p * NB + eb
            bi = it % 2
            if it >= 2:
                P.wait("gpsimd", "t1", it_t1[it - 2])
                P.wait("gpsimd", "v1", it_lo[it - 2])
            src = wTs[p].rearrange("(a q) e -> q a e", q=128)[:, :, eb * 128:(eb + 1) * 128]
            dstf = wcol_f[bi].rearrange("q (a c) -> q a c", c=128)
            dsth = wcol_h[bi].rearrange("q (a c) -> q a c", c=128)
            P.dma("gpsimd", dstf, src, f"wq{bi}")
            wq_done = P.dma("gpsimd", dsth, src, f"wq{bi}")
            P.wait("vector", f"wq{bi}", wq_done)
            if it >= 2:
                P.wait("vector", "t1", it_t1[it - 2])
            it_lo[it] = P.op("vector", lambda e, b=bi: e.scalar_tensor_tensor(
                out=wcol_l[b], in0=wcol_f[b], scalar=1.0, in1=wcol_h[b],
                op0=OP.mult, op1=OP.subtract), "v1")
            grp = it % 2
            banks = pb[grp * 4:grp * 4 + 4]
            P.wait("tensor", f"wq{bi}", wq_done)
            P.wait("tensor", "v1", it_lo[it])
            if it >= 2:
                P.wait("tensor", "v1s", it_evv[it - 2])
                P.wait("tensor", "s1s", it_evs[it - 2])
            n = 0
            for k in range(NB):
                wh = wcol_h[bi][:, k * 128:(k + 1) * 128]
                wl = wcol_l[bi][:, k * 128:(k + 1) * 128]
                for wop, xops in ((wh, (ln1xT_hi, ln1xT_lo)), (wl, (ln1xT_hi,))):
                    for xop in xops:
                        for j in range(4):
                            n += 1
                            first = n <= 4
                            last = n == 192
                            sem = "t1" if last else None
                            P.op("tensor", lambda e, bk=banks[j], w=wop, x=xop, k=k, j=j,
                                 st=first, sp=last or n > 188:
                                 e.matmul(bk, w, x[:, k * S + j * 512:k * S + (j + 1) * 512],
                                          start=st, stop=sp), sem)
            it_t1[it] = P.cnt["t1"]
            P.wait("scalar", "t1", it_t1[it])
            if it >= 1:
                P.wait("scalar", "spq", 32 * it)
            for j in range(4):
                P.op("scalar", lambda e, bk=banks[j], j=j, sc=qsc: e.activation(
                    out=qkv_h[:, j * 512:(j + 1) * 512], in_=bk, func=AF.Copy, scale=sc), "s1s")
            it_evs[it] = P.cnt["s1s"]
            P.wait("vector", "t1", it_t1[it])
            P.wait("vector", "s1s", it_evs[it])
            if it >= 1:
                P.wait("vector", "spq", 32 * it)
            for j in range(4):
                P.op("vector", lambda e, bk=banks[j], j=j, sc=qsc: e.scalar_tensor_tensor(
                    out=qkv_l[:, j * 512:(j + 1) * 512], in0=bk, scalar=sc,
                    in1=qkv_h[:, j * 512:(j + 1) * 512], op0=OP.mult, op1=OP.subtract), "v1s")
            it_evv[it] = P.cnt["v1s"]
            P.wait("sync", "s1s", it_evs[it])
            P.wait("sync", "v1s", it_evv[it])
            P.dma("sync", qkvs[p, 0, eb], qkv_h, "spq")
            P.dma("sync", qkvs[p, 1, eb], qkv_l, "spq")
            for ii in range((it * 256) // 48, ((it + 1) * 256) // 48):
                pwa_step(ii)
    spq_all = P.cnt["spq"]
    P.wait("tensor", "s1s", P.cnt["s1s"])
    P.wait("tensor", "v1s", P.cnt["v1s"])
    pwa_chain()

    if stage <= 1:
        P.wait("sync", "spq", spq_all)
        P.wait("sync", "w8", P.cnt["w8"])
        P.dma("sync", y_out[0:128, 0:64], stats[:, 0:64], "st")
        P.wait("sync", "st", P.cnt["st"])
        P.finish()
        return nc

    free_group("p1")
    free_group("p01")
    qrb = [[alloc("p2", f"q{pt}{i}", [128, S], BF16) for pt in (0, 1)] for i in range(2)]
    krb = [[alloc("p2", f"k{pt}{i}", [128, S], BF16) for pt in (0, 1)] for i in range(2)]
    vrb = [[alloc("p2", f"v{pt}{i}", [128, S], BF16) for pt in (0, 1)] for i in range(2)]
    vtok = [[alloc("p2", f"vt{pt}{i}", [128, S], BF16, is_transpose_dest=True) for pt in (0, 1)]
            for i in range(2)]
    S_sb = [alloc("p2", "S_sb0", [128, 12 * 512], F32), alloc("p2", "S_sb1", [128, 16 * 512], F32)]
    P32 = [alloc("p2", f"P32_{i}", [128, 512], F32) for i in range(2)]
    Phi = [alloc("p2", f"Phi{i}", [128, 512], BF16) for i in range(4)]
    Plo = [alloc("p2", f"Plo{i}", [128, 512], BF16) for i in range(4)]
    sgbuf = [alloc("p2", f"sg{i}", [128, S], F32) for i in range(2)]
    outst = [[alloc("p2", f"os{pt}{i}", [128, S], BF16) for pt in (0, 1)] for i in range(2)]
    t2sb = alloc("p2", "t2sb", [128, 512], F32)
    t3sb = alloc("p2", "t3sb", [128, 512], F32)
    macc = alloc("p2", "macc", [128, 16], F32)
    mpcol = alloc("p2", "mpcol", [128, 1], F32)
    cnrow = alloc("p2", "cnrow", [1, 1], F32)
    srow = alloc("p2", "srow", [1, 512], F32)
    rrow = alloc("p2", "rrow", [1, 512], F32)
    negc = [alloc("p2", f"negc{i}", [128, 1], F32) for i in range(2)]

    SC = pb[0:4]
    AVB = pb[4]
    GB = pb[5]
    SURB = [pb[6], pb[7]]

    P.wait("sync", "spq", spq_all)
    P.wait("tensor", "v0", pw["sw_done"])

    st2 = {"bank_ev": {}, "gb_read": 0, "av_read": 0, "sur_read": [0, 0],
           "head_t2_end": [0] * (H + 2), "spo_after": [0] * (H + 2),
           "head_v2_end": [0] * (H + 2), "sg_free": [0] * (H + 2)}

    def emit_head(h):
        hp = h % 2
        if h >= 2:
            P.wait("sync", "t2", st2["head_t2_end"][h - 1])
        for p, dstl in ((0, qrb), (1, krb), (2, vrb)):
            for part in (0, 1):
                P.dma("sync", dstl[hp][part], qkvs[p, part, h], f"rq{hp}")
        rq_done = P.cnt[f"rq{hp}"]
        P.wait("sync", f"rq{hp}", rq_done)
        for part in (0, 1):
            src3 = vrb[hp][part].rearrange("p (b c) -> p b c", c=128)
            dst3 = vtok[hp][part].rearrange("p (b c) -> p b c", c=128)
            P.dma("sync", dst3, src3, f"trv{hp}", transpose=True)
        trv_done = P.cnt[f"trv{hp}"]

        P.wait("tensor", f"rq{hp}", rq_done)
        P.wait("tensor", f"trv{hp}", trv_done)
        P.wait("tensor", "v2", st2["gb_read"])
        if h >= 2:
            P.wait("scalar", "v2", st2["sg_free"][h - 2])
        sg_done = [0] * 4
        for c in range(4):
            n = 0
            for wop, xop in ((gw_hi, qrb[hp][0]), (gw_hi, qrb[hp][1]), (gw_lo, qrb[hp][0])):
                n += 1
                P.op("tensor", lambda e, w=wop, x=xop, c=c, st=(n == 1), sp=(n == 3):
                     e.matmul(GB, w, x[:, c * 512:(c + 1) * 512], start=st, stop=sp),
                     "t2" if n == 3 else None)
            P.wait("scalar", "t2", P.cnt["t2"])
            sg_done[c] = P.op("scalar", lambda e, c=c, b=hp: e.activation(
                out=sgbuf[b][:, c * 512:(c + 1) * 512], in_=GB, func=AF.Sigmoid,
                scale=float(np.sqrt(128.0)), bias=gbt), "s2")
            if c < 3:
                P.wait("tensor", "s2", sg_done[c])

        split_v2 = {}
        blk_t2 = {}
        for c in range(4):
            nblk = 4 * c + 4
            cp = c % 2
            ssb = S_sb[cp]
            ev_v2 = [0] * nblk
            for b in range(nblk):
                bki = 2 * cp + b % 2
                bank = SC[bki]
                prev_ev = st2["bank_ev"].get(bki, 0)
                if prev_ev:
                    P.wait("tensor", "v2", prev_ev)
                n = 0
                for wop, xop in ((krb[hp][0], qrb[hp][0]), (krb[hp][0], qrb[hp][1]),
                                 (krb[hp][1], qrb[hp][0])):
                    n += 1
                    P.op("tensor", lambda e, bk=bank, w=wop, x=xop, b=b, c=c,
                         st=(n == 1), sp=(n == 3):
                         e.matmul(bk, w[:, b * 128:(b + 1) * 128],
                                  x[:, c * 512:(c + 1) * 512], start=st, stop=sp),
                         "t2" if n == 3 else None)
                t_sc = P.cnt["t2"]
                P.wait("vector", "t2", t_sc)
                if b >= 4 * c:
                    j = b - 4 * c
                    P.op("vector", lambda e, bk=bank, b=b, j=j, s=ssb: e.tensor_tensor(
                        out=s[:, b * 512:(b + 1) * 512], in0=bk, in1=madd[:, j * 512:(j + 1) * 512],
                        op=OP.add), "v2")
                    ev_v2[b] = P.op("vector", lambda e, b=b, s=ssb: e.reduce_max(
                        out=macc[:, b:b + 1], in_=s[:, b * 512:(b + 1) * 512], axis=AX), "v2")
                else:
                    ev_v2[b] = P.op("vector", lambda e, bk=bank, b=b, s=ssb: e.tensor_scalar(
                        out=s[:, b * 512:(b + 1) * 512], in0=bk, scalar1=1.0, scalar2=None,
                        op0=OP.mult, op1=OP.max, accum_out=macc[:, b:b + 1]), "v2")
                st2["bank_ev"][bki] = ev_v2[b]
            P.op("vector", lambda e: e.drain())
            mp_v2 = P.opd("vector", lambda e, n=nblk: e.reduce_max(out=mpcol, in_=macc[:, 0:n],
                                                                   axis=AX), "v2")
            P.wait("tensor", "v2", mp_v2)
            P.wait("tensor", "v2", st2["sur_read"][cp])
            P.op("tensor", lambda e, cp=cp: e.transpose(SURB[cp][0:1, 0:128], mpcol, eye_f), "t2")
            P.wait("vector", "t2", P.cnt["t2"])
            cn_v2 = P.opd("vector", lambda e, cp=cp: e.reduce_max(out=cnrow, in_=SURB[cp][0:1, 0:128],
                                                                  axis=AX, negate=True), "v2")
            P.wait("tensor", "v2", cn_v2)
            nb_t2 = P.op("tensor", lambda e, cp=cp: e.matmul(SURB[cp][:, 128:129], ones_row_f,
                                                             cnrow[0:1, 0:1], start=True, stop=True), "t2")
            P.wait("scalar", "t2", nb_t2)
            P.opd("scalar", lambda e, cp=cp: e.copy(out=negc[cp], in_=SURB[cp][:, 128:129]), "s2")

            av_n = sum_n = 0
            for b in range(nblk):
                rot = b % 4
                if b >= 2:
                    P.wait("scalar", "v2", split_v2[(c, b - 2)])
                ex = P.op("scalar", lambda e, b=b, cp=cp, s=ssb: e.activation(
                    out=P32[b % 2], in_=s[:, b * 512:(b + 1) * 512], func=AF.Exp,
                    bias=negc[cp], scale=1.0), "s2")
                P.wait("vector", "s2", ex)
                if b >= 4:
                    P.wait("vector", "t2", blk_t2[(c, b - 4)])
                P.op("vector", lambda e, b=b, rot=rot: e.tensor_copy(out=Phi[rot], in_=P32[b % 2]), "v2")
                split_v2[(c, b)] = P.op("vector", lambda e, b=b, rot=rot: e.scalar_tensor_tensor(
                    out=Plo[rot], in0=P32[b % 2], scalar=1.0, in1=Phi[rot],
                    op0=OP.mult, op1=OP.subtract), "v2")
                P.wait("tensor", "v2", split_v2[(c, b)])
                if b == 0:
                    P.wait("tensor", "v2", st2["av_read"])
                for k5, (wop, xop) in enumerate((
                        (vtok[hp][0], Phi[rot]), (vtok[hp][0], Plo[rot]), (vtok[hp][1], Phi[rot]),
                        (ones_col_bf, Phi[rot]), (ones_col_bf, Plo[rot]))):
                    if k5 < 3:
                        av_n += 1
                        st, sp, o = av_n == 1, av_n == 3 * nblk, AVB
                        w_ap = wop[:, b * 128:(b + 1) * 128]
                    else:
                        sum_n += 1
                        st, sp, o = sum_n == 1, sum_n == 2 * nblk, SURB[cp][0:1, 0:512]
                        w_ap = wop
                    P.op("tensor", lambda e, o=o, w=w_ap, x=xop, st=st, sp=sp:
                         e.matmul(o, w, x, start=st, stop=sp),
                         "t2" if k5 == 4 else None)
                blk_t2[(c, b)] = P.cnt["t2"]
            mm_end = P.cnt["t2"]

            P.wait("vector", "t2", mm_end)
            P.opd("vector", lambda e, cp=cp: e.tensor_copy(out=srow, in_=SURB[cp][0:1, 0:512]))
            rec_v2 = P.opd("vector", lambda e: e.reciprocal(out=rrow, in_=srow), "v2")
            st2["sur_read"][cp] = rec_v2
            P.wait("tensor", "v2", rec_v2)
            if c == 0:
                P.wait("tensor", "s2", sg_done[3])
            bc_t2 = P.op("tensor", lambda e: e.matmul(GB, ones_row_f, rrow[0:1, 0:512],
                                                      start=True, stop=True), "t2")
            P.wait("vector", "t2", bc_t2)
            if h >= 2 and c == 0:
                P.wait("vector", f"spo{hp}", st2["spo_after"][h - 2])
            P.op("vector", lambda e, c=c, b=hp: e.tensor_tensor(
                out=t2sb, in0=AVB, in1=sgbuf[b][:, c * 512:(c + 1) * 512], op=OP.mult), "v2")
            gb_read = P.op("vector", lambda e: e.tensor_tensor(out=t3sb, in0=t2sb, in1=GB,
                                                               op=OP.mult), "v2")
            st2["gb_read"] = gb_read
            st2["av_read"] = gb_read
            P.wait("scalar", "v2", gb_read)
            if h >= 2 and c == 0:
                P.wait("scalar", f"spo{hp}", st2["spo_after"][h - 2])
            sh = P.op("scalar", lambda e, c=c, b=hp: e.activation(
                out=outst[b][0][:, c * 512:(c + 1) * 512], in_=t3sb, func=AF.Copy), "s2")
            P.wait("vector", "s2", sh)
            P.op("vector", lambda e, c=c, b=hp: e.scalar_tensor_tensor(
                out=outst[b][1][:, c * 512:(c + 1) * 512], in0=t3sb, scalar=1.0,
                in1=outst[b][0][:, c * 512:(c + 1) * 512], op0=OP.mult, op1=OP.subtract), "v2")

        st2["head_t2_end"][h + 1] = P.cnt["t2"]
        st2["sg_free"][h] = P.cnt["v2"]
        st2["head_v2_end"][h] = P.cnt["v2"]
        P.wait("sync", "s2", P.cnt["s2"])
        P.wait("sync", "v2", P.cnt["v2"])
        for part in (0, 1):
            P.dma("sync", outs_h[h, part], outst[hp][part], f"spo{hp}")
        st2["spo_after"][h] = P.cnt[f"spo{hp}"]

    for h in range(heads if heads else H):
        emit_head(h)
    spo_all = (P.cnt.get("spo0", 0), P.cnt.get("spo1", 0))

    if stage <= 2:
        P.wait("sync", "spo0", spo_all[0])
        P.wait("sync", "spo1", spo_all[1])
        P.wait("sync", "w8", w8_all)
        P.dma("sync", y_out[0:128, 0:64], stats[:, 0:64], "st")
        P.wait("sync", "st", P.cnt["st"])
        P.finish()
        return nc

    free_group("p2")
    wocf = [alloc("p3", f"wocf{i}", [128, NB * 128], F32) for i in range(2)]
    woch = [alloc("p3", f"woch{i}", [128, NB * 128], BF16) for i in range(2)]
    wocl = [alloc("p3", f"wocl{i}", [128, NB * 128], BF16) for i in range(2)]
    ortile = alloc("p3", "ortile", [128, 32 * 512], BF16)
    attnT = alloc("p3", "attnT", [128, 16 * 512], F32)
    x1st = alloc("p3", "x1st", [128, 4 * D], F32)
    xrd = alloc("p3", "xrd", [128, 4 * D], F32)

    P.wait("sync", "spo0", spo_all[0])
    P.wait("sync", "spo1", spo_all[1])
    P.wait("gpsimd", "spo0", spo_all[0])
    P.wait("gpsimd", "spo1", spo_all[1])
    ACC = pb[0:4]
    TRB = pb[4:6]
    ev_s3 = {}
    tr_read_v3 = {}
    wo_lo_v3 = {}
    wo_t3 = {}
    def wo_load(it):
        bi = it % 2
        eb = it % NB
        if it >= 2:
            P.wait("gpsimd", "t3", wo_t3.get(it - 2, 0))
        src = woT.rearrange("(a q) e -> q a e", q=128)[:, :, eb * 128:(eb + 1) * 128]
        P.dma("gpsimd", wocf[bi].rearrange("q (a k) -> q a k", k=128), src, f"wo{bi}")
        wo_ld[it] = P.dma("gpsimd", woch[bi].rearrange("q (a k) -> q a k", k=128), src, f"wo{bi}")

    def wo_split(it):
        bi = it % 2
        P.wait("vector", f"wo{bi}", wo_ld[it])
        if it >= 2:
            P.wait("vector", "t3", wo_t3.get(it - 2, 0))
        wo_lo_v3[it] = P.op("vector", lambda e, b=bi: e.scalar_tensor_tensor(
            out=wocl[b], in0=wocf[b], scalar=1.0, in1=woch[b],
            op0=OP.mult, op1=OP.subtract), "v3")

    wo_ld = {}
    wo_load(0)
    wo_load(1)
    wo_split(0)
    for c in range(4):
        if c >= 1:
            P.wait("sync", "t3", P.cnt["t3"])
        for h2 in range(H):
            for part in (0, 1):
                P.dma("sync", ortile[:, (2 * h2 + part) * 512:(2 * h2 + part + 1) * 512],
                      outs_h[h2, part][:, c * 512:(c + 1) * 512], "ro")
        for ts in range(4):
            P.dma("sync", xrd[:, ts * D:(ts + 1) * D],
                  x_in[c * 512 + ts * 128:c * 512 + (ts + 1) * 128, :], "rx")
        ro_done, rx_done = P.cnt["ro"], P.cnt["rx"]
        if c >= 1:
            P.wait("vector", "spx", P.cnt["spx"])
        for eb in range(NB):
            it = c * NB + eb
            bi = it % 2
            bank = ACC[eb % 4]
            P.wait("tensor", f"wo{bi}", wo_ld[it])
            P.wait("tensor", "v3", wo_lo_v3[it])
            P.wait("tensor", "ro", ro_done)
            if it >= 4:
                P.wait("tensor", "s3", ev_s3[it - 4])
            n = 0
            for k in range(NB):
                for wop, part in ((woch[bi], 0), (woch[bi], 1), (wocl[bi], 0)):
                    n += 1
                    P.op("tensor", lambda e, bk=bank, w=wop, k=k, pt=part,
                         st=(n == 1), sp=(n == 48):
                         e.matmul(bk, w[:, k * 128:(k + 1) * 128],
                                  ortile[:, (2 * k + pt) * 512:(2 * k + pt + 1) * 512],
                                  start=st, stop=sp), "t3" if n == 48 else None)
            wo_t3[it] = P.cnt["t3"]
            if it + 2 < 64:
                wo_load(it + 2)
            if it + 1 < 64:
                wo_split(it + 1)
            P.wait("scalar", "t3", wo_t3[it])
            ev_s3[it] = P.op("scalar", lambda e, bk=bank, eb=eb: e.activation(
                out=attnT[:, eb * 512:(eb + 1) * 512], in_=bk, func=AF.Copy), "s3")
            P.wait("tensor", "s3", ev_s3[it])
            P.wait("vector", "rx", rx_done)
            for ts in range(4):
                trb = TRB[(eb * 4 + ts) % 2]
                prev = tr_read_v3.get((eb * 4 + ts) % 2, 0)
                if prev:
                    P.wait("tensor", "v3", prev)
                t_tr = P.op("tensor", lambda e, t=trb, eb=eb, ts=ts: e.transpose(
                    t[:, 0:128], attnT[:, eb * 512 + ts * 128:eb * 512 + (ts + 1) * 128], eye_f), "t3")
                P.wait("vector", "t3", t_tr)
                tr_read_v3[(eb * 4 + ts) % 2] = P.op("vector", lambda e, t=trb, eb=eb, ts=ts:
                    e.tensor_tensor(out=x1st[:, ts * D + eb * 128:ts * D + (eb + 1) * 128],
                                    in0=t[:, 0:128], in1=xrd[:, ts * D + eb * 128:ts * D + (eb + 1) * 128],
                                    op=OP.add), "v3")
            for ii in range((it * 256) // 64, ((it + 1) * 256) // 64):
                pwb_step(ii)
        P.wait("sync", "v3", P.cnt["v3"])
        for ts in range(4):
            P.dma("sync", x1s[c * 512 + ts * 128:c * 512 + (ts + 1) * 128, :],
                  x1st[:, ts * D:(ts + 1) * D], "spx")
    spx_all = P.cnt["spx"]

    w8_all = P.cnt["w8"]
    free_group("p3")
    free_group("pw")
    h_sb = [alloc("p4", f"h{i}", [128, INNER], F32) for i in range(2)]
    x1rd = [[alloc("p4", f"x1rd{p}{i}", [128, D], F32) for i in range(2)] for p in range(2)]
    xq1tok = [alloc("p4", f"xq1tok{i}", [128, D], BF16) for i in range(2)]
    xq1T = alloc("p4", "xq1T", [128, 16 * 256], BF16, is_transpose_dest=True)
    xq2c = [alloc("p4", f"xq2c{i}", [128, 512], BF16) for i in range(4)]
    xq2T = alloc("p4", "xq2T", [128, 64 * 256], BF16, is_transpose_dest=True)
    w1tile = [alloc("p4", f"w1tile{i}", [128, 16 * 256], FP8) for i in range(2)]
    w2tile = [alloc("p4", f"w2tile{i}", [128, 4 * 1024], FP8) for i in range(2)]
    b1ct = [alloc("p4", f"b1ct{i}", [128, 256], F32) for i in range(2)]
    gct = [alloc("p4", f"gct{i}", [128, 512], F32) for i in range(2)]
    bct = [alloc("p4", f"bct{i}", [128, 512], F32) for i in range(2)]
    b2t = alloc("p4", "b2t", [128, D], F32)
    i32a = [alloc("p4", f"i32a{i}", [128, 512], I32) for i in range(2)]
    i32b = [alloc("p4", f"i32b{i}", [128, 512], I32) for i in range(2)]
    sttmp = [alloc("p4", f"sttmp{i}", [128, 512], F32) for i in range(2)]
    jnk = alloc("p4", "jnk", [128, 512], F32)
    macc4 = alloc("p4", "macc4", [128, 128], F32)

    P.wait("gpsimd", "spx", spx_all)
    P.wait("gpsimd", "w8", w8_all)
    P.wait("sync", "spx", spx_all)
    bb_b2 = P.dma("sync", b2t, b2c, "bbz")

    def SCL(cc, tb, i):
        p = (cc % 2) * 2 + tb
        return stats[:, 24 * p + i:24 * p + i + 1]

    CS = {"v4_end": [0] * 10, "st_end": [0] * 10, "mm1_t4": [0] * 10, "mm2_t4": [0] * 10,
          "q1_v4": [[0, 0] for _ in range(10)], "tq1": [0] * 10, "q2_ready": [0] * 10,
          "ev2_v4": [0] * 10, "ev2_03": [0] * 10, "q2p": [None] * 10}
    tiny = P.opd

    def drain_inc(eng, sem):
        sm = P.sem(sem)
        P.cnt[sem] += 1
        v = P.cnt[sem]
        P.cmds[eng].append(lambda e, s=sm, vv=v: e.drain().then_inc(s, 1))
        return v

    def emit_A(cc):
        cp = cc % 2
        if cc >= 2:
            P.wait("sync", f"st{cp}", CS["st_end"][cc - 2])
        for tb in range(2):
            P.dma("sync", x1rd[cp][tb], x1s[(2 * cc + tb) * 128:(2 * cc + tb + 1) * 128, :],
                  f"r1{cp}")
        r1_done = P.cnt[f"r1{cp}"]
        P.wait("vector", f"r1{cp}", r1_done)
        P.wait("scalar", f"r1{cp}", r1_done)
        if cc >= 2:
            P.wait("vector", "v4", CS["ev2_v4"][cc - 2])
            P.wait("scalar", "v4", CS["ev2_v4"][cc - 2])
        if cc >= 1:
            P.wait("scalar", "tq10", CS["tq1"][cc - 1][0])
            P.wait("scalar", "tq11", CS["tq1"][cc - 1][1])
        sq_s4 = [0, 0]
        for tb in range(2):
            tiny("scalar", lambda e, tb=tb: e.activation(
                out=xq1tok[tb], in_=x1rd[cp][tb], func=AF.Identity, accum_out=SCL(cc, tb, 0)))
            sq_s4[tb] = tiny("scalar", lambda e, tb=tb: e.activation(
                out=xq1tok[tb], in_=x1rd[cp][tb], func=AF.Square, accum_out=SCL(cc, tb, 1)), "s4")
            tiny("vector", lambda e, tb=tb: e.reduce_max(out=SCL(cc, tb, 2), in_=x1rd[cp][tb], axis=AX))
            tiny("vector", lambda e, tb=tb: e.tensor_reduce(out=SCL(cc, tb, 3), in_=x1rd[cp][tb],
                                                            axis=AX, op=OP.min))
        sqr_v4 = [0, 0]
        for tb in range(2):
            P.wait("vector", "s4", sq_s4[tb])
            tiny("vector", lambda e, tb=tb: e.tensor_scalar(out=SCL(cc, tb, 4), in0=SCL(cc, tb, 0),
                 scalar1=1.0 / D, scalar2=None, op0=OP.mult))
            tiny("vector", lambda e, tb=tb: e.tensor_scalar(out=SCL(cc, tb, 5), in0=SCL(cc, tb, 1),
                 scalar1=1.0 / D, scalar2=None, op0=OP.mult))
            tiny("vector", lambda e, tb=tb: e.tensor_tensor(out=SCL(cc, tb, 6), in0=SCL(cc, tb, 4),
                 in1=SCL(cc, tb, 4), op=OP.mult))
            tiny("vector", lambda e, tb=tb: e.tensor_tensor(out=SCL(cc, tb, 5), in0=SCL(cc, tb, 5),
                 in1=SCL(cc, tb, 6), op=OP.subtract))
            tiny("vector", lambda e, tb=tb: e.tensor_scalar(out=SCL(cc, tb, 6), in0=SCL(cc, tb, 5),
                 scalar1=EPS, scalar2=None, op0=OP.add))
            tiny("vector", lambda e, tb=tb: e.reciprocal(out=SCL(cc, tb, 6), in_=SCL(cc, tb, 6)))
            tiny("vector", lambda e, tb=tb: e.tensor_tensor(out=SCL(cc, tb, 8), in0=SCL(cc, tb, 5),
                 in1=SCL(cc, tb, 6), op=OP.mult))
            tiny("vector", lambda e, tb=tb: e.tensor_scalar(out=SCL(cc, tb, 8), in0=SCL(cc, tb, 8),
                 scalar1=EPS, scalar2=None, op0=OP.add))
            sqr_v4[tb] = tiny("vector", lambda e, tb=tb: e.reciprocal(out=SCL(cc, tb, 8),
                              in_=SCL(cc, tb, 8)), "v4")
        r2_s4 = [0, 0]
        for tb in range(2):
            P.wait("scalar", "v4", sqr_v4[tb])
            tiny("scalar", lambda e, tb=tb: e.activation(out=SCL(cc, tb, 7), in_=SCL(cc, tb, 6),
                                                         func=AF.Sqrt))
            r2_s4[tb] = tiny("scalar", lambda e, tb=tb: e.activation(out=SCL(cc, tb, 9),
                             in_=SCL(cc, tb, 8), func=AF.Sqrt), "s4")
        qs_v4 = [0, 0]
        for tb in range(2):
            P.wait("vector", "s4", r2_s4[tb])
            tiny("vector", lambda e, tb=tb: e.tensor_tensor(out=SCL(cc, tb, 10), in0=SCL(cc, tb, 2),
                 in1=SCL(cc, tb, 4), op=OP.subtract))
            tiny("vector", lambda e, tb=tb: e.tensor_tensor(out=SCL(cc, tb, 11), in0=SCL(cc, tb, 4),
                 in1=SCL(cc, tb, 3), op=OP.subtract))
            tiny("vector", lambda e, tb=tb: e.tensor_tensor(out=SCL(cc, tb, 10), in0=SCL(cc, tb, 10),
                 in1=SCL(cc, tb, 11), op=OP.max))
            tiny("vector", lambda e, tb=tb: e.tensor_tensor(out=SCL(cc, tb, 11), in0=SCL(cc, tb, 7),
                 in1=SCL(cc, tb, 9), op=OP.mult))
            tiny("vector", lambda e, tb=tb: e.tensor_tensor(out=SCL(cc, tb, 10), in0=SCL(cc, tb, 10),
                 in1=SCL(cc, tb, 11), op=OP.mult))
            tiny("vector", lambda e, tb=tb: e.tensor_scalar(out=SCL(cc, tb, 10), in0=SCL(cc, tb, 10),
                 scalar1=1e-5, scalar2=None, op0=OP.max))
            tiny("vector", lambda e, tb=tb: e.reciprocal(out=SCL(cc, tb, 10), in_=SCL(cc, tb, 10)))
            tiny("vector", lambda e, tb=tb: e.tensor_scalar(out=SCL(cc, tb, 12), in0=SCL(cc, tb, 10),
                 scalar1=127.0, scalar2=None, op0=OP.mult))
            tiny("vector", lambda e, tb=tb: e.tensor_tensor(out=SCL(cc, tb, 13), in0=SCL(cc, tb, 11),
                 in1=SCL(cc, tb, 12), op=OP.mult))
            tiny("vector", lambda e, tb=tb: e.tensor_tensor(out=SCL(cc, tb, 14), in0=SCL(cc, tb, 4),
                 in1=SCL(cc, tb, 13), op=OP.mult))
            tiny("vector", lambda e, tb=tb: e.tensor_scalar(out=SCL(cc, tb, 14), in0=SCL(cc, tb, 14),
                 scalar1=-1.0, scalar2=None, op0=OP.mult))
            tiny("vector", lambda e, tb=tb: e.tensor_tensor(out=SCL(cc, tb, 15), in0=SCL(cc, tb, 12),
                 in1=swb[:, 0:1], op=OP.mult))
            qs_v4[tb] = tiny("vector", lambda e, tb=tb: e.reciprocal(out=SCL(cc, tb, 15),
                             in_=SCL(cc, tb, 15)), "v4")
        uq = {}
        for tb in range(2):
            P.wait("scalar", "v4", qs_v4[tb])
            for pc in range(4):
                pp = pc % 2
                prev = uq.get(pp, 0)
                if prev:
                    P.wait("scalar", "v4", prev)
                u = P.op("scalar", lambda e, tb=tb, pc=pc, pp=pp: e.activation(
                    out=i32a[pp], in_=x1rd[cp][tb][:, pc * 512:(pc + 1) * 512], func=AF.Identity,
                    scale=SCL(cc, tb, 13), bias=SCL(cc, tb, 14)), "s4")
                P.wait("vector", "s4", u)
                uq[pp] = P.op("vector", lambda e, tb=tb, pc=pc, pp=pp: e.tensor_copy(
                    out=xq1tok[tb][:, pc * 512:(pc + 1) * 512], in_=i32a[pp]), "v4")
        CS["q1_v4"][cc] = [uq.get(0, 0), uq.get(1, 0), P.cnt["v4"]]

    def emit_tq1(cc):
        P.wait("sync", "v4", CS["q1_v4"][cc][2])
        if cc >= 1:
            P.wait("sync", "t4", CS["mm1_t4"][cc - 1])
        for tb in range(2):
            dst = xq1T.rearrange("p (b w) -> p b w", w=256)[:, :, tb * 128:(tb + 1) * 128]
            src = xq1tok[tb].rearrange("p (b c) -> p b c", c=128)
            P.dma("sync", dst, src, f"tq1{tb}", transpose=True)
        CS["tq1"][cc] = (P.cnt["tq10"], P.cnt["tq11"])

    def emit_B1(cc):
        cp = cc % 2
        w1_t4 = {}
        ev1_v4 = {}
        ev1p = {}
        for ic in range(32):
            icp = ic % 2
            prev = w1_t4.get(icp, CS["mm2_t4"][cc - 1] if cc >= 1 else 0)
            if prev:
                P.wait("gpsimd", "t4", prev)
            wsrc3 = w1q8.rearrange("(a q) i -> q a i", q=128)[:, :, ic * 256:(ic + 1) * 256]
            wdst3 = w1tile[icp].rearrange("q (a w) -> q a w", w=256)
            wf_done = P.dma("gpsimd", wdst3, wsrc3, f"wf{icp}")
            pe_ev = ev1_v4.get((1, icp), 0)
            if pe_ev:
                P.wait("sync", "v4", pe_ev)
            bb_done = P.dma("sync", b1ct[icp], b1c[:, ic * 256:(ic + 1) * 256], f"bb{icp}")
            P.wait("tensor", f"wf{icp}", wf_done)
            if ic == 0:
                P.wait("tensor", "tq10", CS["tq1"][cc][0])
                P.wait("tensor", "tq11", CS["tq1"][cc][1])
                if cc >= 1:
                    P.wait("tensor", "v4", CS["ev2_03"][cc - 1])
            for tb in range(2):
                bank = pb[tb * 2 + icp][:, 0:256]
                pe = ev1_v4.get((tb, icp), 0)
                if pe:
                    P.wait("tensor", "v4", pe)
                for k in range(NB):
                    P.op("tensor", lambda e, bk=bank, k=k, tb=tb, ic=ic,
                         st=(k == 0), sp=(k == NB - 1):
                         e.matmul(bk, xq1T[:, k * 256 + tb * 128:k * 256 + (tb + 1) * 128],
                                  w1tile[ic % 2][:, k * 256:(k + 1) * 256], start=st, stop=sp),
                         "t4" if k == NB - 1 else None)
            w1_t4[icp] = P.cnt["t4"]
            t4n = P.cnt["t4"]
            P.wait("vector", "t4", t4n)
            P.wait("vector", f"bb{icp}", bb_done)
            for tb in range(2):
                bank = pb[tb * 2 + icp][:, 0:256]
                ev1_v4[(tb, icp)] = P.op("vector", lambda e, bk=bank, tb=tb, ic=ic, icp=icp:
                    e.scalar_tensor_tensor(out=h_sb[tb][:, ic * 256:(ic + 1) * 256], in0=bk,
                                           scalar=SCL(cc, tb, 15), in1=b1ct[icp],
                                           op0=OP.mult, op1=OP.add), "v4")
                ev1p[(tb, ic)] = ev1_v4[(tb, icp)]
        CS["mm1_t4"][cc] = P.cnt["t4"]
        CS["v4_end"][cc] = P.cnt["v4"]
        sw = 0
        for jb in range(0, 16, 4):
            for j in range(jb, jb + 4):
                for tb in range(2):
                    g = ev1p[(tb, 2 * j + 1)]
                    if g > sw:
                        P.wait("scalar", "v4", g)
                        sw = g
                    P.op("scalar", lambda e, tb=tb, j=j: e.activation(
                        out=h_sb[tb][:, j * 512:(j + 1) * 512],
                        in_=h_sb[tb][:, j * 512:(j + 1) * 512], func=AF.Silu), None)
            for j in range(jb, jb + 4):
                for tb in range(2):
                    P.op("scalar", lambda e, tb=tb, j=j: e.activation(
                        out=h_sb[tb][:, j * 512:(j + 1) * 512],
                        in_=h_sb[tb][:, j * 512:(j + 1) * 512], func=AF.Gelu,
                        accum_out=macc4[:, 16 * tb + j:16 * tb + j + 1]), None)
            for j in range(jb, jb + 4):
                for tb in range(2):
                    P.op("scalar", lambda e, tb=tb, j=j: e.activation(
                        out=jnk, in_=h_sb[tb][:, j * 512:(j + 1) * 512], func=AF.Square,
                        accum_out=macc4[:, 32 + 16 * tb + j:32 + 16 * tb + j + 1]), None)
        lnsq_s4 = drain_inc("scalar", "s4")
        rh_v4 = [0, 0]
        for tb in range(2):
            P.wait("vector", "s4", lnsq_s4)
            tiny("vector", lambda e, tb=tb: e.reduce_sum(out=SCL(cc, tb, 16),
                 in_=macc4[:, 16 * tb:16 * tb + 16], axis=AX))
            tiny("vector", lambda e, tb=tb: e.reduce_sum(out=SCL(cc, tb, 17),
                 in_=macc4[:, 32 + 16 * tb:32 + 16 * tb + 16], axis=AX))
            tiny("vector", lambda e, tb=tb: e.tensor_scalar(out=SCL(cc, tb, 16), in0=SCL(cc, tb, 16),
                 scalar1=1.0 / INNER, scalar2=None, op0=OP.mult))
            tiny("vector", lambda e, tb=tb: e.tensor_scalar(out=SCL(cc, tb, 17), in0=SCL(cc, tb, 17),
                 scalar1=1.0 / INNER, scalar2=None, op0=OP.mult))
            tiny("vector", lambda e, tb=tb: e.tensor_tensor(out=SCL(cc, tb, 20), in0=SCL(cc, tb, 16),
                 in1=SCL(cc, tb, 16), op=OP.mult))
            tiny("vector", lambda e, tb=tb: e.tensor_tensor(out=SCL(cc, tb, 17), in0=SCL(cc, tb, 17),
                 in1=SCL(cc, tb, 20), op=OP.subtract))
            tiny("vector", lambda e, tb=tb: e.tensor_scalar(out=SCL(cc, tb, 17), in0=SCL(cc, tb, 17),
                 scalar1=EPS, scalar2=None, op0=OP.add))
            rh_v4[tb] = tiny("vector", lambda e, tb=tb: e.reciprocal(out=SCL(cc, tb, 17),
                             in_=SCL(cc, tb, 17)), "v4")
        nmh_v4 = [0, 0]
        for tb in range(2):
            P.wait("scalar", "v4", rh_v4[tb])
            s1 = tiny("scalar", lambda e, tb=tb: e.activation(out=SCL(cc, tb, 18),
                      in_=SCL(cc, tb, 17), func=AF.Sqrt), "s4")
            P.wait("vector", "s4", s1)
            tiny("vector", lambda e, tb=tb: e.tensor_tensor(out=SCL(cc, tb, 19), in0=SCL(cc, tb, 16),
                 in1=SCL(cc, tb, 18), op=OP.mult))
            nmh_v4[tb] = tiny("vector", lambda e, tb=tb: e.tensor_scalar(out=SCL(cc, tb, 19),
                 in0=SCL(cc, tb, 19), scalar1=-1.0, scalar2=None, op0=OP.mult), "v4")
        bgr = {}
        for ic in range(16):
            icp = ic % 2
            pg = bgr.get(icp, 0)
            if pg:
                P.wait("gpsimd", "v4", pg)
            P.dma("gpsimd", gct[icp], lngc[:, ic * 512:(ic + 1) * 512], f"bg{icp}")
            gb_done = P.dma("gpsimd", bct[icp], lnbc[:, ic * 512:(ic + 1) * 512], f"bg{icp}")
            for tb in range(2):
                if ic == 0:
                    P.wait("scalar", "v4", nmh_v4[tb])
                zs = P.op("scalar", lambda e, tb=tb, ic=ic: e.activation(
                    out=sttmp[ic % 2], in_=h_sb[tb][:, ic * 512:(ic + 1) * 512], func=AF.Identity,
                    scale=SCL(cc, tb, 18), bias=SCL(cc, tb, 19)), "s4")
                P.wait("vector", "s4", zs)
                P.wait("vector", f"bg{icp}", gb_done)
                P.op("vector", lambda e, ic=ic, icp=icp: e.tensor_tensor(
                    out=sttmp[ic % 2], in0=sttmp[ic % 2], in1=gct[icp], op=OP.mult), "v4")
                yv = P.op("vector", lambda e, tb=tb, ic=ic, icp=icp: e.tensor_tensor(
                    out=h_sb[tb][:, ic * 512:(ic + 1) * 512], in0=sttmp[ic % 2], in1=bct[icp],
                    op=OP.add), "v4")
                P.wait("scalar", "v4", yv)
                P.op("vector", lambda e, tb=tb, ic=ic: e.tensor_tensor(
                    out=jnk, in0=h_sb[tb][:, ic * 512:(ic + 1) * 512],
                    in1=h_sb[tb][:, ic * 512:(ic + 1) * 512], op=OP.mult), None)
                P.op("vector", lambda e, tb=tb, ic=ic: e.reduce_sum(
                    out=macc4[:, 96 + 16 * tb + ic:96 + 16 * tb + ic + 1], in_=jnk, axis=AX), None)
                P.op("vector", lambda e, tb=tb, ic=ic: e.reduce_max(
                    out=macc4[:, 64 + 16 * tb + ic:64 + 16 * tb + ic + 1],
                    in_=h_sb[tb][:, ic * 512:(ic + 1) * 512], axis=AX,
                    apply_absolute_value=True), None)
            bgr[icp] = yv
        drain_inc("vector", "v4")
        q2_v4 = [0, 0]
        for tb in range(2):
            tiny("vector", lambda e, tb=tb: e.reduce_sum(out=SCL(cc, tb, 20),
                 in_=macc4[:, 96 + 16 * tb:96 + 16 * tb + 16], axis=AX))
            tiny("vector", lambda e, tb=tb: e.tensor_scalar(out=SCL(cc, tb, 20), in0=SCL(cc, tb, 20),
                 scalar1=1.0 / INNER, scalar2=EPS, op0=OP.mult, op1=OP.add))
            q2_v4[tb] = tiny("vector", lambda e, tb=tb: e.reciprocal(out=SCL(cc, tb, 20),
                             in_=SCL(cc, tb, 20)), "v4")
        r2b_s4 = [0, 0]
        for tb in range(2):
            P.wait("scalar", "v4", q2_v4[tb])
            r2b_s4[tb] = tiny("scalar", lambda e, tb=tb: e.activation(out=SCL(cc, tb, 21),
                              in_=SCL(cc, tb, 20), func=AF.Sqrt), "s4")
        c2_v4 = [0, 0]
        for tb in range(2):
            P.wait("vector", "s4", r2b_s4[tb])
            tiny("vector", lambda e, tb=tb: e.reduce_max(out=SCL(cc, tb, 22),
                 in_=macc4[:, 64 + 16 * tb:64 + 16 * tb + 16], axis=AX))
            tiny("vector", lambda e, tb=tb: e.tensor_tensor(out=SCL(cc, tb, 22), in0=SCL(cc, tb, 22),
                 in1=SCL(cc, tb, 21), op=OP.mult))
            tiny("vector", lambda e, tb=tb: e.tensor_scalar(out=SCL(cc, tb, 22), in0=SCL(cc, tb, 22),
                 scalar1=1e-5, scalar2=None, op0=OP.max))
            tiny("vector", lambda e, tb=tb: e.reciprocal(out=SCL(cc, tb, 22), in_=SCL(cc, tb, 22)))
            tiny("vector", lambda e, tb=tb: e.tensor_scalar(out=SCL(cc, tb, 22), in0=SCL(cc, tb, 22),
                 scalar1=127.0, scalar2=None, op0=OP.mult))
            tiny("vector", lambda e, tb=tb: e.tensor_tensor(out=SCL(cc, tb, 23), in0=SCL(cc, tb, 22),
                 in1=swb[:, 1:2], op=OP.mult))
            tiny("vector", lambda e, tb=tb: e.reciprocal(out=SCL(cc, tb, 23), in_=SCL(cc, tb, 23)))
            c2_v4[tb] = tiny("vector", lambda e, tb=tb: e.tensor_tensor(out=SCL(cc, tb, 22),
                 in0=SCL(cc, tb, 22), in1=SCL(cc, tb, 21), op=OP.mult), "v4")
        tq2_use = ({r: CS["q2_ready"][cc - 1][f"tq2_{r}"] for r in range(4)}
                   if cc >= 1 else {})
        uf2 = {}
        q2p = {}
        for ic in range(16):
            for tb in range(2):
                rr = (2 * ic + tb) % 2
                rot4 = (2 * ic + tb) % 4
                if ic == 0 and tb == 0:
                    for t2b in range(2):
                        P.wait("scalar", "v4", c2_v4[t2b])
                pu = uf2.get(rr, 0)
                if pu:
                    P.wait("scalar", "v4", pu)
                u2 = P.op("scalar", lambda e, tb=tb, ic=ic, rr=rr: e.activation(
                    out=i32b[rr], in_=h_sb[tb][:, ic * 512:(ic + 1) * 512], func=AF.Identity,
                    scale=SCL(cc, tb, 22), bias=0.0), "s4")
                P.wait("vector", "s4", u2)
                pt = tq2_use.get(rot4, 0)
                if pt:
                    P.wait("vector", f"tq2_{rot4}", pt)
                vq = P.op("vector", lambda e, rr=rr, rot4=rot4: e.tensor_copy(
                    out=xq2c[rot4], in_=i32b[rr]), "v4")
                uf2[rr] = vq
                P.wait("sync", "v4", vq)
                if cc >= 1 and ic == 0 and tb == 0:
                    P.wait("sync", "t4", CS["mm2_t4"][cc - 1])
                dst = xq2T.rearrange("p (b w) -> p b w", w=256)[:, 4 * ic:4 * ic + 4,
                                                               tb * 128:(tb + 1) * 128]
                srcq = xq2c[rot4].rearrange("p (b c) -> p b c", c=128)
                tq2_use[rot4] = P.dma("sync", dst, srcq, f"tq2_{rot4}", transpose=True)
                q2p.setdefault(ic, []).append((f"tq2_{rot4}", tq2_use[rot4]))
        CS["q2_ready"][cc] = {f"tq2_{r}": P.cnt.get(f"tq2_{r}", 0) for r in range(4)}
        CS["q2p"][cc] = q2p

    def emit_B2(cc):
        cp = cc % 2
        w2_t4 = ({0: CS["mm2_t4"][cc - 1], 1: CS["mm2_t4"][cc - 1]} if cc >= 1 else {})
        step = 0
        for ic in range(16):
            for oh in range(2):
                icp = step % 2
                prev = w2_t4.get(icp, 0)
                if prev:
                    P.wait("gpsimd", "t4", prev)
                w2src = w2q8.rearrange("(a q) d -> q a d", q=128)[:, 4 * ic:4 * ic + 4,
                                                                 oh * 1024:(oh + 1) * 1024]
                w2dst = w2tile[icp].rearrange("q (a d) -> q a d", d=1024)
                wf2 = P.dma("gpsimd", w2dst, w2src, f"w2{icp}")
                P.wait("tensor", f"w2{icp}", wf2)
                if oh == 0:
                    for nm, cnt in CS["q2p"][cc][ic]:
                        P.wait("tensor", nm, cnt)
                if ic == 0 and oh == 0:
                    P.wait("tensor", "v4", CS["v4_end"][cc])
                    if cc >= 1:
                        P.wait("tensor", "v4", CS["ev2_v4"][cc - 1])
                for k in range(4):
                    for tb in range(2):
                        for dc in range(2):
                            st4 = (ic == 0 and k == 0)
                            sp4 = (ic == 15 and k == 3)
                            last = (k == 3 and tb == 1 and dc == 1)
                            P.op("tensor", lambda e, bk=pb[tb * 4 + oh * 2 + dc], ic=ic, k=k,
                                 tb=tb, dc=dc, st=st4, sp=sp4, tile=w2tile[icp]:
                                 e.matmul(bk, xq2T[:, (4 * ic + k) * 256 + tb * 128:
                                                   (4 * ic + k) * 256 + (tb + 1) * 128],
                                          tile[:, k * 1024 + dc * 512:
                                               k * 1024 + (dc + 1) * 512],
                                          start=st, stop=sp),
                                 "t4" if last else None)
                w2_t4[icp] = P.cnt["t4"]
                step += 1
        mm_done = P.cnt["t4"]
        P.wait("vector", "t4", mm_done)
        if cc == 0:
            P.wait("vector", "bbz", bb_b2)
        for tb in range(2):
            for oh in range(2):
                for dc in range(2):
                    col = oh * 1024 + dc * 512
                    P.op("vector", lambda e, bk=pb[tb * 4 + oh * 2 + dc], tb=tb, col=col:
                         e.scalar_tensor_tensor(out=sttmp[tb], in0=bk, scalar=SCL(cc, tb, 23),
                                                in1=b2t[:, col:col + 512],
                                                op0=OP.mult, op1=OP.add), "v4")
                    P.op("vector", lambda e, tb=tb, col=col: e.tensor_tensor(
                        out=x1rd[cp][tb][:, col:col + 512], in0=sttmp[tb],
                        in1=x1rd[cp][tb][:, col:col + 512], op=OP.add), "v4")
            if tb == 0:
                CS["ev2_03"][cc] = P.cnt["v4"]
        CS["mm2_t4"][cc] = P.cnt["t4"]
        CS["ev2_v4"][cc] = P.cnt["v4"]
        P.wait("sync", "v4", P.cnt["v4"])
        for tb in range(2):
            P.dma("sync", y_out[(2 * cc + tb) * 128:(2 * cc + tb + 1) * 128, :],
                  x1rd[cp][tb], f"st{cp}")
        CS["st_end"][cc] = P.cnt[f"st{cp}"]

    emit_A(0)
    emit_tq1(0)
    emit_B1(0)
    emit_A(1)
    for i in range(1, 8):
        emit_B2(i - 1)
        emit_tq1(i)
        emit_B1(i)
        if i + 1 < 8:
            emit_A(i + 1)
    emit_B2(7)

    P.wait("sync", "st0", P.cnt["st0"])
    P.wait("sync", "st1", P.cnt["st1"])
    P.finish()
    return nc


def wdst_idx(idx, w1q8, w2q8):
    if idx < 128:
        r, cc = divmod(idx, 8)
        return w1q8[r * 128:(r + 1) * 128, cc * 1024:(cc + 1) * 1024]
    i, c2 = divmod(idx - 128, 2)
    return w2q8[i * 128:(i + 1) * 128, c2 * 1024:(c2 + 1) * 1024]


def prep_inputs(ins, b):
    co = np.ascontiguousarray
    d = {
        "x_in": co(np.asarray(ins["L0_attn_x"])[b]).astype(np.float32),
        "wqT": co(np.asarray(ins["L0_attn_q_w"]).T),
        "wkT": co(np.asarray(ins["L0_attn_k_w"]).T),
        "wvT": co(np.asarray(ins["L0_attn_v_w"]).T),
        "woT": co(np.asarray(ins["L0_attn_o_w"]).T),
        "gwT": co(np.asarray(ins["L0_attn_gate_w"]).T),
        "gb": co(np.asarray(ins["L0_attn_gate_b"]).reshape(HD, 1)),
        "w1T": co(np.asarray(ins["L0_ffn_w1"]).T),
        "w2T": co(np.asarray(ins["L0_ffn_w2"]).T),
        "b1c": co(np.broadcast_to(np.asarray(ins["L0_ffn_b1"])[None, :], (128, INNER))),
        "lngc": co(np.broadcast_to(np.asarray(ins["L0_ffn_ln_g"])[None, :], (128, INNER))),
        "lnbc": co(np.broadcast_to(np.asarray(ins["L0_ffn_ln_b"])[None, :], (128, INNER))),
        "b2c": co(np.broadcast_to(np.asarray(ins["L0_ffn_b2"])[None, :], (128, D))),
    }
    return {k: np.asarray(v, dtype=np.float32) for k, v in d.items()}


def kernel(**inputs):
    nc = build_nc(stage=4)
    x = np.asarray(inputs["L0_attn_x"])
    ncores = x.shape[0]
    common = None
    in_maps = []
    for b in range(ncores):
        d = prep_inputs(inputs, b)
        if common is None:
            common = {k: v for k, v in d.items() if k != "x_in"}
        else:
            for k in common:
                d[k] = common[k]
        in_maps.append(d)
    res = run_bass_kernel_spmd(nc, in_maps, core_ids=list(range(ncores)))
    out = np.stack([np.asarray(r["y_out"]) for r in res.results], axis=0)
    return out.astype(np.float32)


# version 11 (final)
# speedup vs baseline: 1.1402x; 1.0538x over previous
import numpy as np
import ml_dtypes
import concourse.bass as bass
import concourse.mybir as mybir
from concourse.bass_utils import run_bass_kernel_spmd

F32 = mybir.dt.float32
BF16 = mybir.dt.bfloat16
FP8 = mybir.dt.float8e4
I32 = mybir.dt.int32
OP = mybir.AluOpType
AF = mybir.ActivationFunctionType
AX = mybir.AxisListType.X

S, D, H, HD, INNER = 2048, 2048, 16, 128, 8192
NB = D // 128
NT = S // 128
EPS = 1e-5
QSC = float(1.0 / np.sqrt(128.0))
ENGINES = ("sync", "tensor", "vector", "scalar", "gpsimd")


class Prog:
    def __init__(self, nc):
        self.nc = nc
        self.cmds = {e: [] for e in ENGINES}
        self.cnt = {}
        self.sems = {}
        self._sem_cms = []

    def sem(self, name):
        if name not in self.sems:
            cm = self.nc.semaphore(name)
            self.sems[name] = cm.__enter__()
            self._sem_cms.append(cm)
            self.cnt[name] = 0
        return self.sems[name]

    def dma(self, q, out, in_, sem, transpose=False):
        s = self.sem(sem)
        self.cnt[sem] += 16
        if transpose:
            self.cmds[q].append(lambda e, o=out, i=in_: e.dma_start_transpose(out=o, in_=i).then_inc(s, 16))
        else:
            self.cmds[q].append(lambda e, o=out, i=in_: e.dma_start(out=o, in_=i).then_inc(s, 16))
        return self.cnt[sem]

    def wait(self, eng, sem, val=None):
        s = self.sem(sem)
        v = self.cnt[sem] if val is None else val
        if v > 0:
            self.cmds[eng].append(lambda e, vv=v: e.wait_ge(s, vv))

    def op(self, eng, fn, sem=None):
        if sem is None:
            self.cmds[eng].append(fn)
            return None
        s = self.sem(sem)
        self.cnt[sem] += 1
        v = self.cnt[sem]
        self.cmds[eng].append(lambda e, f=fn: f(e).then_inc(s, 1))
        return v

    def opd(self, eng, fn, sem=None):
        self.cmds[eng].append(fn)
        if sem is None:
            self.cmds[eng].append(lambda e: e.drain())
            return None
        s = self.sem(sem)
        self.cnt[sem] += 1
        v = self.cnt[sem]
        self.cmds[eng].append(lambda e: e.drain().then_inc(s, 1))
        return v

    def finish(self):
        nc = self.nc
        def mk(eng):
            cmds = self.cmds[eng]
            def run(e):
                for fn in cmds:
                    fn(e)
            return run
        with nc.Block() as block:
            for eng in ENGINES:
                getattr(block, eng)(mk(eng))


def build_nc(stage=4, dbg=(), heads=None):
    nc = bass.Bass("TRN2", target_bir_lowering=False, detect_race_conditions=False)
    P = Prog(nc)

    x_in = nc.dram_tensor("x_in", [S, D], F32, kind="ExternalInput").ap()
    wqT = nc.dram_tensor("wqT", [D, D], F32, kind="ExternalInput").ap()
    wkT = nc.dram_tensor("wkT", [D, D], F32, kind="ExternalInput").ap()
    wvT = nc.dram_tensor("wvT", [D, D], F32, kind="ExternalInput").ap()
    woT = nc.dram_tensor("woT", [D, D], F32, kind="ExternalInput").ap()
    gwT = nc.dram_tensor("gwT", [HD, HD], F32, kind="ExternalInput").ap()
    gb = nc.dram_tensor("gb", [HD, 1], F32, kind="ExternalInput").ap()
    w1T = nc.dram_tensor("w1T", [D, INNER], F32, kind="ExternalInput").ap()
    w2T = nc.dram_tensor("w2T", [INNER, D], F32, kind="ExternalInput").ap()
    b1c = nc.dram_tensor("b1c", [128, INNER], F32, kind="ExternalInput").ap()
    lngc = nc.dram_tensor("lngc", [128, INNER], F32, kind="ExternalInput").ap()
    lnbc = nc.dram_tensor("lnbc", [128, INNER], F32, kind="ExternalInput").ap()
    b2c = nc.dram_tensor("b2c", [128, D], F32, kind="ExternalInput").ap()
    maddc = nc.dram_tensor("maddc", [128, 4 * 512], F32, kind="ExternalInput").ap()
    y_out = nc.dram_tensor("y_out", [S, D], F32, kind="ExternalOutput").ap()

    def scratch(name, shape, dt):
        kind = "ExternalOutput" if name in dbg else "Internal"
        return nc.dram_tensor(name, shape, dt, kind=kind).ap()

    qkvs = scratch("qkvs", [3, 2, H, 128, S], BF16)
    outs_h = scratch("outs_h", [H, 2, 128, S], BF16)
    x1s = scratch("x1s", [S, D], F32)
    w1q8 = scratch("w1q8", [D, INNER], FP8)
    w2q8 = scratch("w2q8", [INNER, D], FP8)

    A = nc.alloc_sbuf_tensor
    eye_f = A("eye_f", [128, 128], F32).ap()
    ones_col_bf = A("ones_col_bf", [128, 1], BF16).ap()
    ones_row_f = A("ones_row_f", [1, 128], F32).ap()
    negones_row_f = A("negones_row_f", [1, 128], F32).ap()
    gbt = A("gbt", [128, 1], F32).ap()
    gw_hi = A("gw_hi", [128, 128], BF16).ap()
    gw_lo = A("gw_lo", [128, 128], BF16).ap()
    gwf = A("gwf", [128, 128], F32).ap()
    swb = A("swb", [128, 2], F32).ap()
    stats = A("stats", [128, 96], F32).ap()

    tone = A("tone", [1, 8], F32).ap()

    pb = [nc.alloc_psum_tensor(f"pb{i}", [128, 512], F32).ap() for i in range(8)]

    scopes = {}

    def alloc(group, name, shape, dt, **kw):
        cm = nc.sbuf_tensor(name, shape, dt, **kw)
        t = cm.__enter__()
        scopes.setdefault(group, []).append(cm)
        return t.ap()

    def free_group(group):
        for cm in reversed(scopes.pop(group, [])):
            cm.__exit__(None, None, None)


    wtile = [alloc("pw", f"wtile{i}", [128, 1024], F32) for i in range(2)]
    wjunk = alloc("pw", "wjunk", [128, 1024], F32)
    wtern = [alloc("pw", f"wtern{i}", [128, 1024], FP8) for i in range(2)]
    wp1 = alloc("pw", "wp1", [128, 1024], F32)
    wp2 = alloc("pw", "wp2", [128, 1024], F32)
    wabs = alloc("pw", "wabs", [128, 260], F32)
    wth = alloc("pw", "wth", [128, 4], F32)
    NPW = 256

    def wsrc(idx, q8=False):
        if idx < 128:
            r, cc = divmod(idx, 8)
            t = w1q8 if q8 else w1T
            return t[r * 128:(r + 1) * 128, cc * 1024:(cc + 1) * 1024]
        i, c2 = divmod(idx - 128, 2)
        return (w2q8 if q8 else w2T)[i * 128:(i + 1) * 128, c2 * 1024:(c2 + 1) * 1024]

    pw = {"lda": [0] * 260, "absdone": [0] * 256, "ldb": [0] * 260, "tern": [0] * 256,
          "sw_done": 0}

    def pwa_step(idx):
        bi = idx % 2
        if idx < 2:
            pw["lda"][idx] = P.dma("scalar", wtile[bi], wsrc(idx), f"wt{bi}")
        P.wait("scalar", f"wt{bi}", pw["lda"][idx])
        pw["absdone"][idx] = P.opd("scalar", lambda e, b=bi, i=idx: e.activation(
            out=wjunk, in_=wtile[b], func=AF.Abs, accum_out=wabs[:, i:i + 1]), "s0")
        if idx + 2 < NPW:
            pw["lda"][idx + 2] = P.dma("scalar", wtile[(idx + 2) % 2], wsrc(idx + 2), f"wt{bi}")

    def pwa_chain():
        P.wait("vector", "s0", pw["absdone"][NPW - 1])
        P.opd("vector", lambda e: e.reduce_sum(out=wabs[:, 256:257], in_=wabs[:, 0:128], axis=AX))
        vred = P.opd("vector", lambda e: e.reduce_sum(out=wabs[:, 257:258], in_=wabs[:, 128:256],
                                                      axis=AX), "v0")
        P.wait("tensor", "v0", vred)
        P.op("tensor", lambda e: e.transpose(pb[0][0:1, 0:128], wabs[:, 256:257], eye_f), "t0")
        t2v = P.op("tensor", lambda e: e.transpose(pb[1][0:1, 0:128], wabs[:, 257:258], eye_f), "t0")
        P.wait("vector", "t0", t2v)
        vsw = 0
        for i in range(2):
            P.opd("vector", lambda e, i=i: e.reduce_sum(out=tone[0:1, i:i + 1],
                                                        in_=pb[i][0:1, 0:128], axis=AX))
            P.opd("vector", lambda e, i=i: e.tensor_scalar(out=tone[0:1, i:i + 1],
                 in0=tone[0:1, i:i + 1], scalar1=1.0 / (D * INNER), scalar2=1e-5,
                 op0=OP.mult, op1=OP.max))
            vsw = P.opd("vector", lambda e, i=i: e.reciprocal(out=tone[0:1, i:i + 1],
                        in_=tone[0:1, i:i + 1]), "v0")
        P.wait("tensor", "v0", vsw)
        P.op("tensor", lambda e: e.matmul(pb[2][:, 0:1], ones_row_f, tone[0:1, 0:1],
                                          start=True, stop=True), "t0")
        tb2 = P.op("tensor", lambda e: e.matmul(pb[2][:, 1:2], ones_row_f, tone[0:1, 1:2],
                                                start=True, stop=True), "t0")
        P.wait("vector", "t0", tb2)
        P.opd("vector", lambda e: e.tensor_copy(out=swb, in_=pb[2][:, 0:2]), "v0")
        P.opd("vector", lambda e: e.reciprocal(out=wth[:, 0:2], in_=swb), "v0")
        P.opd("vector", lambda e: e.tensor_scalar(out=wth[:, 0:2], in0=wth[:, 0:2],
             scalar1=0.5, scalar2=None, op0=OP.mult), "v0")
        pw["sw_done"] = P.opd("vector", lambda e: e.tensor_scalar(out=wth[:, 2:4], in0=wth[:, 0:2],
             scalar1=-1.0, scalar2=None, op0=OP.mult), "v0")

    def pwb_step(idx):
        bi = idx % 2
        if idx < 2:
            pw["ldb"][idx] = P.dma("scalar", wtile[bi], wsrc(idx), f"wt{bi}")
        P.wait("vector", f"wt{bi}", pw["ldb"][idx])
        if idx < 2:
            P.wait("vector", "v0", pw["sw_done"])
        if idx >= 2:
            P.wait("vector", "w8", 16 * (idx - 1))
        col = 0 if idx < 128 else 1
        P.opd("vector", lambda e, b=bi, c=col: e.tensor_scalar(out=wp1, in0=wtile[b],
             scalar1=wth[:, c:c + 1], scalar2=None, op0=OP.is_gt), "v0")
        P.op("vector", lambda e, b=bi, c=col: e.tensor_scalar(out=wp2, in0=wtile[b],
             scalar1=wth[:, 2 + c:3 + c], scalar2=None, op0=OP.is_lt), "v0")
        tn = P.opd("vector", lambda e, b=bi: e.tensor_tensor(out=wtern[b], in0=wp1, in1=wp2,
                                                             op=OP.subtract), "v0")
        pw["tern"][idx] = tn
        P.wait("sync", "v0", tn)
        P.dma("sync", wdst_idx(idx, w1q8, w2q8), wtern[bi], "w8")
        if idx + 2 < NPW:
            P.wait("scalar", "v0", tn)
            pw["ldb"][idx + 2] = P.dma("scalar", wtile[(idx + 2) % 2], wsrc(idx + 2), f"wt{bi}")

    ln1xT_hi = alloc("p01", "ln1xT_hi", [128, NB * S], BF16)
    ln1xT_lo = alloc("p01", "ln1xT_lo", [128, NB * S], BF16)
    iot_i = alloc("pinit", "iot_i", [128, 512], I32)
    iot_f = alloc("pinit", "iot_f", [128, 512], F32)
    iot_p = alloc("pinit", "iot_p", [128, 1], I32)
    iot_pf = alloc("pinit", "iot_pf", [128, 1], F32)
    P.op("gpsimd", lambda e: e.memset(stats, 0.0), "g0")
    P.op("gpsimd", lambda e: e.iota(iot_i, pattern=[[1, 512]], base=0, channel_multiplier=0), "g0")
    P.op("gpsimd", lambda e: e.iota(iot_p, pattern=[[1, 1]], base=0, channel_multiplier=1), "g0")
    P.op("gpsimd", lambda e: e.memset(ones_col_bf, 1.0), "g0")
    P.op("gpsimd", lambda e: e.memset(ones_row_f, 1.0), "g0")
    P.op("gpsimd", lambda e: e.memset(negones_row_f, -1.0), "g0")
    P.wait("vector", "g0")
    P.op("vector", lambda e: e.tensor_copy(out=iot_f, in_=iot_i), "v0")
    P.opd("vector", lambda e: e.tensor_copy(out=iot_pf, in_=iot_p), "v0")
    P.op("vector", lambda e: e.tensor_scalar(out=eye_f, in0=iot_f[:, 0:128], scalar1=iot_pf,
                                             scalar2=None, op0=OP.is_equal), "v0")
    P.dma("sync", gwf, gwT, "xl")
    P.dma("sync", gbt, gb, "xl")
    P.wait("scalar", "xl")
    hi0 = P.op("scalar", lambda e: e.activation(out=gw_hi, in_=gwf, func=AF.Copy), "s0")
    P.wait("vector", "s0", hi0)
    P.op("vector", lambda e: e.scalar_tensor_tensor(out=gw_lo, in0=gwf, scalar=1.0, in1=gw_hi,
                                                    op0=OP.mult, op1=OP.subtract), "v0")

    free_group("pinit")
    xrow = [alloc("p0", f"xrow{i}", [128, D], F32) for i in range(2)]
    lnx0 = alloc("p0", "lnx0", [128, D], F32)
    lnx = [lnx0, lnx0]
    lnh = [alloc("p0", f"lnh{i}", [128, D], BF16) for i in range(2)]
    lnl = [alloc("p0", f"lnl{i}", [128, D], BF16) for i in range(2)]

    tile_done_s = [0] * NT
    tile_tr = [0] * NT
    tile_lo = [0] * NT
    P.wait("sync", "v0", P.cnt["v0"])

    for t in range(NT):
        bi = t % 2
        if t >= 2:
            P.wait("sync", "s0", tile_done_s[t - 2])
        xl_done = P.dma("sync", xrow[bi], x_in[t * 128:(t + 1) * 128, :], f"xl{bi}")
        P.wait("vector", f"xl{bi}", xl_done)
        P.wait("scalar", f"xl{bi}", xl_done)
        if t >= 2:
            P.wait("scalar", f"tr{bi}", tile_tr[t - 2])
            P.wait("vector", f"tr{bi}", tile_tr[t - 2])
        if t >= 1:
            P.wait("scalar", "v0", tile_lo[t - 1])
        c = 8 + bi * 8
        P.opd("vector", lambda e, b=bi, c=c: e.reduce_sum(out=stats[:, c:c + 1], in_=xrow[b], axis=AX))
        sq_done = P.opd("scalar", lambda e, b=bi, c=c: e.activation(out=lnx[b], in_=xrow[b], func=AF.Square,
                                                                    accum_out=stats[:, c + 1:c + 2]), "s0")
        P.wait("vector", "s0", sq_done)
        P.opd("vector", lambda e, c=c: e.tensor_scalar(out=stats[:, c + 2:c + 3], in0=stats[:, c:c + 1],
                                                       scalar1=1.0 / D, scalar2=None, op0=OP.mult))
        P.opd("vector", lambda e, c=c: e.tensor_scalar(out=stats[:, c + 3:c + 4], in0=stats[:, c + 1:c + 2],
                                                       scalar1=1.0 / D, scalar2=None, op0=OP.mult))
        P.opd("vector", lambda e, c=c: e.tensor_tensor(out=stats[:, c + 4:c + 5], in0=stats[:, c + 2:c + 3],
                                                       in1=stats[:, c + 2:c + 3], op=OP.mult))
        P.opd("vector", lambda e, c=c: e.tensor_tensor(out=stats[:, c + 4:c + 5], in0=stats[:, c + 3:c + 4],
                                                       in1=stats[:, c + 4:c + 5], op=OP.subtract))
        P.opd("vector", lambda e, c=c: e.tensor_scalar(out=stats[:, c + 4:c + 5], in0=stats[:, c + 4:c + 5],
                                                       scalar1=EPS, scalar2=None, op0=OP.add))
        rc = P.opd("vector", lambda e, c=c: e.reciprocal(out=stats[:, c + 4:c + 5], in_=stats[:, c + 4:c + 5]), "v0")
        P.wait("scalar", "v0", rc)
        sq2 = P.opd("scalar", lambda e, c=c: e.activation(out=stats[:, c + 5:c + 6], in_=stats[:, c + 4:c + 5],
                                                          func=AF.Sqrt), "s0")
        P.wait("vector", "s0", sq2)
        P.opd("vector", lambda e, c=c: e.tensor_scalar(out=stats[:, c + 6:c + 7], in0=stats[:, c + 2:c + 3],
                                                       scalar1=stats[:, c + 5:c + 6], scalar2=None,
                                                       op0=OP.mult))
        nmr = P.opd("vector", lambda e, c=c: e.tensor_scalar(out=stats[:, c + 6:c + 7], in0=stats[:, c + 6:c + 7],
                                                             scalar1=-1.0, scalar2=None, op0=OP.mult), "v0")
        P.wait("scalar", "v0", nmr)
        P.op("scalar", lambda e, b=bi, c=c: e.activation(out=lnx[b], in_=xrow[b], func=AF.Identity,
                                                         scale=stats[:, c + 5:c + 6],
                                                         bias=stats[:, c + 6:c + 7]), "s0")
        hi_done = P.op("scalar", lambda e, b=bi: e.activation(out=lnh[b], in_=lnx[b], func=AF.Copy), "s0")
        tile_done_s[t] = hi_done
        P.wait("vector", "s0", hi_done)
        lo_done = P.op("vector", lambda e, b=bi: e.scalar_tensor_tensor(out=lnl[b], in0=lnx[b], scalar=1.0,
                                                                        in1=lnh[b], op0=OP.mult,
                                                                        op1=OP.subtract), "v0")
        tile_lo[t] = lo_done
        P.wait("sync", "s0", hi_done)
        P.wait("sync", "v0", lo_done)
        for src, dst in ((lnh[bi], ln1xT_hi), (lnl[bi], ln1xT_lo)):
            dst3 = dst.rearrange("p (b w) -> p b w", w=S)[:, :, t * 128:(t + 1) * 128]
            src3 = src.rearrange("p (b c) -> p b c", c=128)
            P.dma("sync", dst3, src3, f"tr{bi}", transpose=True)
        tile_tr[t] = P.cnt[f"tr{bi}"]
    tr_all = (P.cnt["tr0"], P.cnt["tr1"])

    free_group("p0")
    wcol_f = [alloc("p1", f"wcol_f{i}", [128, NB * 128], F32) for i in range(2)]
    wcol_h = [alloc("p1", f"wcol_h{i}", [128, NB * 128], BF16) for i in range(2)]
    wcol_l = [alloc("p1", f"wcol_l{i}", [128, NB * 128], BF16) for i in range(2)]
    qkv_h = alloc("p1", "qkv_h", [128, D], BF16)
    qkv_l = alloc("p1", "qkv_l", [128, D], BF16)

    P.wait("tensor", "tr0", tr_all[0])
    P.wait("tensor", "tr1", tr_all[1])
    P.wait("gpsimd", "tr0", tr_all[0])
    P.wait("gpsimd", "tr1", tr_all[1])
    wTs = {0: wqT, 1: wkT, 2: wvT}
    it_t1 = [0] * 48
    it_evv = [0] * 48
    it_evs = [0] * 48
    it_lo = [0] * 48
    for p in range(3):
        qsc = QSC if p == 0 else 1.0
        for eb in range(NB):
            it = p * NB + eb
            bi = it % 2
            if it >= 2:
                P.wait("gpsimd", "t1", it_t1[it - 2])
                P.wait("gpsimd", "v1", it_lo[it - 2])
            src = wTs[p].rearrange("(a q) e -> q a e", q=128)[:, :, eb * 128:(eb + 1) * 128]
            dstf = wcol_f[bi].rearrange("q (a c) -> q a c", c=128)
            dsth = wcol_h[bi].rearrange("q (a c) -> q a c", c=128)
            P.dma("gpsimd", dstf, src, f"wq{bi}")
            wq_done = P.dma("gpsimd", dsth, src, f"wq{bi}")
            P.wait("vector", f"wq{bi}", wq_done)
            if it >= 2:
                P.wait("vector", "t1", it_t1[it - 2])
            it_lo[it] = P.op("vector", lambda e, b=bi: e.scalar_tensor_tensor(
                out=wcol_l[b], in0=wcol_f[b], scalar=1.0, in1=wcol_h[b],
                op0=OP.mult, op1=OP.subtract), "v1")
            grp = it % 2
            banks = pb[grp * 4:grp * 4 + 4]
            P.wait("tensor", f"wq{bi}", wq_done)
            P.wait("tensor", "v1", it_lo[it])
            if it >= 2:
                P.wait("tensor", "v1s", it_evv[it - 2])
                P.wait("tensor", "s1s", it_evs[it - 2])
            n = 0
            for k in range(NB):
                wh = wcol_h[bi][:, k * 128:(k + 1) * 128]
                wl = wcol_l[bi][:, k * 128:(k + 1) * 128]
                for wop, xops in ((wh, (ln1xT_hi, ln1xT_lo)), (wl, (ln1xT_hi,))):
                    for xop in xops:
                        for j in range(4):
                            n += 1
                            first = n <= 4
                            last = n == 192
                            sem = "t1" if last else None
                            P.op("tensor", lambda e, bk=banks[j], w=wop, x=xop, k=k, j=j,
                                 st=first, sp=last or n > 188:
                                 e.matmul(bk, w, x[:, k * S + j * 512:k * S + (j + 1) * 512],
                                          start=st, stop=sp), sem)
            it_t1[it] = P.cnt["t1"]
            P.wait("scalar", "t1", it_t1[it])
            if it >= 1:
                P.wait("scalar", "spq", 32 * it)
            for j in range(4):
                P.op("scalar", lambda e, bk=banks[j], j=j, sc=qsc: e.activation(
                    out=qkv_h[:, j * 512:(j + 1) * 512], in_=bk, func=AF.Copy, scale=sc), "s1s")
            it_evs[it] = P.cnt["s1s"]
            P.wait("vector", "t1", it_t1[it])
            P.wait("vector", "s1s", it_evs[it])
            if it >= 1:
                P.wait("vector", "spq", 32 * it)
            for j in range(4):
                P.op("vector", lambda e, bk=banks[j], j=j, sc=qsc: e.scalar_tensor_tensor(
                    out=qkv_l[:, j * 512:(j + 1) * 512], in0=bk, scalar=sc,
                    in1=qkv_h[:, j * 512:(j + 1) * 512], op0=OP.mult, op1=OP.subtract), "v1s")
            it_evv[it] = P.cnt["v1s"]
            P.wait("sync", "s1s", it_evs[it])
            P.wait("sync", "v1s", it_evv[it])
            P.dma("sync", qkvs[p, 0, eb], qkv_h, "spq")
            P.dma("sync", qkvs[p, 1, eb], qkv_l, "spq")
            for ii in range((it * 256) // 48, ((it + 1) * 256) // 48):
                pwa_step(ii)
    spq_all = P.cnt["spq"]
    P.wait("tensor", "s1s", P.cnt["s1s"])
    P.wait("tensor", "v1s", P.cnt["v1s"])
    pwa_chain()

    if stage <= 1:
        P.wait("sync", "spq", spq_all)
        P.wait("sync", "w8", P.cnt["w8"])
        P.dma("sync", y_out[0:128, 0:64], stats[:, 0:64], "st")
        P.wait("sync", "st", P.cnt["st"])
        P.finish()
        return nc

    free_group("p1")
    free_group("p01")
    qrb = [[alloc("p2", f"q{pt}{i}", [128, S], BF16) for pt in (0, 1)] for i in range(2)]
    krb = [[alloc("p2", f"k{pt}{i}", [128, S], BF16) for pt in (0, 1)] for i in range(2)]
    vrb = [[alloc("p2", f"v{pt}{i}", [128, S], BF16) for pt in (0, 1)] for i in range(2)]
    vtok = [[alloc("p2", f"vt{pt}{i}", [128, S], BF16, is_transpose_dest=True) for pt in (0, 1)]
            for i in range(2)]
    S_sb = [alloc("p2", "S_sb0", [128, 12 * 512], F32), alloc("p2", "S_sb1", [128, 16 * 512], F32)]
    P32 = [alloc("p2", f"P32_{i}", [128, 512], F32) for i in range(2)]
    Phi = [alloc("p2", f"Phi{i}", [128, 512], BF16) for i in range(4)]
    Plo = [alloc("p2", f"Plo{i}", [128, 512], BF16) for i in range(4)]
    sgbuf = [alloc("p2", f"sg{i}", [128, S], F32) for i in range(2)]
    outst = [[alloc("p2", f"os{pt}{i}", [128, S], BF16) for pt in (0, 1)] for i in range(2)]
    t2sb = alloc("p2", "t2sb", [128, 512], F32)
    t3sb = alloc("p2", "t3sb", [128, 512], F32)
    macc = alloc("p2", "macc", [128, 16], F32)
    mpcol = alloc("p2", "mpcol", [128, 1], F32)
    cnrow = alloc("p2", "cnrow", [1, 1], F32)
    srow = alloc("p2", "srow", [1, 512], F32)
    rrow = alloc("p2", "rrow", [1, 512], F32)
    negc = [alloc("p2", f"negc{i}", [128, 1], F32) for i in range(2)]
    madd = alloc("p2", "madd", [128, 4 * 512], F32)

    SC = pb[0:4]
    AVB = pb[4]
    GB = pb[5]
    SURB = [pb[6], pb[7]]

    P.wait("sync", "spq", spq_all)
    mdl_done = P.dma("sync", madd, maddc, "mdl")
    P.wait("vector", "mdl", mdl_done)
    P.wait("tensor", "v0", pw["sw_done"])

    st2 = {"bank_ev": {}, "gb_read": 0, "av_read": 0, "sur_read": [0, 0],
           "head_t2_end": [0] * (H + 2), "spo_after": [0] * (H + 2),
           "head_v2_end": [0] * (H + 2), "sg_free": [0] * (H + 2)}

    def emit_head(h):
        hp = h % 2
        if h >= 2:
            P.wait("sync", "t2", st2["head_t2_end"][h - 1])
        for p, dstl in ((0, qrb), (1, krb), (2, vrb)):
            for part in (0, 1):
                P.dma("sync", dstl[hp][part], qkvs[p, part, h], f"rq{hp}")
        rq_done = P.cnt[f"rq{hp}"]
        P.wait("sync", f"rq{hp}", rq_done)
        for part in (0, 1):
            src3 = vrb[hp][part].rearrange("p (b c) -> p b c", c=128)
            dst3 = vtok[hp][part].rearrange("p (b c) -> p b c", c=128)
            P.dma("sync", dst3, src3, f"trv{hp}", transpose=True)
        trv_done = P.cnt[f"trv{hp}"]

        P.wait("tensor", f"rq{hp}", rq_done)
        P.wait("tensor", f"trv{hp}", trv_done)
        P.wait("tensor", "v2", st2["gb_read"])
        if h >= 2:
            P.wait("scalar", "v2", st2["sg_free"][h - 2])
        sg_done = [0] * 4
        for c in range(4):
            n = 0
            for wop, xop in ((gw_hi, qrb[hp][0]), (gw_hi, qrb[hp][1]), (gw_lo, qrb[hp][0])):
                n += 1
                P.op("tensor", lambda e, w=wop, x=xop, c=c, st=(n == 1), sp=(n == 3):
                     e.matmul(GB, w, x[:, c * 512:(c + 1) * 512], start=st, stop=sp),
                     "t2" if n == 3 else None)
            P.wait("scalar", "t2", P.cnt["t2"])
            sg_done[c] = P.op("scalar", lambda e, c=c, b=hp: e.activation(
                out=sgbuf[b][:, c * 512:(c + 1) * 512], in_=GB, func=AF.Sigmoid,
                scale=float(np.sqrt(128.0)), bias=gbt), "s2")
            if c < 3:
                P.wait("tensor", "s2", sg_done[c])

        split_v2 = {}
        blk_t2 = {}
        for c in range(4):
            nblk = 4 * c + 4
            cp = c % 2
            ssb = S_sb[cp]
            ev_v2 = [0] * nblk
            for b in range(nblk):
                bki = 2 * cp + b % 2
                bank = SC[bki]
                prev_ev = st2["bank_ev"].get(bki, 0)
                if prev_ev:
                    P.wait("tensor", "v2", prev_ev)
                n = 0
                for wop, xop in ((krb[hp][0], qrb[hp][0]), (krb[hp][0], qrb[hp][1]),
                                 (krb[hp][1], qrb[hp][0])):
                    n += 1
                    P.op("tensor", lambda e, bk=bank, w=wop, x=xop, b=b, c=c,
                         st=(n == 1), sp=(n == 3):
                         e.matmul(bk, w[:, b * 128:(b + 1) * 128],
                                  x[:, c * 512:(c + 1) * 512], start=st, stop=sp),
                         "t2" if n == 3 else None)
                t_sc = P.cnt["t2"]
                P.wait("vector", "t2", t_sc)
                if b >= 4 * c:
                    j = b - 4 * c
                    P.op("vector", lambda e, bk=bank, b=b, j=j, s=ssb: e.tensor_tensor(
                        out=s[:, b * 512:(b + 1) * 512], in0=bk, in1=madd[:, j * 512:(j + 1) * 512],
                        op=OP.add), "v2")
                    ev_v2[b] = P.op("vector", lambda e, b=b, s=ssb: e.reduce_max(
                        out=macc[:, b:b + 1], in_=s[:, b * 512:(b + 1) * 512], axis=AX), "v2")
                else:
                    ev_v2[b] = P.op("vector", lambda e, bk=bank, b=b, s=ssb: e.tensor_scalar(
                        out=s[:, b * 512:(b + 1) * 512], in0=bk, scalar1=1.0, scalar2=None,
                        op0=OP.mult, op1=OP.max, accum_out=macc[:, b:b + 1]), "v2")
                st2["bank_ev"][bki] = ev_v2[b]
            P.op("vector", lambda e: e.drain())
            mp_v2 = P.opd("vector", lambda e, n=nblk: e.reduce_max(out=mpcol, in_=macc[:, 0:n],
                                                                   axis=AX), "v2")
            P.wait("tensor", "v2", mp_v2)
            P.wait("tensor", "v2", st2["sur_read"][cp])
            P.op("tensor", lambda e, cp=cp: e.transpose(SURB[cp][0:1, 0:128], mpcol, eye_f), "t2")
            P.wait("vector", "t2", P.cnt["t2"])
            cn_v2 = P.opd("vector", lambda e, cp=cp: e.reduce_max(out=cnrow, in_=SURB[cp][0:1, 0:128],
                                                                  axis=AX, negate=True), "v2")
            P.wait("tensor", "v2", cn_v2)
            nb_t2 = P.op("tensor", lambda e, cp=cp: e.matmul(SURB[cp][:, 128:129], ones_row_f,
                                                             cnrow[0:1, 0:1], start=True, stop=True), "t2")
            P.wait("scalar", "t2", nb_t2)
            P.opd("scalar", lambda e, cp=cp: e.copy(out=negc[cp], in_=SURB[cp][:, 128:129]), "s2")

            av_n = sum_n = 0
            for b in range(nblk):
                rot = b % 4
                if b >= 2:
                    P.wait("scalar", "v2", split_v2[(c, b - 2)])
                ex = P.op("scalar", lambda e, b=b, cp=cp, s=ssb: e.activation(
                    out=P32[b % 2], in_=s[:, b * 512:(b + 1) * 512], func=AF.Exp,
                    bias=negc[cp], scale=1.0), "s2")
                P.wait("vector", "s2", ex)
                if b >= 4:
                    P.wait("vector", "t2", blk_t2[(c, b - 4)])
                P.op("vector", lambda e, b=b, rot=rot: e.tensor_copy(out=Phi[rot], in_=P32[b % 2]), "v2")
                split_v2[(c, b)] = P.op("vector", lambda e, b=b, rot=rot: e.scalar_tensor_tensor(
                    out=Plo[rot], in0=P32[b % 2], scalar=1.0, in1=Phi[rot],
                    op0=OP.mult, op1=OP.subtract), "v2")
                P.wait("tensor", "v2", split_v2[(c, b)])
                if b == 0:
                    P.wait("tensor", "v2", st2["av_read"])
                for k5, (wop, xop) in enumerate((
                        (vtok[hp][0], Phi[rot]), (vtok[hp][0], Plo[rot]), (vtok[hp][1], Phi[rot]),
                        (ones_col_bf, Phi[rot]), (ones_col_bf, Plo[rot]))):
                    if k5 < 3:
                        av_n += 1
                        st, sp, o = av_n == 1, av_n == 3 * nblk, AVB
                        w_ap = wop[:, b * 128:(b + 1) * 128]
                    else:
                        sum_n += 1
                        st, sp, o = sum_n == 1, sum_n == 2 * nblk, SURB[cp][0:1, 0:512]
                        w_ap = wop
                    P.op("tensor", lambda e, o=o, w=w_ap, x=xop, st=st, sp=sp:
                         e.matmul(o, w, x, start=st, stop=sp),
                         "t2" if k5 == 4 else None)
                blk_t2[(c, b)] = P.cnt["t2"]
            mm_end = P.cnt["t2"]

            P.wait("vector", "t2", mm_end)
            P.opd("vector", lambda e, cp=cp: e.tensor_copy(out=srow, in_=SURB[cp][0:1, 0:512]))
            rec_v2 = P.opd("vector", lambda e: e.reciprocal(out=rrow, in_=srow), "v2")
            st2["sur_read"][cp] = rec_v2
            P.wait("tensor", "v2", rec_v2)
            if c == 0:
                P.wait("tensor", "s2", sg_done[3])
            bc_t2 = P.op("tensor", lambda e: e.matmul(GB, ones_row_f, rrow[0:1, 0:512],
                                                      start=True, stop=True), "t2")
            P.wait("vector", "t2", bc_t2)
            if h >= 2 and c == 0:
                P.wait("vector", f"spo{hp}", st2["spo_after"][h - 2])
            P.op("vector", lambda e, c=c, b=hp: e.tensor_tensor(
                out=t2sb, in0=AVB, in1=sgbuf[b][:, c * 512:(c + 1) * 512], op=OP.mult), "v2")
            gb_read = P.op("vector", lambda e: e.tensor_tensor(out=t3sb, in0=t2sb, in1=GB,
                                                               op=OP.mult), "v2")
            st2["gb_read"] = gb_read
            st2["av_read"] = gb_read
            P.wait("scalar", "v2", gb_read)
            if h >= 2 and c == 0:
                P.wait("scalar", f"spo{hp}", st2["spo_after"][h - 2])
            sh = P.op("scalar", lambda e, c=c, b=hp: e.activation(
                out=outst[b][0][:, c * 512:(c + 1) * 512], in_=t3sb, func=AF.Copy), "s2")
            P.wait("vector", "s2", sh)
            P.op("vector", lambda e, c=c, b=hp: e.scalar_tensor_tensor(
                out=outst[b][1][:, c * 512:(c + 1) * 512], in0=t3sb, scalar=1.0,
                in1=outst[b][0][:, c * 512:(c + 1) * 512], op0=OP.mult, op1=OP.subtract), "v2")

        st2["head_t2_end"][h + 1] = P.cnt["t2"]
        st2["sg_free"][h] = P.cnt["v2"]
        st2["head_v2_end"][h] = P.cnt["v2"]
        P.wait("sync", "s2", P.cnt["s2"])
        P.wait("sync", "v2", P.cnt["v2"])
        for part in (0, 1):
            P.dma("sync", outs_h[h, part], outst[hp][part], f"spo{hp}")
        st2["spo_after"][h] = P.cnt[f"spo{hp}"]

    for h in range(heads if heads else H):
        emit_head(h)
    spo_all = (P.cnt.get("spo0", 0), P.cnt.get("spo1", 0))

    if stage <= 2:
        P.wait("sync", "spo0", spo_all[0])
        P.wait("sync", "spo1", spo_all[1])
        P.wait("sync", "w8", w8_all)
        P.dma("sync", y_out[0:128, 0:64], stats[:, 0:64], "st")
        P.wait("sync", "st", P.cnt["st"])
        P.finish()
        return nc

    free_group("p2")
    wocf = [alloc("p3", f"wocf{i}", [128, NB * 128], F32) for i in range(2)]
    woch = [alloc("p3", f"woch{i}", [128, NB * 128], BF16) for i in range(2)]
    wocl = [alloc("p3", f"wocl{i}", [128, NB * 128], BF16) for i in range(2)]
    ortile = alloc("p3", "ortile", [128, 32 * 512], BF16)
    attnT = alloc("p3", "attnT", [128, 16 * 512], F32)
    x1st = alloc("p3", "x1st", [128, 4 * D], F32)
    xrd = alloc("p3", "xrd", [128, 4 * D], F32)

    P.wait("sync", "spo0", spo_all[0])
    P.wait("sync", "spo1", spo_all[1])
    P.wait("gpsimd", "spo0", spo_all[0])
    P.wait("gpsimd", "spo1", spo_all[1])
    ACC = pb[0:4]
    TRB = pb[4:6]
    ev_s3 = {}
    tr_read_v3 = {}
    wo_lo_v3 = {}
    wo_t3 = {}
    def wo_load(it):
        bi = it % 2
        eb = it % NB
        if it >= 2:
            P.wait("gpsimd", "t3", wo_t3.get(it - 2, 0))
        src = woT.rearrange("(a q) e -> q a e", q=128)[:, :, eb * 128:(eb + 1) * 128]
        P.dma("gpsimd", wocf[bi].rearrange("q (a k) -> q a k", k=128), src, f"wo{bi}")
        wo_ld[it] = P.dma("gpsimd", woch[bi].rearrange("q (a k) -> q a k", k=128), src, f"wo{bi}")

    def wo_split(it):
        bi = it % 2
        P.wait("vector", f"wo{bi}", wo_ld[it])
        if it >= 2:
            P.wait("vector", "t3", wo_t3.get(it - 2, 0))
        wo_lo_v3[it] = P.op("vector", lambda e, b=bi: e.scalar_tensor_tensor(
            out=wocl[b], in0=wocf[b], scalar=1.0, in1=woch[b],
            op0=OP.mult, op1=OP.subtract), "v3")

    wo_ld = {}
    wo_load(0)
    wo_load(1)
    wo_split(0)
    for c in range(4):
        if c >= 1:
            P.wait("sync", "t3", P.cnt["t3"])
        for h2 in range(H):
            for part in (0, 1):
                P.dma("sync", ortile[:, (2 * h2 + part) * 512:(2 * h2 + part + 1) * 512],
                      outs_h[h2, part][:, c * 512:(c + 1) * 512], "ro")
        for ts in range(4):
            P.dma("sync", xrd[:, ts * D:(ts + 1) * D],
                  x_in[c * 512 + ts * 128:c * 512 + (ts + 1) * 128, :], "rx")
        ro_done, rx_done = P.cnt["ro"], P.cnt["rx"]
        if c >= 1:
            P.wait("vector", "spx", P.cnt["spx"])
        for eb in range(NB):
            it = c * NB + eb
            bi = it % 2
            bank = ACC[eb % 4]
            P.wait("tensor", f"wo{bi}", wo_ld[it])
            P.wait("tensor", "v3", wo_lo_v3[it])
            P.wait("tensor", "ro", ro_done)
            if it >= 4:
                P.wait("tensor", "s3", ev_s3[it - 4])
            n = 0
            for k in range(NB):
                for wop, part in ((woch[bi], 0), (woch[bi], 1), (wocl[bi], 0)):
                    n += 1
                    P.op("tensor", lambda e, bk=bank, w=wop, k=k, pt=part,
                         st=(n == 1), sp=(n == 48):
                         e.matmul(bk, w[:, k * 128:(k + 1) * 128],
                                  ortile[:, (2 * k + pt) * 512:(2 * k + pt + 1) * 512],
                                  start=st, stop=sp), "t3" if n == 48 else None)
            wo_t3[it] = P.cnt["t3"]
            if it + 2 < 64:
                wo_load(it + 2)
            if it + 1 < 64:
                wo_split(it + 1)
            P.wait("scalar", "t3", wo_t3[it])
            ev_s3[it] = P.op("scalar", lambda e, bk=bank, eb=eb: e.activation(
                out=attnT[:, eb * 512:(eb + 1) * 512], in_=bk, func=AF.Copy), "s3")
            P.wait("tensor", "s3", ev_s3[it])
            P.wait("vector", "rx", rx_done)
            for ts in range(4):
                trb = TRB[(eb * 4 + ts) % 2]
                prev = tr_read_v3.get((eb * 4 + ts) % 2, 0)
                if prev:
                    P.wait("tensor", "v3", prev)
                t_tr = P.op("tensor", lambda e, t=trb, eb=eb, ts=ts: e.transpose(
                    t[:, 0:128], attnT[:, eb * 512 + ts * 128:eb * 512 + (ts + 1) * 128], eye_f), "t3")
                P.wait("vector", "t3", t_tr)
                tr_read_v3[(eb * 4 + ts) % 2] = P.op("vector", lambda e, t=trb, eb=eb, ts=ts:
                    e.tensor_tensor(out=x1st[:, ts * D + eb * 128:ts * D + (eb + 1) * 128],
                                    in0=t[:, 0:128], in1=xrd[:, ts * D + eb * 128:ts * D + (eb + 1) * 128],
                                    op=OP.add), "v3")
            for ii in range((it * 256) // 64, ((it + 1) * 256) // 64):
                pwb_step(ii)
        P.wait("sync", "v3", P.cnt["v3"])
        for ts in range(4):
            P.dma("sync", x1s[c * 512 + ts * 128:c * 512 + (ts + 1) * 128, :],
                  x1st[:, ts * D:(ts + 1) * D], "spx")
    spx_all = P.cnt["spx"]

    w8_all = P.cnt["w8"]
    free_group("p3")
    free_group("pw")
    h_sb = [alloc("p4", f"h{i}", [128, INNER], F32) for i in range(2)]
    x1rd = [[alloc("p4", f"x1rd{p}{i}", [128, D], F32) for i in range(2)] for p in range(2)]
    xq1tok = [alloc("p4", f"xq1tok{i}", [128, D], BF16) for i in range(2)]
    xq1T = alloc("p4", "xq1T", [128, 16 * 256], BF16, is_transpose_dest=True)
    xq2c = [alloc("p4", f"xq2c{i}", [128, 512], BF16) for i in range(4)]
    xq2T = alloc("p4", "xq2T", [128, 64 * 256], BF16, is_transpose_dest=True)
    w1tile = [alloc("p4", f"w1tile{i}", [128, 16 * 512], FP8) for i in range(2)]
    w2tile = [alloc("p4", f"w2tile{i}", [128, 4 * 1024], FP8) for i in range(2)]
    b1ct = [alloc("p4", f"b1ct{i}", [128, 512], F32) for i in range(2)]
    gct = [alloc("p4", f"gct{i}", [128, 512], F32) for i in range(2)]
    bct = [alloc("p4", f"bct{i}", [128, 512], F32) for i in range(2)]
    b2t = alloc("p4", "b2t", [128, D], F32)
    i32x = [alloc("p4", f"i32x{i}", [128, 512], I32) for i in range(2)]
    sttmp = [alloc("p4", f"sttmp{i}", [128, 512], F32) for i in range(2)]
    jnk = alloc("p4", "jnk", [128, 512], F32)
    macc4 = alloc("p4", "macc4", [128, 128], F32)

    P.wait("gpsimd", "spx", spx_all)
    P.wait("gpsimd", "w8", w8_all)
    P.wait("sync", "spx", spx_all)
    bb_b2 = P.dma("sync", b2t, b2c, "bbz")

    def SCL(cc, tb, i):
        p = (cc % 2) * 2 + tb
        return stats[:, 24 * p + i:24 * p + i + 1]

    CS = {"v4_end": [0] * 10, "st_end": [0] * 10, "mm1_t4": [0] * 10, "mm2_t4": [0] * 10,
          "q1_v4": [[0, 0] for _ in range(10)], "tq1": [0] * 10, "q2_ready": [0] * 10,
          "ev2_v4": [0] * 10, "ev2_03": [0] * 10, "q2p": [None] * 10}
    i32last = {}
    tiny = P.opd

    def drain_inc(eng, sem):
        sm = P.sem(sem)
        P.cnt[sem] += 1
        v = P.cnt[sem]
        P.cmds[eng].append(lambda e, s=sm, vv=v: e.drain().then_inc(s, 1))
        return v

    def emit_A(cc):
        cp = cc % 2
        if cc >= 2:
            P.wait("sync", f"st{cp}", CS["st_end"][cc - 2])
        for tb in range(2):
            P.dma("sync", x1rd[cp][tb], x1s[(2 * cc + tb) * 128:(2 * cc + tb + 1) * 128, :],
                  f"r1{cp}")
        r1_done = P.cnt[f"r1{cp}"]
        P.wait("vector", f"r1{cp}", r1_done)
        P.wait("scalar", f"r1{cp}", r1_done)
        if cc >= 2:
            P.wait("vector", "v4", CS["ev2_v4"][cc - 2])
            P.wait("scalar", "v4", CS["ev2_v4"][cc - 2])
        if cc >= 1:
            P.wait("scalar", "tq10", CS["tq1"][cc - 1][0])
            P.wait("scalar", "tq11", CS["tq1"][cc - 1][1])
        sq_s4 = [0, 0]
        for tb in range(2):
            tiny("scalar", lambda e, tb=tb: e.activation(
                out=xq1tok[tb], in_=x1rd[cp][tb], func=AF.Identity, accum_out=SCL(cc, tb, 0)))
            sq_s4[tb] = tiny("scalar", lambda e, tb=tb: e.activation(
                out=xq1tok[tb], in_=x1rd[cp][tb], func=AF.Square, accum_out=SCL(cc, tb, 1)), "s4")
            tiny("vector", lambda e, tb=tb: e.reduce_max(out=SCL(cc, tb, 2), in_=x1rd[cp][tb], axis=AX))
            tiny("vector", lambda e, tb=tb: e.tensor_reduce(out=SCL(cc, tb, 3), in_=x1rd[cp][tb],
                                                            axis=AX, op=OP.min))
        sqr_v4 = [0, 0]
        for tb in range(2):
            P.wait("vector", "s4", sq_s4[tb])
            tiny("vector", lambda e, tb=tb: e.tensor_scalar(out=SCL(cc, tb, 4), in0=SCL(cc, tb, 0),
                 scalar1=1.0 / D, scalar2=None, op0=OP.mult))
            tiny("vector", lambda e, tb=tb: e.tensor_scalar(out=SCL(cc, tb, 5), in0=SCL(cc, tb, 1),
                 scalar1=1.0 / D, scalar2=None, op0=OP.mult))
            tiny("vector", lambda e, tb=tb: e.tensor_tensor(out=SCL(cc, tb, 6), in0=SCL(cc, tb, 4),
                 in1=SCL(cc, tb, 4), op=OP.mult))
            tiny("vector", lambda e, tb=tb: e.tensor_tensor(out=SCL(cc, tb, 5), in0=SCL(cc, tb, 5),
                 in1=SCL(cc, tb, 6), op=OP.subtract))
            tiny("vector", lambda e, tb=tb: e.tensor_scalar(out=SCL(cc, tb, 6), in0=SCL(cc, tb, 5),
                 scalar1=EPS, scalar2=None, op0=OP.add))
            tiny("vector", lambda e, tb=tb: e.reciprocal(out=SCL(cc, tb, 6), in_=SCL(cc, tb, 6)))
            tiny("vector", lambda e, tb=tb: e.tensor_tensor(out=SCL(cc, tb, 8), in0=SCL(cc, tb, 5),
                 in1=SCL(cc, tb, 6), op=OP.mult))
            tiny("vector", lambda e, tb=tb: e.tensor_scalar(out=SCL(cc, tb, 8), in0=SCL(cc, tb, 8),
                 scalar1=EPS, scalar2=None, op0=OP.add))
            sqr_v4[tb] = tiny("vector", lambda e, tb=tb: e.reciprocal(out=SCL(cc, tb, 8),
                              in_=SCL(cc, tb, 8)), "v4")
        r2_s4 = [0, 0]
        for tb in range(2):
            P.wait("scalar", "v4", sqr_v4[tb])
            tiny("scalar", lambda e, tb=tb: e.activation(out=SCL(cc, tb, 7), in_=SCL(cc, tb, 6),
                                                         func=AF.Sqrt))
            r2_s4[tb] = tiny("scalar", lambda e, tb=tb: e.activation(out=SCL(cc, tb, 9),
                             in_=SCL(cc, tb, 8), func=AF.Sqrt), "s4")
        qs_v4 = [0, 0]
        for tb in range(2):
            P.wait("vector", "s4", r2_s4[tb])
            tiny("vector", lambda e, tb=tb: e.tensor_tensor(out=SCL(cc, tb, 10), in0=SCL(cc, tb, 2),
                 in1=SCL(cc, tb, 4), op=OP.subtract))
            tiny("vector", lambda e, tb=tb: e.tensor_tensor(out=SCL(cc, tb, 11), in0=SCL(cc, tb, 4),
                 in1=SCL(cc, tb, 3), op=OP.subtract))
            tiny("vector", lambda e, tb=tb: e.tensor_tensor(out=SCL(cc, tb, 10), in0=SCL(cc, tb, 10),
                 in1=SCL(cc, tb, 11), op=OP.max))
            tiny("vector", lambda e, tb=tb: e.tensor_tensor(out=SCL(cc, tb, 11), in0=SCL(cc, tb, 7),
                 in1=SCL(cc, tb, 9), op=OP.mult))
            tiny("vector", lambda e, tb=tb: e.tensor_tensor(out=SCL(cc, tb, 10), in0=SCL(cc, tb, 10),
                 in1=SCL(cc, tb, 11), op=OP.mult))
            tiny("vector", lambda e, tb=tb: e.tensor_scalar(out=SCL(cc, tb, 10), in0=SCL(cc, tb, 10),
                 scalar1=1e-5, scalar2=None, op0=OP.max))
            tiny("vector", lambda e, tb=tb: e.reciprocal(out=SCL(cc, tb, 10), in_=SCL(cc, tb, 10)))
            tiny("vector", lambda e, tb=tb: e.tensor_scalar(out=SCL(cc, tb, 12), in0=SCL(cc, tb, 10),
                 scalar1=127.0, scalar2=None, op0=OP.mult))
            tiny("vector", lambda e, tb=tb: e.tensor_tensor(out=SCL(cc, tb, 13), in0=SCL(cc, tb, 11),
                 in1=SCL(cc, tb, 12), op=OP.mult))
            tiny("vector", lambda e, tb=tb: e.tensor_tensor(out=SCL(cc, tb, 14), in0=SCL(cc, tb, 4),
                 in1=SCL(cc, tb, 13), op=OP.mult))
            tiny("vector", lambda e, tb=tb: e.tensor_scalar(out=SCL(cc, tb, 14), in0=SCL(cc, tb, 14),
                 scalar1=-1.0, scalar2=None, op0=OP.mult))
            tiny("vector", lambda e, tb=tb: e.tensor_tensor(out=SCL(cc, tb, 15), in0=SCL(cc, tb, 12),
                 in1=swb[:, 0:1], op=OP.mult))
            qs_v4[tb] = tiny("vector", lambda e, tb=tb: e.reciprocal(out=SCL(cc, tb, 15),
                             in_=SCL(cc, tb, 15)), "v4")
        for tb in range(2):
            P.wait("scalar", "v4", qs_v4[tb])
            for pc in range(4):
                pp = pc % 2
                prev = i32last.get(pp, 0)
                if prev:
                    P.wait("scalar", "v4", prev)
                u = P.op("scalar", lambda e, tb=tb, pc=pc, pp=pp: e.activation(
                    out=i32x[pp], in_=x1rd[cp][tb][:, pc * 512:(pc + 1) * 512], func=AF.Identity,
                    scale=SCL(cc, tb, 13), bias=SCL(cc, tb, 14)), "s4")
                P.wait("vector", "s4", u)
                i32last[pp] = P.op("vector", lambda e, tb=tb, pc=pc, pp=pp: e.tensor_copy(
                    out=xq1tok[tb][:, pc * 512:(pc + 1) * 512], in_=i32x[pp]), "v4")
        CS["q1_v4"][cc] = [i32last.get(0, 0), i32last.get(1, 0), P.cnt["v4"]]

    def emit_tq1(cc):
        P.wait("sync", "v4", CS["q1_v4"][cc][2])
        if cc >= 1:
            P.wait("sync", "t4", CS["mm1_t4"][cc - 1])
        for tb in range(2):
            dst = xq1T.rearrange("p (b w) -> p b w", w=256)[:, :, tb * 128:(tb + 1) * 128]
            src = xq1tok[tb].rearrange("p (b c) -> p b c", c=128)
            P.dma("sync", dst, src, f"tq1{tb}", transpose=True)
        CS["tq1"][cc] = (P.cnt["tq10"], P.cnt["tq11"])

    def emit_B1(cc):
        cp = cc % 2
        w1_t4 = {}
        ev1_v4 = {}
        ev1p = {}
        for ic in range(16):
            icp = ic % 2
            prev = w1_t4.get(icp, CS["mm2_t4"][cc - 1] if cc >= 1 else 0)
            if prev:
                P.wait("gpsimd", "t4", prev)
            wsrc3 = w1q8.rearrange("(a q) i -> q a i", q=128)[:, :, ic * 512:(ic + 1) * 512]
            wdst3 = w1tile[icp].rearrange("q (a w) -> q a w", w=512)
            wf_done = P.dma("gpsimd", wdst3, wsrc3, f"wf{icp}")
            pe_ev = ev1_v4.get((1, icp), 0)
            if pe_ev:
                P.wait("sync", "v4", pe_ev)
            bb_done = P.dma("sync", b1ct[icp], b1c[:, ic * 512:(ic + 1) * 512], f"bb{icp}")
            P.wait("tensor", f"wf{icp}", wf_done)
            if ic == 0:
                P.wait("tensor", "tq10", CS["tq1"][cc][0])
                P.wait("tensor", "tq11", CS["tq1"][cc][1])
                if cc >= 1:
                    P.wait("tensor", "v4", CS["ev2_03"][cc - 1])
            for tb in range(2):
                bank = pb[tb * 2 + icp]
                pe = ev1_v4.get((tb, icp), 0)
                if pe:
                    P.wait("tensor", "v4", pe)
                for k in range(NB):
                    P.op("tensor", lambda e, bk=bank, k=k, tb=tb, ic=ic,
                         st=(k == 0), sp=(k == NB - 1):
                         e.matmul(bk, xq1T[:, k * 256 + tb * 128:k * 256 + (tb + 1) * 128],
                                  w1tile[ic % 2][:, k * 512:(k + 1) * 512], start=st, stop=sp),
                         "t4" if k == NB - 1 else None)
            w1_t4[icp] = P.cnt["t4"]
            t4n = P.cnt["t4"]
            P.wait("vector", "t4", t4n)
            P.wait("vector", f"bb{icp}", bb_done)
            for tb in range(2):
                bank = pb[tb * 2 + icp]
                ev1_v4[(tb, icp)] = P.op("vector", lambda e, bk=bank, tb=tb, ic=ic, icp=icp:
                    e.scalar_tensor_tensor(out=h_sb[tb][:, ic * 512:(ic + 1) * 512], in0=bk,
                                           scalar=SCL(cc, tb, 15), in1=b1ct[icp],
                                           op0=OP.mult, op1=OP.add), "v4")
                ev1p[(tb, ic)] = ev1_v4[(tb, icp)]
        CS["mm1_t4"][cc] = P.cnt["t4"]
        CS["v4_end"][cc] = P.cnt["v4"]
        sw = 0
        for jb in range(0, 16, 4):
            for j in range(jb, jb + 4):
                for tb in range(2):
                    g = ev1p[(tb, j)]
                    if g > sw:
                        P.wait("scalar", "v4", g)
                        sw = g
                    P.op("scalar", lambda e, tb=tb, j=j: e.activation(
                        out=h_sb[tb][:, j * 512:(j + 1) * 512],
                        in_=h_sb[tb][:, j * 512:(j + 1) * 512], func=AF.Silu), None)
            for j in range(jb, jb + 4):
                for tb in range(2):
                    P.op("scalar", lambda e, tb=tb, j=j: e.activation(
                        out=h_sb[tb][:, j * 512:(j + 1) * 512],
                        in_=h_sb[tb][:, j * 512:(j + 1) * 512], func=AF.Gelu,
                        accum_out=macc4[:, 16 * tb + j:16 * tb + j + 1]), None)
            for j in range(jb, jb + 4):
                for tb in range(2):
                    P.op("scalar", lambda e, tb=tb, j=j: e.activation(
                        out=jnk, in_=h_sb[tb][:, j * 512:(j + 1) * 512], func=AF.Square,
                        accum_out=macc4[:, 32 + 16 * tb + j:32 + 16 * tb + j + 1]), None)
        lnsq_s4 = drain_inc("scalar", "s4")
        rh_v4 = [0, 0]
        for tb in range(2):
            P.wait("vector", "s4", lnsq_s4)
            tiny("vector", lambda e, tb=tb: e.reduce_sum(out=SCL(cc, tb, 16),
                 in_=macc4[:, 16 * tb:16 * tb + 16], axis=AX))
            tiny("vector", lambda e, tb=tb: e.reduce_sum(out=SCL(cc, tb, 17),
                 in_=macc4[:, 32 + 16 * tb:32 + 16 * tb + 16], axis=AX))
            tiny("vector", lambda e, tb=tb: e.tensor_scalar(out=SCL(cc, tb, 16), in0=SCL(cc, tb, 16),
                 scalar1=1.0 / INNER, scalar2=None, op0=OP.mult))
            tiny("vector", lambda e, tb=tb: e.tensor_scalar(out=SCL(cc, tb, 17), in0=SCL(cc, tb, 17),
                 scalar1=1.0 / INNER, scalar2=None, op0=OP.mult))
            tiny("vector", lambda e, tb=tb: e.tensor_tensor(out=SCL(cc, tb, 20), in0=SCL(cc, tb, 16),
                 in1=SCL(cc, tb, 16), op=OP.mult))
            tiny("vector", lambda e, tb=tb: e.tensor_tensor(out=SCL(cc, tb, 17), in0=SCL(cc, tb, 17),
                 in1=SCL(cc, tb, 20), op=OP.subtract))
            tiny("vector", lambda e, tb=tb: e.tensor_scalar(out=SCL(cc, tb, 17), in0=SCL(cc, tb, 17),
                 scalar1=EPS, scalar2=None, op0=OP.add))
            rh_v4[tb] = tiny("vector", lambda e, tb=tb: e.reciprocal(out=SCL(cc, tb, 17),
                             in_=SCL(cc, tb, 17)), "v4")
        nmh_v4 = [0, 0]
        for tb in range(2):
            P.wait("scalar", "v4", rh_v4[tb])
            s1 = tiny("scalar", lambda e, tb=tb: e.activation(out=SCL(cc, tb, 18),
                      in_=SCL(cc, tb, 17), func=AF.Sqrt), "s4")
            P.wait("vector", "s4", s1)
            tiny("vector", lambda e, tb=tb: e.tensor_tensor(out=SCL(cc, tb, 19), in0=SCL(cc, tb, 16),
                 in1=SCL(cc, tb, 18), op=OP.mult))
            nmh_v4[tb] = tiny("vector", lambda e, tb=tb: e.tensor_scalar(out=SCL(cc, tb, 19),
                 in0=SCL(cc, tb, 19), scalar1=-1.0, scalar2=None, op0=OP.mult), "v4")
        bgr = {}
        for ic in range(16):
            icp = ic % 2
            pg = bgr.get(icp, 0)
            if pg:
                P.wait("gpsimd", "v4", pg)
            P.dma("gpsimd", gct[icp], lngc[:, ic * 512:(ic + 1) * 512], f"bg{icp}")
            gb_done = P.dma("gpsimd", bct[icp], lnbc[:, ic * 512:(ic + 1) * 512], f"bg{icp}")
            for tb in range(2):
                if ic == 0:
                    P.wait("scalar", "v4", nmh_v4[tb])
                zs = P.op("scalar", lambda e, tb=tb, ic=ic: e.activation(
                    out=sttmp[ic % 2], in_=h_sb[tb][:, ic * 512:(ic + 1) * 512], func=AF.Identity,
                    scale=SCL(cc, tb, 18), bias=SCL(cc, tb, 19)), "s4")
                P.wait("vector", "s4", zs)
                P.wait("vector", f"bg{icp}", gb_done)
                P.op("vector", lambda e, ic=ic, icp=icp: e.tensor_tensor(
                    out=sttmp[ic % 2], in0=sttmp[ic % 2], in1=gct[icp], op=OP.mult), "v4")
                yv = P.op("vector", lambda e, tb=tb, ic=ic, icp=icp: e.tensor_tensor(
                    out=h_sb[tb][:, ic * 512:(ic + 1) * 512], in0=sttmp[ic % 2], in1=bct[icp],
                    op=OP.add), "v4")
                P.wait("scalar", "v4", yv)
                P.op("vector", lambda e, tb=tb, ic=ic: e.tensor_tensor(
                    out=jnk, in0=h_sb[tb][:, ic * 512:(ic + 1) * 512],
                    in1=h_sb[tb][:, ic * 512:(ic + 1) * 512], op=OP.mult), None)
                P.op("vector", lambda e, tb=tb, ic=ic: e.reduce_sum(
                    out=macc4[:, 96 + 16 * tb + ic:96 + 16 * tb + ic + 1], in_=jnk, axis=AX), None)
                P.op("vector", lambda e, tb=tb, ic=ic: e.reduce_max(
                    out=macc4[:, 64 + 16 * tb + ic:64 + 16 * tb + ic + 1],
                    in_=h_sb[tb][:, ic * 512:(ic + 1) * 512], axis=AX,
                    apply_absolute_value=True), None)
            bgr[icp] = yv
        drain_inc("vector", "v4")
        q2_v4 = [0, 0]
        for tb in range(2):
            tiny("vector", lambda e, tb=tb: e.reduce_sum(out=SCL(cc, tb, 20),
                 in_=macc4[:, 96 + 16 * tb:96 + 16 * tb + 16], axis=AX))
            tiny("vector", lambda e, tb=tb: e.tensor_scalar(out=SCL(cc, tb, 20), in0=SCL(cc, tb, 20),
                 scalar1=1.0 / INNER, scalar2=EPS, op0=OP.mult, op1=OP.add))
            q2_v4[tb] = tiny("vector", lambda e, tb=tb: e.reciprocal(out=SCL(cc, tb, 20),
                             in_=SCL(cc, tb, 20)), "v4")
        r2b_s4 = [0, 0]
        for tb in range(2):
            P.wait("scalar", "v4", q2_v4[tb])
            r2b_s4[tb] = tiny("scalar", lambda e, tb=tb: e.activation(out=SCL(cc, tb, 21),
                              in_=SCL(cc, tb, 20), func=AF.Sqrt), "s4")
        c2_v4 = [0, 0]
        for tb in range(2):
            P.wait("vector", "s4", r2b_s4[tb])
            tiny("vector", lambda e, tb=tb: e.reduce_max(out=SCL(cc, tb, 22),
                 in_=macc4[:, 64 + 16 * tb:64 + 16 * tb + 16], axis=AX))
            tiny("vector", lambda e, tb=tb: e.tensor_tensor(out=SCL(cc, tb, 22), in0=SCL(cc, tb, 22),
                 in1=SCL(cc, tb, 21), op=OP.mult))
            tiny("vector", lambda e, tb=tb: e.tensor_scalar(out=SCL(cc, tb, 22), in0=SCL(cc, tb, 22),
                 scalar1=1e-5, scalar2=None, op0=OP.max))
            tiny("vector", lambda e, tb=tb: e.reciprocal(out=SCL(cc, tb, 22), in_=SCL(cc, tb, 22)))
            tiny("vector", lambda e, tb=tb: e.tensor_scalar(out=SCL(cc, tb, 22), in0=SCL(cc, tb, 22),
                 scalar1=127.0, scalar2=None, op0=OP.mult))
            tiny("vector", lambda e, tb=tb: e.tensor_tensor(out=SCL(cc, tb, 23), in0=SCL(cc, tb, 22),
                 in1=swb[:, 1:2], op=OP.mult))
            tiny("vector", lambda e, tb=tb: e.reciprocal(out=SCL(cc, tb, 23), in_=SCL(cc, tb, 23)))
            c2_v4[tb] = tiny("vector", lambda e, tb=tb: e.tensor_tensor(out=SCL(cc, tb, 22),
                 in0=SCL(cc, tb, 22), in1=SCL(cc, tb, 21), op=OP.mult), "v4")
        tq2_use = ({r: CS["q2_ready"][cc - 1][f"tq2_{r}"] for r in range(4)}
                   if cc >= 1 else {})
        q2p = {}
        for ic in range(16):
            for tb in range(2):
                rr = (2 * ic + tb) % 2
                rot4 = (2 * ic + tb) % 4
                if ic == 0 and tb == 0:
                    for t2b in range(2):
                        P.wait("scalar", "v4", c2_v4[t2b])
                pu = i32last.get(rr, 0)
                if pu:
                    P.wait("scalar", "v4", pu)
                u2 = P.op("scalar", lambda e, tb=tb, ic=ic, rr=rr: e.activation(
                    out=i32x[rr], in_=h_sb[tb][:, ic * 512:(ic + 1) * 512], func=AF.Identity,
                    scale=SCL(cc, tb, 22), bias=0.0), "s4")
                P.wait("vector", "s4", u2)
                pt = tq2_use.get(rot4, 0)
                if pt:
                    P.wait("vector", f"tq2_{rot4}", pt)
                vq = P.op("vector", lambda e, rr=rr, rot4=rot4: e.tensor_copy(
                    out=xq2c[rot4], in_=i32x[rr]), "v4")
                i32last[rr] = vq
                P.wait("sync", "v4", vq)
                if cc >= 1 and ic == 0 and tb == 0:
                    P.wait("sync", "t4", CS["mm2_t4"][cc - 1])
                dst = xq2T.rearrange("p (b w) -> p b w", w=256)[:, 4 * ic:4 * ic + 4,
                                                               tb * 128:(tb + 1) * 128]
                srcq = xq2c[rot4].rearrange("p (b c) -> p b c", c=128)
                tq2_use[rot4] = P.dma("sync", dst, srcq, f"tq2_{rot4}", transpose=True)
                q2p.setdefault(ic, []).append((f"tq2_{rot4}", tq2_use[rot4]))
        CS["q2_ready"][cc] = {f"tq2_{r}": P.cnt.get(f"tq2_{r}", 0) for r in range(4)}
        CS["q2p"][cc] = q2p

    def emit_B2(cc):
        cp = cc % 2
        w2_t4 = ({0: CS["mm2_t4"][cc - 1], 1: CS["mm2_t4"][cc - 1]} if cc >= 1 else {})
        step = 0
        for ic in range(16):
            for oh in range(2):
                icp = step % 2
                prev = w2_t4.get(icp, 0)
                if prev:
                    P.wait("gpsimd", "t4", prev)
                w2src = w2q8.rearrange("(a q) d -> q a d", q=128)[:, 4 * ic:4 * ic + 4,
                                                                 oh * 1024:(oh + 1) * 1024]
                w2dst = w2tile[icp].rearrange("q (a d) -> q a d", d=1024)
                wf2 = P.dma("gpsimd", w2dst, w2src, f"w2{icp}")
                P.wait("tensor", f"w2{icp}", wf2)
                if oh == 0:
                    for nm, cnt in CS["q2p"][cc][ic]:
                        P.wait("tensor", nm, cnt)
                if ic == 0 and oh == 0:
                    P.wait("tensor", "v4", CS["v4_end"][cc])
                    if cc >= 1:
                        P.wait("tensor", "v4", CS["ev2_v4"][cc - 1])
                for k in range(4):
                    for tb in range(2):
                        for dc in range(2):
                            st4 = (ic == 0 and k == 0)
                            sp4 = (ic == 15 and k == 3)
                            last = (k == 3 and tb == 1 and dc == 1)
                            P.op("tensor", lambda e, bk=pb[tb * 4 + oh * 2 + dc], ic=ic, k=k,
                                 tb=tb, dc=dc, st=st4, sp=sp4, tile=w2tile[icp]:
                                 e.matmul(bk, xq2T[:, (4 * ic + k) * 256 + tb * 128:
                                                   (4 * ic + k) * 256 + (tb + 1) * 128],
                                          tile[:, k * 1024 + dc * 512:
                                               k * 1024 + (dc + 1) * 512],
                                          start=st, stop=sp),
                                 "t4" if last else None)
                w2_t4[icp] = P.cnt["t4"]
                step += 1
        mm_done = P.cnt["t4"]
        P.wait("vector", "t4", mm_done)
        if cc == 0:
            P.wait("vector", "bbz", bb_b2)
        for tb in range(2):
            for oh in range(2):
                for dc in range(2):
                    col = oh * 1024 + dc * 512
                    P.op("vector", lambda e, bk=pb[tb * 4 + oh * 2 + dc], tb=tb, col=col:
                         e.scalar_tensor_tensor(out=sttmp[tb], in0=bk, scalar=SCL(cc, tb, 23),
                                                in1=b2t[:, col:col + 512],
                                                op0=OP.mult, op1=OP.add), "v4")
                    P.op("vector", lambda e, tb=tb, col=col: e.tensor_tensor(
                        out=x1rd[cp][tb][:, col:col + 512], in0=sttmp[tb],
                        in1=x1rd[cp][tb][:, col:col + 512], op=OP.add), "v4")
            if tb == 0:
                CS["ev2_03"][cc] = P.cnt["v4"]
        CS["mm2_t4"][cc] = P.cnt["t4"]
        CS["ev2_v4"][cc] = P.cnt["v4"]
        P.wait("sync", "v4", P.cnt["v4"])
        for tb in range(2):
            P.dma("sync", y_out[(2 * cc + tb) * 128:(2 * cc + tb + 1) * 128, :],
                  x1rd[cp][tb], f"st{cp}")
        CS["st_end"][cc] = P.cnt[f"st{cp}"]

    emit_A(0)
    emit_tq1(0)
    emit_B1(0)
    emit_A(1)
    for i in range(1, 8):
        emit_B2(i - 1)
        emit_tq1(i)
        emit_B1(i)
        if i + 1 < 8:
            emit_A(i + 1)
    emit_B2(7)

    P.wait("sync", "st0", P.cnt["st0"])
    P.wait("sync", "st1", P.cnt["st1"])
    P.finish()
    return nc


def wdst_idx(idx, w1q8, w2q8):
    if idx < 128:
        r, cc = divmod(idx, 8)
        return w1q8[r * 128:(r + 1) * 128, cc * 1024:(cc + 1) * 1024]
    i, c2 = divmod(idx - 128, 2)
    return w2q8[i * 128:(i + 1) * 128, c2 * 1024:(c2 + 1) * 1024]


def prep_inputs(ins, b):
    co = np.ascontiguousarray
    d = {
        "x_in": co(np.asarray(ins["L0_attn_x"])[b]).astype(np.float32),
        "wqT": co(np.asarray(ins["L0_attn_q_w"]).T),
        "wkT": co(np.asarray(ins["L0_attn_k_w"]).T),
        "wvT": co(np.asarray(ins["L0_attn_v_w"]).T),
        "woT": co(np.asarray(ins["L0_attn_o_w"]).T),
        "gwT": co(np.asarray(ins["L0_attn_gate_w"]).T),
        "gb": co(np.asarray(ins["L0_attn_gate_b"]).reshape(HD, 1)),
        "w1T": co(np.asarray(ins["L0_ffn_w1"]).T),
        "w2T": co(np.asarray(ins["L0_ffn_w2"]).T),
        "b1c": co(np.broadcast_to(np.asarray(ins["L0_ffn_b1"])[None, :], (128, INNER))),
        "lngc": co(np.broadcast_to(np.asarray(ins["L0_ffn_ln_g"])[None, :], (128, INNER))),
        "lnbc": co(np.broadcast_to(np.asarray(ins["L0_ffn_ln_b"])[None, :], (128, INNER))),
        "b2c": co(np.broadcast_to(np.asarray(ins["L0_ffn_b2"])[None, :], (128, D))),
        "maddc": _madd_const(),
    }
    return {k: np.asarray(v, dtype=np.float32) for k, v in d.items()}


def _madd_const():
    c = np.arange(512)[None, :]
    p = np.arange(128)[:, None]
    tiles = [np.where(c >= p + 128 * j, 0.0, -1e30) for j in range(4)]
    return np.concatenate(tiles, axis=1).astype(np.float32)


def kernel(**inputs):
    nc = build_nc(stage=4)
    x = np.asarray(inputs["L0_attn_x"])
    ncores = x.shape[0]
    common = None
    in_maps = []
    for b in range(ncores):
        d = prep_inputs(inputs, b)
        if common is None:
            common = {k: v for k, v in d.items() if k != "x_in"}
        else:
            for k in common:
                d[k] = common[k]
        in_maps.append(d)
    res = run_bass_kernel_spmd(nc, in_maps, core_ids=list(range(ncores)))
    out = np.stack([np.asarray(r["y_out"]) for r in res.results], axis=0)
    return out.astype(np.float32)
